# version 08
# baseline (speedup 1.0000x reference)
import numpy as np
import ml_dtypes

import concourse.bass as bass
import concourse.mybir as mybir
from concourse.bass import IndirectOffsetOnAxis
from concourse.bass_utils import run_bass_kernel_spmd

F32 = mybir.dt.float32
BF16 = mybir.dt.bfloat16
I32 = mybir.dt.int32
AF = mybir.ActivationFunctionType
ALU = mybir.AluOpType
AX = mybir.AxisListType
bfdt = ml_dtypes.bfloat16

NCORES = 8
N = 8192
D = 2048
DH = 128
HPC = 2
TPC = N // NCORES
E = 4
CAP = 2048
HALF = CAP // 2
HDIM = 8192
SCALE = float(DH) ** -0.5
BIG = 99999.0
NBLK = 16
NTILE = 64

PERM = np.concatenate([np.arange(0, 16), np.arange(32, 48),
                       np.arange(16, 32), np.arange(48, 128)])

_CACHE = {}
DEBUG = False


class Sem:
    def __init__(self, h):
        self.h = h
        self.v = 0

    def inc(self, n=1):
        self.v += n
        return self.v


class Gate:
    def __init__(self):
        self.sem = None
        self.val = 0

    def set(self, sem, val):
        self.sem = sem
        self.val = val


class Arena:
    def __init__(self, ap, cols, name):
        self.ap = ap
        self.cols = cols
        self.off = 0
        self.marks = []
        self.name = name
        self.peak = 0

    def take(self, w, align=16):
        if self.off % align:
            self.off += align - (self.off % align)
        o = self.off
        self.off += w
        self.peak = max(self.peak, self.off)
        assert self.off <= self.cols, f"{self.name} overflow {self.off}>{self.cols}"
        return o

    def push(self):
        self.marks.append(self.off)

    def pop(self):
        self.off = self.marks.pop()


def build_program(debug=False):
    nc = bass.Bass("TRN2", target_bir_lowering=False)

    def ext(name, shape, dt):
        return nc.dram_tensor(name, shape, dt, kind="ExternalInput").ap()

    x_sl = ext("x_sl", [TPC, D], F32)
    wqkvTi = ext("wqkvTi", [D, 768], F32)
    woutFi = ext("woutFi", [D, D], F32)
    aoixi = ext("aoixi", [128, 16], I32)
    wgi = ext("wgi", [D, E], F32)
    bgi = ext("bgi", [1, E], F32)
    w1i = ext("w1i", [D, HDIM], F32)
    w2i = ext("w2i", [HDIM, D], F32)
    rotAi = ext("rotAi", [64, N], BF16)
    rotBi = ext("rotBi", [64, N], BF16)
    trimT4i = ext("trimT4i", [128, 2048], BF16)
    identbfi = ext("identbfi", [128, 128], BF16)
    identf32i = ext("identf32i", [128, 128], F32)
    utrii = ext("utrii", [128, 128], BF16)
    onesbfi = ext("onesbfi", [128, 128], BF16)
    onesf32i = ext("onesf32i", [128, 128], F32)
    wseli = ext("wseli", [128, E], F32)
    hbasei = ext("hbasei", [128, 1], F32)
    ownpmi = ext("ownpmi", [128, 8], I32)

    out_sl = nc.dram_tensor("out_sl", [TPC, D], F32, kind="ExternalOutput").ap()

    cc_xn_in = nc.dram_tensor("cc_xn_in", [TPC, D], BF16)
    cc_xn_out = nc.dram_tensor("cc_xn_out", [N, D], BF16)
    cc_ao_in = [nc.dram_tensor(f"cc_ao_in{h}", [128, N], BF16)
                for h in range(2)]
    cc_ao_out = [nc.dram_tensor(f"cc_ao_out{h}", [NCORES * 128 * 8, N // 8],
                                BF16) for h in range(2)]
    cc_sown = nc.dram_tensor("cc_sown", [TPC, D], F32)
    cc_y_in = nc.dram_tensor("cc_y_in", [HALF, D], BF16)
    cc_y_out = nc.dram_tensor("cc_y_out", [NCORES * HALF, D], BF16)
    idxp_buf = nc.dram_tensor("idxp_buf", [HALF, 2], F32)
    rowab_buf = nc.dram_tensor("rowab_buf", [N, 2], F32)
    qkv1_buf = nc.dram_tensor("qkv1_buf", [128, 3 * N], BF16)
    if debug:
        dbg_o = {
            "d_xn": nc.dram_tensor("d_xn", [N, D], BF16,
                                   kind="ExternalOutput").ap(),
            "d_lg": nc.dram_tensor("d_lg", [128, 256], F32,
                                   kind="ExternalOutput").ap(),
            "d_rt": nc.dram_tensor("d_rt", [128, 1024], F32,
                                   kind="ExternalOutput").ap(),
            "d_rab": nc.dram_tensor("d_rab", [N, 2], F32,
                                    kind="ExternalOutput").ap(),
            "d_idx": nc.dram_tensor("d_idx", [HALF, 2], F32,
                                    kind="ExternalOutput").ap(),

            "d_qkv1": nc.dram_tensor("d_qkv1", [128, 3 * N], BF16,
                                     kind="ExternalOutput").ap(),
            "d_s": nc.dram_tensor("d_s", [1024, D], F32,
                                  kind="ExternalOutput").ap(),
            "d_y": nc.dram_tensor("d_y", [NCORES * HALF, D], BF16,
                                  kind="ExternalOutput").ap(),
        }

    import contextlib
    es = contextlib.ExitStack()
    nc._keepalive = es

    prog = {k: [] for k in ("sync", "tensor", "vector", "scalar", "gpsimd")}

    def op(eng, fn):
        prog[eng].append(fn)

    def sem(name):
        return Sem(es.enter_context(nc.semaphore(name)))

    te = sem("te")
    ve = sem("ve")
    se = sem("sescal")
    geio = sem("geio")
    cxn = sem("cxn")
    cy = sem("cy")
    s_const = sem("s_const")
    s_lnx = sem("s_lnx")
    s_xni = sem("s_xni")
    s_w0 = sem("s_w0")
    s_xnt = [sem("s_xnt0"), sem("s_xnt1")]
    s_rot = [sem("s_rot0"), sem("s_rot1")]
    s_st = [sem("s_st0"), sem("s_st1"), sem("s_st2")]
    s_rel = sem("s_rel")
    s_ao = sem("s_ao")
    cao = sem("cao")
    s_alh = sem("s_alh")
    s_wo = [sem("s_wo0"), sem("s_wo1")]
    s_sst = sem("s_sst")
    s_rab = sem("s_rab")
    s_scat = sem("s_scat")
    s_idx = sem("s_idx")
    s_rabg = sem("s_rabg")
    s_xg = [sem("s_xg0"), sem("s_xg1")]
    s_wout = sem("s_wout")
    s_w1 = [sem("s_w10"), sem("s_w11")]
    s_w2 = [sem("s_w20"), sem("s_w21")]
    s_yst = [sem("s_yst0"), sem("s_yst1")]
    s_y5 = sem("s_y5")
    s_p5x = sem("s_p5x")
    s_p5s = sem("s_p5s")
    s_out = sem("s_out")

    def wait(eng, s, v):
        if v <= 0:
            return
        op(eng, lambda e, _h=s.h, _v=v: e.wait_ge(_h, _v))

    def gwait(eng, gate):
        if gate.sem is not None and gate.val > 0:
            wait(eng, gate.sem, gate.val)

    def dma(eng, out, in_, s, transpose=False):
        def f(e, _o=out, _i=in_, _h=s.h, _t=transpose):
            if _t:
                e.dma_start(out=_o, in_=_i, transpose=True).then_inc(_h, 16)
            else:
                e.dma_start(out=_o, in_=_i).then_inc(_h, 16)
        op(eng, f)
        return s.inc(16)

    def mm(out, lhsT, rhs, start, stop, s=None):
        def f(e, _o=out, _l=lhsT, _r=rhs, _a=start, _b=stop, _s=s):
            i = e.matmul(_o, _l, _r, start=_a, stop=_b)
            if _s is not None:
                i.then_inc(_s.h, 1)
        op("tensor", f)
        return s.inc() if s is not None else None

    def petr(out, in_, ident, s=None):
        def f(e, _o=out, _i=in_, _d=ident, _s=s):
            i = e.transpose(_o, _i, _d)
            if _s is not None:
                i.then_inc(_s.h, 1)
        op("tensor", f)
        return s.inc() if s is not None else None

    EDGE_ALL = [False]

    def vop(meth, *args, s=None, eng="vector", **kw):
        if EDGE_ALL[0] and eng == "vector" and s is None:
            s = ve

        def f(e, _m=meth, _a=args, _k=kw, _s=s):
            i = getattr(e, _m)(*_a, **_k)
            if _s is not None:
                i.then_inc(_s.h, 1)
        op(eng, f)
        r = s.inc() if s is not None else None
        if EDGE_ALL[0] and eng == "vector":
            wait("vector", ve, ve.v)
        return r

    def act(out, in_, func, bias=0.0, scale=1.0, accum=None, s=None):
        def f(e, _o=out, _i=in_, _f=func, _b=bias, _c=scale, _ac=accum, _s=s):
            if _ac is not None:
                i = e.activation(_o, _i, _f, bias=_b, scale=_c, accum_out=_ac)
            else:
                i = e.activation(_o, _i, _f, bias=_b, scale=_c)
            if _s is not None:
                i.then_inc(_s.h, 1)
        op("scalar", f)
        return s.inc() if s is not None else None

    def coll(kind, alu, ins, outs, s):
        def f(e, _k=kind, _a=alu, _i=ins, _o=outs, _h=s.h):
            e.collective_compute(_k, _a, replica_groups=[list(range(NCORES))],
                                 ins=_i, outs=_o).then_inc(_h, 1)
        op("gpsimd", f)
        return s.inc()

    ABF, AFC, AIC = 81056, 12048, 608
    abf_t = nc.alloc_sbuf_tensor("abf", [128, ABF], BF16)
    af_t = nc.alloc_sbuf_tensor("af32", [128, AFC], F32)
    ai_t = nc.alloc_sbuf_tensor("ai32", [128, AIC], I32)
    abf = Arena(abf_t.ap(), ABF, "bf")
    af = Arena(af_t.ap(), AFC, "f32")
    ai = Arena(ai_t.ap(), AIC, "i32")

    def bfv(o, w):
        return abf.ap[:, o:o + w]

    def fv(o, w):
        return af.ap[:, o:o + w]

    def ivv(o, w):
        return ai.ap[:, o:o + w]

    pb = [es.enter_context(nc.psum_tensor(f"pb{i}", [128, 512], F32))
          for i in range(7)]
    pbm = es.enter_context(nc.psum_tensor("pbm", [128, 512], F32))
    pbg = [Gate() for _ in range(7)]
    pbmg = Gate()

    o_identbf = abf.take(128)
    o_utri = abf.take(128)
    o_onesbf = abf.take(128)
    o_selbf = abf.take(64)
    o_identf = af.take(128)
    o_onesf = af.take(128)
    o_wsel = af.take(E)
    o_hbase = af.take(1)
    o_iota = af.take(64)
    o_logits = af.take(256)
    o_t1 = af.take(64)
    o_cnt = af.take(256)
    o_p = af.take(256)
    o_score = af.take(256)
    o_eqm = af.take(256)
    o_rank = af.take(256)
    o_sel = af.take(256)
    o_r1 = af.take(256)
    o_r2 = af.take(256)
    o_cnt4 = af.take(4)
    o_lo = af.take(4)
    o_hi = af.take(4)
    o_mid = af.take(4)
    o_bc = af.take(4)
    o_ge4 = af.take(4)
    o_th = af.take(4)
    o_thbc = af.take(4)
    o_rr = af.take(4)
    o_colsum = af.take(64)
    o_colp = af.take(64)
    o_cps = af.take(64)
    o_capc = af.take(1)
    o_myrec = af.take(128)
    o_rowa = af.take(64)
    o_rowb = af.take(64)
    o_rowi = af.take(128)
    o_mysel = af.take(64)
    o_myrank = af.take(64)
    o_myp = af.take(64)
    o_pif = af.take(64)
    o_rab = af.take(16)
    o_rab2 = af.take(16)
    o_ptil = af.take(8)
    o_invr = af.take(512)
    o_ir1 = af.take(512)
    o_eps = af.take(1)
    o_ss4 = af.take(4)
    o_ss64 = af.take(64)
    o_ownpm = ai.take(8)
    o_aoix = ai.take(16)
    o_pi = ai.take(64)
    o_iadj = ai.take(64)
    o_ish = ai.take(64)
    o_tok = ai.take(8)
    o_rowA8 = ai.take(8)
    o_rowB8 = ai.take(8)
    o_im = ai.take(64)

    abf.push()
    o_xg = abf.take(2 * 2048)
    o_xgt = abf.take(16 * 1024)
    abf.push()
    o_qt = abf.take(N)
    o_kt = abf.take(N)
    o_vt = abf.take(N)
    abf.push()
    o_wqkvT = abf.take(16 * 768)
    o_wg = abf.take(16 * E)
    o_bg = abf.take(E)
    o_xnt = [abf.take(16 * 512) for _ in range(2)]
    o_xnstage = abf.take(2048)
    o_rot = [abf.take(1024) for _ in range(2)]
    o_rsc = abf.take(1024)
    o_qkv1st = abf.take(3 * 512)
    af.push()
    o_xsb = af.take(2048)
    o_stat = af.take(8)

    dma("sync", bfv(o_identbf, 128), identbfi, s_const)
    dma("sync", bfv(o_utri, 128), utrii, s_const)
    dma("sync", bfv(o_onesbf, 128), onesbfi, s_const)
    dma("sync", fv(o_identf, 128), identf32i, s_const)
    dma("sync", fv(o_onesf, 128), onesf32i, s_const)
    dma("sync", fv(o_wsel, E), wseli, s_const)
    dma("sync", fv(o_hbase, 1), hbasei, s_const)
    dma("sync", ivv(o_ownpm, 8), ownpmi, s_const)
    dma("sync", ivv(o_aoix, 16), aoixi, s_const)

    dma("gpsimd", bfv(o_wqkvT, 16 * 768),
        wqkvTi.rearrange("(a p) f -> p a f", p=128), s_w0)
    dma("gpsimd", bfv(o_wg, 16 * E),
        wgi.rearrange("(a p) e -> p a e", p=128), s_w0)
    dma("gpsimd", abf.ap[0:1, o_bg:o_bg + E], bgi, s_w0)
    vop("iota", ivv(o_pi, 64), pattern=[[128, 64]], base=0,
        channel_multiplier=1, eng="gpsimd", s=geio)

    wait("vector", geio, 1)
    vop("memset", fv(o_eps, 1), 1e-05)
    vop("tensor_copy", fv(o_iota, 64), ivv(o_pi, 64), s=ve)

    for k in range(8):
        if k >= 1:
            wait("sync", ve, ln_last_ve)
        dma("sync", fv(o_xsb, 2048), x_sl[k * 128:(k + 1) * 128, :], s_lnx)
        wait("vector", s_lnx, s_lnx.v)
        vop("reduce_sum", fv(o_stat, 1), fv(o_xsb, 2048), axis=AX.X, s=ve)
        wait("vector", ve, ve.v)
        vop("tensor_scalar", fv(o_stat + 1, 1), fv(o_stat, 1), 1.0 / D, None,
            op0=ALU.mult, s=ve)
        wait("vector", ve, ve.v)
        vop("tensor_scalar", fv(o_xsb, 2048), fv(o_xsb, 2048),
            fv(o_stat + 1, 1), None, op0=ALU.subtract, s=ve)
        wait("scalar", ve, ve.v)
        if k >= 1:
            wait("scalar", s_xni, ln_last_gq)
        act(bfv(o_xnstage, 2048), fv(o_xsb, 2048), AF.Square,
            accum=fv(o_stat + 2, 1), s=se)
        wait("scalar", se, se.v)
        act(fv(o_stat + 3, 1), fv(o_stat + 2, 1), AF.Ln, bias=fv(o_eps, 1),
            scale=1.0 / D, s=se)
        wait("scalar", se, se.v)
        act(fv(o_stat + 4, 1), fv(o_stat + 3, 1), AF.Exp, scale=-0.5, s=se)
        wait("vector", se, se.v)
        vop("tensor_scalar", bfv(o_xnstage, 2048), fv(o_xsb, 2048),
            fv(o_stat + 4, 1), None, op0=ALU.mult, s=ve)
        ln_last_ve = ve.v
        wait("gpsimd", ve, ve.v)
        dma("gpsimd", cc_xn_in.ap()[k * 128:(k + 1) * 128, :],
            bfv(o_xnstage, 2048), s_xni)
        ln_last_gq = s_xni.v
    wait("gpsimd", s_xni, s_xni.v)
    coll("AllGather", ALU.bypass, [cc_xn_in.ap().opt()],
         [cc_xn_out.ap().opt()], cxn)

    xnt_gate = [Gate(), Gate()]
    rot_gate = [Gate(), Gate()]
    st_gate = [Gate(), Gate(), Gate()]
    xnt_ld = {}
    rot_ld = {}

    def emit_xnt_load(j):
        buf = j % 2
        gwait("sync", xnt_gate[buf])
        for d16 in range(16):
            src = cc_xn_out.ap()[j * 512:(j + 1) * 512,
                                 d16 * 128:(d16 + 1) * 128]
            dma("sync", bfv(o_xnt[buf] + d16 * 512, 512), src, s_xnt[buf],
                transpose=True)
        xnt_ld[j] = s_xnt[buf].v

    def emit_rot_load(j):
        buf = j % 2
        gwait("sync", rot_gate[buf])
        dma("sync", abf.ap[0:64, o_rot[buf]:o_rot[buf] + 512],
            rotAi[:, j * 512:(j + 1) * 512], s_rot[buf])
        dma("sync", abf.ap[0:64, o_rot[buf] + 512:o_rot[buf] + 1024],
            rotBi[:, j * 512:(j + 1) * 512], s_rot[buf])
        rot_ld[j] = s_rot[buf].v

    wait("sync", cxn, 1)
    emit_xnt_load(0)
    emit_rot_load(0)
    emit_xnt_load(1)
    emit_rot_load(1)

    wait("tensor", s_w0, s_w0.v)
    wait("tensor", s_const, s_const.v)
    for j in range(NBLK):
        buf = j % 2
        wait("tensor", s_xnt[buf], xnt_ld[j])
        for F in range(4):
            gwait("tensor", pbg[F])
            for d16 in range(16):
                mm(pb[F][:, 0:512], bfv(o_wqkvT + d16 * 768 + F * 128, 128),
                   bfv(o_xnt[buf] + d16 * 512, 512),
                   d16 == 0, d16 == 15, s=te if d16 == 15 else None)
        gwait("tensor", pbmg)
        vpair_te = {}
        for pair in range(2):
            gwait("tensor", pbg[4])
            for tt in (2 * pair, 2 * pair + 1):
                colo = (tt % 2) * 256
                for d16 in range(16):
                    lhs = bfv(o_xnt[buf] + d16 * 512 + tt * 128, 128)
                    mm(pb[4][:, colo:colo + 256], lhs,
                       bfv(o_wqkvT + d16 * 768 + 512, 256),
                       d16 == 0, d16 == 15, s=te if d16 == 15 else None)
                    mm(pbm[:, tt * 4:tt * 4 + 4], lhs, bfv(o_wg + d16 * 4, 4),
                       d16 == 0, False)
                mm(pbm[:, tt * 4:tt * 4 + 4],
                   abf.ap[0:1, o_onesbf:o_onesbf + 128],
                   abf.ap[0:1, o_bg:o_bg + E], False, True,
                   s=te if tt == 3 else None)
            vpair_te[pair] = te.v
            wait("vector", te, te.v)
            for tt in (2 * pair, 2 * pair + 1):
                colo = (tt % 2) * 256
                vop("tensor_copy", bfv(o_vt + (j * 4 + tt) * 128, 128),
                    pb[4][:, colo:colo + 128])
                gwait("vector", st_gate[2])
                vop("tensor_copy", bfv(o_qkv1st + 1024 + tt * 128, 128),
                    pb[4][:, colo + 128:colo + 256], s=ve)
            pbg[4].set(ve, ve.v)
        te_blk = te.v
        xnt_gate[buf].set(te, te_blk)
        if j + 2 < NBLK:
            emit_xnt_load(j + 2)

        wait("vector", te, te_blk)
        wait("vector", s_rot[buf], rot_ld[j])
        for F in range(4):
            h = F % 2
            if h == 0:
                dsto = (o_qt if F == 0 else o_kt) + j * 512
            else:
                dsto = o_qkv1st + (F // 2) * 512
                gwait("vector", st_gate[F // 2])
            vop("tensor_copy", bfv(dsto, 512), pb[F][:, 0:512],
                s=ve if F == 3 else None)
        for F in range(4):
            pbg[F].set(ve, ve.v)
        for F in range(4):
            h = F % 2
            dsto = ((o_qt if F == 0 else o_kt) + j * 512) if h == 0 \
                else (o_qkv1st + (F // 2) * 512)
            in0 = abf.ap[0:32, dsto:dsto + 512]
            in32 = abf.ap[32:64, dsto:dsto + 512]
            cA = abf.ap[0:32, o_rot[buf]:o_rot[buf] + 512]
            sA = abf.ap[32:64, o_rot[buf]:o_rot[buf] + 512]
            sB = abf.ap[0:32, o_rot[buf] + 512:o_rot[buf] + 1024]
            cB = abf.ap[32:64, o_rot[buf] + 512:o_rot[buf] + 1024]
            t1 = abf.ap[0:32, o_rsc:o_rsc + 512]
            t2 = abf.ap[0:32, o_rsc + 512:o_rsc + 1024]
            t3 = abf.ap[32:64, o_rsc:o_rsc + 512]
            t4 = abf.ap[32:64, o_rsc + 512:o_rsc + 1024]
            vop("tensor_mul", t1, in0, cA)
            vop("tensor_mul", t2, in32, sA)
            vop("tensor_mul", t3, in32, cB)
            vop("tensor_mul", t4, in0, sB, s=ve)
            wait("vector", ve, ve.v)
            vop("tensor_add", in0, t1, t2)
            vop("tensor_add", in32, t3, t4, s=ve)
        rot_gate[buf].set(ve, ve.v)
        if j + 2 < NBLK:
            emit_rot_load(j + 2)
        vop("tensor_copy", fv(o_logits + j * 16, 16), pbm[:, 0:16], s=ve)
        pbmg.set(ve, ve.v)
        wait("sync", ve, ve.v)
        dma("sync", qkv1_buf.ap()[:, j * 512:(j + 1) * 512],
            bfv(o_qkv1st, 512), s_st[0])
        st_gate[0].set(s_st[0], s_st[0].v)
        dma("sync", qkv1_buf.ap()[:, N + j * 512:N + (j + 1) * 512],
            bfv(o_qkv1st + 512, 512), s_st[1])
        st_gate[1].set(s_st[1], s_st[1].v)
        dma("sync", qkv1_buf.ap()[:, 2 * N + j * 512:2 * N + (j + 1) * 512],
            bfv(o_qkv1st + 1024, 512), s_st[2])
        st_gate[2].set(s_st[2], s_st[2].v)
    P1_VE = ve.v
    P1_TE = te.v
    abf.pop()
    af.pop()

    EDGE_ALL[0] = True

    def eview(base, e):
        return af.ap[:, base:base + 256].rearrange("p (i e) -> p e i",
                                                   e=E)[:, e, :]

    vop("tensor_tensor", fv(o_t1, 64), eview(o_logits, 0),
        eview(o_logits, 1), op=ALU.max)
    vop("tensor_tensor", fv(o_r1, 64), eview(o_logits, 2),
        eview(o_logits, 3), op=ALU.max)
    vop("tensor_tensor", fv(o_t1, 64), fv(o_t1, 64), fv(o_r1, 64), op=ALU.max)
    for e in range(E):
        ce = eview(o_cnt, e)
        vop("memset", ce, 0.0)
        for ep in range(E):
            if ep == e:
                continue
            opk = ALU.is_gt if ep > e else ALU.is_ge
            vop("tensor_tensor", fv(o_r1, 64), eview(o_logits, ep),
                eview(o_logits, e), op=opk)
            vop("tensor_add", ce, ce, fv(o_r1, 64))
    for e in range(E):
        vop("tensor_tensor", eview(o_r1, e), eview(o_logits, e),
            fv(o_t1, 64), op=ALU.subtract, s=ve if e == E - 1 else None)
    wait("scalar", ve, ve.v)
    act(fv(o_p, 256), fv(o_r1, 256), AF.Exp, s=se)
    wait("vector", se, se.v)
    vop("tensor_scalar", fv(o_r2, 256), fv(o_cnt, 256), 1.5, None,
        op0=ALU.is_le)
    vop("tensor_mul", fv(o_p, 256), fv(o_p, 256), fv(o_r2, 256))
    vop("tensor_tensor", fv(o_t1, 64), eview(o_p, 0), eview(o_p, 1),
        op=ALU.add)
    vop("tensor_tensor", fv(o_r1, 64), eview(o_p, 2), eview(o_p, 3),
        op=ALU.add)
    vop("tensor_add", fv(o_t1, 64), fv(o_t1, 64), fv(o_r1, 64))
    vop("reciprocal", fv(o_r1, 64), fv(o_t1, 64), s=ve)
    wait("vector", ve, ve.v)
    for e in range(E):
        vop("tensor_tensor", eview(o_p, e), eview(o_p, e), fv(o_r1, 64),
            op=ALU.mult)
    vop("tensor_scalar", fv(o_score, 256), fv(o_r2, 256), 1.0, None,
        op0=ALU.subtract)
    vop("tensor_mul", fv(o_r2, 256), fv(o_p, 256), fv(o_r2, 256))
    vop("tensor_add", fv(o_score, 256), fv(o_score, 256), fv(o_r2, 256))
    vop("memset", fv(o_lo, 4), -2.0)
    vop("memset", fv(o_hi, 4), 2.0)
    vop("memset", fv(o_capc, 1), float(CAP) - 0.5, s=ve)

    def count4(cmp_op, scal_off, iota_mode=False):
        for e in range(E):
            tgt = eview(o_r1, e)
            if iota_mode:
                vop("tensor_scalar", tgt, fv(o_iota, 64),
                    fv(scal_off + e, 1), None, op0=cmp_op)
                vop("tensor_mul", tgt, tgt, eview(o_eqm, e))
            else:
                vop("tensor_scalar", tgt, eview(o_score, e),
                    fv(scal_off + e, 1), None, op0=cmp_op)
            vop("reduce_sum", fv(o_cnt4 + e, 1), tgt, axis=AX.X, s=ve)
            wait("vector", ve, ve.v)
        wait("tensor", ve, ve.v)
        gwait("tensor", pbmg)
        mm(pbm[:, 0:4], fv(o_onesf, 128), fv(o_cnt4, 4), True, True, s=te)
        wait("vector", te, te.v)
        vop("tensor_copy", fv(o_bc, 4), pbm[:, 0:4], s=ve)
        pbmg.set(ve, ve.v)

    for it in range(34):
        vop("tensor_tensor", fv(o_mid, 4), fv(o_lo, 4), fv(o_hi, 4),
            op=ALU.add)
        vop("tensor_scalar", fv(o_mid, 4), fv(o_mid, 4), 0.5, None,
            op0=ALU.mult, s=ve)
        count4(ALU.is_gt, o_mid)
        vop("tensor_scalar", fv(o_ge4, 4), fv(o_bc, 4), fv(o_capc, 1),
            None, op0=ALU.is_ge)
        vop("tensor_copy", ivv(o_im, 4), fv(o_ge4, 4))
        vop("select", fv(o_ss4, 4), ivv(o_im, 4), fv(o_mid, 4), fv(o_lo, 4))
        vop("tensor_copy", fv(o_lo, 4), fv(o_ss4, 4))
        vop("select", fv(o_ss4, 4), ivv(o_im, 4), fv(o_hi, 4), fv(o_mid, 4))
        vop("tensor_copy", fv(o_hi, 4), fv(o_ss4, 4), s=ve)
    vop("tensor_copy", fv(o_th, 4), fv(o_hi, 4), s=ve)
    count4(ALU.is_gt, o_th)
    vop("tensor_scalar", fv(o_rr, 4), fv(o_bc, 4), -1.0, float(CAP),
        op0=ALU.mult, op1=ALU.add, s=ve)
    for e in range(E):
        vop("tensor_scalar", eview(o_eqm, e), eview(o_score, e),
            fv(o_th + e, 1), None, op0=ALU.is_equal)
    vop("memset", fv(o_lo, 4), -1.0)
    vop("memset", fv(o_hi, 4), float(N), s=ve)
    for it in range(16):
        vop("tensor_tensor", fv(o_mid, 4), fv(o_lo, 4), fv(o_hi, 4),
            op=ALU.add)
        vop("tensor_scalar", fv(o_mid, 4), fv(o_mid, 4), 0.5, None,
            op0=ALU.mult, s=ve)
        count4(ALU.is_le, o_mid, iota_mode=True)
        vop("tensor_tensor", fv(o_ge4, 4), fv(o_bc, 4), fv(o_rr, 4),
            op=ALU.is_ge)
        vop("tensor_copy", ivv(o_im, 4), fv(o_ge4, 4))
        vop("select", fv(o_ss4, 4), ivv(o_im, 4), fv(o_mid, 4), fv(o_hi, 4))
        vop("tensor_copy", fv(o_hi, 4), fv(o_ss4, 4))
        vop("select", fv(o_ss4, 4), ivv(o_im, 4), fv(o_lo, 4), fv(o_mid, 4))
        vop("tensor_copy", fv(o_lo, 4), fv(o_ss4, 4), s=ve)
    for e in range(E):
        vop("tensor_scalar", eview(o_r1, e), eview(o_score, e),
            fv(o_th + e, 1), None, op0=ALU.is_gt)
        vop("tensor_scalar", eview(o_r2, e), fv(o_iota, 64),
            fv(o_hi + e, 1), None, op0=ALU.is_le)
        vop("tensor_mul", eview(o_r2, e), eview(o_r2, e), eview(o_eqm, e))
        vop("tensor_tensor", eview(o_sel, e), eview(o_r1, e), eview(o_r2, e),
            op=ALU.add)
    vop("tensor_mul", fv(o_p, 256), fv(o_p, 256), fv(o_sel, 256), s=ve)

    for e in range(E):
        vop("tensor_copy", bfv(o_selbf, 64), eview(o_sel, e), s=ve)
        wait("tensor", ve, ve.v)
        gwait("tensor", pbmg)
        mm(pbm[:, 64:128], bfv(o_utri, 128), bfv(o_selbf, 64), True, True)
        mm(pbm[0:1, 192:256], abf.ap[:, o_onesbf:o_onesbf + 1],
           bfv(o_selbf, 64), True, True, s=te)
        wait("vector", te, te.v)
        vop("tensor_copy", af.ap[0:1, o_colsum:o_colsum + 64],
            pbm[0:1, 192:256])
        vop("tensor_copy", af.ap[0:1, o_colp:o_colp + 64], pbm[0:1, 192:256])
        cur, oth = o_colp, o_cps
        sh = 1
        while sh < 64:
            vop("tensor_copy", af.ap[0:1, oth:oth + sh],
                af.ap[0:1, cur:cur + sh])
            vop("tensor_tensor", af.ap[0:1, oth + sh:oth + 64],
                af.ap[0:1, cur + sh:cur + 64],
                af.ap[0:1, cur:cur + 64 - sh], op=ALU.add)
            cur, oth = oth, cur
            sh *= 2
        vop("tensor_tensor", af.ap[0:1, o_colp:o_colp + 64],
            af.ap[0:1, cur:cur + 64],
            af.ap[0:1, o_colsum:o_colsum + 64], op=ALU.subtract, s=ve)
        wait("tensor", ve, ve.v)
        mm(pbm[:, 128:192], af.ap[0:1, o_onesf:o_onesf + 128],
           af.ap[0:1, o_colp:o_colp + 64], True, True, s=te)
        wait("vector", te, te.v)
        vop("tensor_copy", fv(o_r1, 64), pbm[:, 128:192])
        vop("tensor_tensor", eview(o_rank, e), pbm[:, 64:128],
            eview(o_sel, e), op=ALU.subtract)
        vop("tensor_tensor", eview(o_rank, e), eview(o_rank, e),
            fv(o_r1, 64), op=ALU.add, s=ve)
        pbmg.set(ve, ve.v)

    for src_b, dst_o in ((o_sel, o_mysel), (o_rank, o_myrank), (o_p, o_myp)):
        vop("memset", fv(dst_o, 64), 0.0)
        for e in range(E):
            vop("tensor_scalar", fv(o_r1, 64), eview(src_b, e),
                fv(o_wsel + e, 1), None, op0=ALU.mult)
            vop("tensor_add", fv(dst_o, 64), fv(dst_o, 64), fv(o_r1, 64))
    vop("tensor_scalar", fv(o_r1, 64), fv(o_myrank, 64), fv(o_hbase, 1),
        None, op0=ALU.subtract)
    vop("tensor_scalar", fv(o_r2, 64), fv(o_r1, 64), -0.5, None, op0=ALU.is_ge)
    vop("tensor_mul", fv(o_r2, 64), fv(o_r2, 64), fv(o_mysel, 64))
    vop("tensor_scalar", fv(o_mysel, 64), fv(o_r1, 64), float(HALF) - 0.5,
        None, op0=ALU.is_le)
    vop("tensor_mul", fv(o_r2, 64), fv(o_r2, 64), fv(o_mysel, 64))
    vop("tensor_copy", ivv(o_iadj, 64), fv(o_r1, 64))
    vop("memset", ivv(o_pi, 64), 127)
    vop("tensor_tensor", ivv(o_pi, 64), ivv(o_iadj, 64), ivv(o_pi, 64),
        op=ALU.bitwise_and)
    vop("memset", ivv(o_ish, 64), 3)
    vop("tensor_tensor", ivv(o_pi, 64), ivv(o_pi, 64), ivv(o_ish, 64),
        op=ALU.logical_shift_left)
    vop("memset", ivv(o_ish, 64), 7)
    vop("tensor_tensor", ivv(o_ish, 64), ivv(o_iadj, 64), ivv(o_ish, 64),
        op=ALU.logical_shift_right)
    vop("tensor_tensor", ivv(o_pi, 64), ivv(o_pi, 64), ivv(o_ish, 64),
        op=ALU.add)
    vop("tensor_copy", fv(o_pif, 64), ivv(o_pi, 64))
    vop("tensor_mul", fv(o_pif, 64), fv(o_pif, 64), fv(o_r2, 64))
    vop("tensor_scalar", fv(o_r1, 64), fv(o_r2, 64), -BIG, BIG, op0=ALU.mult,
        op1=ALU.add)
    vop("tensor_add", fv(o_pif, 64), fv(o_pif, 64), fv(o_r1, 64))
    vop("tensor_copy", ivv(o_pi, 64), fv(o_pif, 64))
    rec = af.ap[:, o_myrec:o_myrec + 128].rearrange("p (i z) -> p z i", z=2)
    vop("tensor_copy", rec[:, 0, :], fv(o_iota, 64))
    vop("tensor_copy", rec[:, 1, :], fv(o_myp, 64), s=ve)
    vop("memset", fv(o_rowa, 64), BIG)
    vop("memset", fv(o_rowb, 64), BIG)
    for e in range(E - 1, -1, -1):
        vop("tensor_scalar", fv(o_r1, 64), eview(o_rank, e), 1.0,
            float(CAP * e), op0=ALU.mult, op1=ALU.add)
        vop("tensor_copy", ivv(o_im, 64), eview(o_sel, e))
        vop("select", fv(o_ss64, 64), ivv(o_im, 64), fv(o_r1, 64),
            fv(o_rowa, 64))
        vop("tensor_copy", fv(o_rowa, 64), fv(o_ss64, 64))
    for e in range(E):
        vop("tensor_scalar", fv(o_r1, 64), eview(o_rank, e), 1.0,
            float(CAP * e), op0=ALU.mult, op1=ALU.add)
        vop("tensor_copy", ivv(o_im, 64), eview(o_sel, e))
        vop("select", fv(o_ss64, 64), ivv(o_im, 64), fv(o_r1, 64),
            fv(o_rowb, 64))
        vop("tensor_copy", fv(o_rowb, 64), fv(o_ss64, 64))
    vop("tensor_tensor", fv(o_r1, 64), eview(o_sel, 0), eview(o_sel, 1),
        op=ALU.add)
    vop("tensor_tensor", fv(o_r2, 64), eview(o_sel, 2), eview(o_sel, 3),
        op=ALU.add)
    vop("tensor_add", fv(o_r1, 64), fv(o_r1, 64), fv(o_r2, 64))
    vop("tensor_scalar", fv(o_r2, 64), fv(o_r1, 64), 1.5, None, op0=ALU.is_ge)
    vop("tensor_copy", ivv(o_im, 64), fv(o_r2, 64))
    vop("memset", fv(o_r1, 64), BIG)
    vop("select", fv(o_ss64, 64), ivv(o_im, 64), fv(o_rowb, 64), fv(o_r1, 64))
    vop("tensor_copy", fv(o_rowb, 64), fv(o_ss64, 64))
    rowi = af.ap[:, o_rowi:o_rowi + 128].rearrange("p (i z) -> p z i", z=2)
    vop("tensor_copy", rowi[:, 0, :], fv(o_rowa, 64))
    vop("tensor_copy", rowi[:, 1, :], fv(o_rowb, 64), s=ve)
    EDGE_ALL[0] = False
    VE_ROUTE = ve.v

    wait("gpsimd", ve, VE_ROUTE)
    dma("gpsimd", rowab_buf.ap().rearrange("(p i) z -> p (i z)", p=128),
        fv(o_rowi, 128), s_rab)

    def scat(e):
        with e.register("b1023") as r:
            e.reg_mov(r, HALF - 1)
            for i in range(64):
                e.indirect_dma_start(
                    out=idxp_buf.ap(),
                    out_offset=IndirectOffsetOnAxis(ivv(o_pi + i, 1), axis=0),
                    in_=af.ap[:, o_myrec + i * 2:o_myrec + i * 2 + 2],
                    in_offset=None, bounds_check=r,
                    oob_is_err=False).then_inc(s_scat.h, 16)
    op("gpsimd", scat)
    s_scat.inc(16 * 64)
    wait("gpsimd", s_scat, s_scat.v)
    dma("gpsimd", fv(o_rab, 16),
        idxp_buf.ap().rearrange("(p k) z -> p (k z)", p=128), s_idx)

    def rowabg(e):
        with e.register("brb") as r:
            e.reg_mov(r, N - 1)
            for k in range(8):
                e.indirect_dma_start(
                    out=af.ap[:, o_rab2 + k * 2:o_rab2 + k * 2 + 2],
                    out_offset=None, in_=rowab_buf.ap(),
                    in_offset=IndirectOffsetOnAxis(ivv(o_ownpm + k, 1), axis=0),
                    bounds_check=r, oob_is_err=False).then_inc(s_rabg.h, 16)
    op("gpsimd", rowabg)
    s_rabg.inc(16 * 8)
    wait("gpsimd", s_rab, s_rab.v)
    EDGE_ALL[0] = True
    wait("vector", s_idx, s_idx.v)
    wait("vector", s_rabg, s_rabg.v)
    rab = af.ap[:, o_rab:o_rab + 16].rearrange("p (k z) -> p z k", z=2)
    vop("tensor_copy", ivv(o_tok, 8), rab[:, 0, :])
    vop("tensor_copy", fv(o_ptil, 8), rab[:, 1, :])
    rab2 = af.ap[:, o_rab2:o_rab2 + 16].rearrange("p (k z) -> p z k", z=2)
    vop("tensor_copy", ivv(o_rowA8, 8), rab2[:, 0, :])
    vop("tensor_copy", ivv(o_rowB8, 8), rab2[:, 1, :], s=ve)
    EDGE_ALL[0] = False
    VE_TOK = ve.v

    wait("gpsimd", ve, VE_TOK)
    xg_gate = [Gate(), Gate()]
    xgs_gate = [Gate(), Gate()]
    gq_xg = {}
    o_xgf = af.take(256)
    for k in range(8):
        gwait("gpsimd", xg_gate[k % 2])

        def gat(e, _k=k):
            with e.register(f"bx{_k}") as r:
                e.reg_mov(r, N - 1)
                e.indirect_dma_start(
                    out=bfv(o_xg + (_k % 2) * 2048, 2048), out_offset=None,
                    in_=cc_xn_out.ap(),
                    in_offset=IndirectOffsetOnAxis(ivv(o_tok + _k, 1), axis=0),
                    bounds_check=r, oob_is_err=False).then_inc(
                        s_xg[_k % 2].h, 16)
        op("gpsimd", gat)
        gq_xg[k] = s_xg[k % 2].inc(16)
        wait("vector", s_xg[k % 2], gq_xg[k])
        for d16 in range(16):
            sl = d16 % 2
            gwait("vector", xgs_gate[sl])
            vop("tensor_copy", fv(o_xgf + sl * 128, 128),
                bfv(o_xg + (k % 2) * 2048 + d16 * 128, 128), s=ve)
            if d16 == 15:
                xg_gate[k % 2].set(ve, ve.v)
            wait("tensor", ve, ve.v)
            gwait("tensor", pbg[sl])
            petr(pb[sl][:, 0:128], fv(o_xgf + sl * 128, 128),
                 fv(o_identf, 128), s=te)
            xgs_gate[sl].set(te, te.v)
            wait("vector", te, te.v)
            vop("tensor_copy", bfv(o_xgt + d16 * 1024 + k * 128, 128),
                pb[sl][:, 0:128], s=ve)
            pbg[sl].set(ve, ve.v)
    VE_XGT = ve.v

    abf.push()
    o_pt = [abf.take(512) for _ in range(3)]
    o_drow = abf.take(N)
    o_dmul = abf.take(1024)
    o_negbf = abf.take(128)
    o_aoutT = abf.take(HPC * N)
    o_trimt4 = abf.take(2048)
    o_bias2 = af.take(512)
    s_tm4 = sem("s_tm4")
    dma("sync", bfv(o_trimt4, 2048), trimT4i, s_tm4)
    wait("vector", s_tm4, s_tm4.v)
    pt_gate = [Gate() for _ in range(3)]
    av_gate = [Gate(), Gate()]
    av_gate[0].set(pbg[4].sem, pbg[4].val)
    av_gate[1].set(pbg[5].sem, pbg[5].val)
    den_gate = [Gate(), Gate()]
    den_gate[1].set(pbg[3].sem, pbg[3].val)
    dmg = [Gate(), Gate()]
    pending_norm = [None]

    def gwait2(eng, ga, gb):
        if ga.sem is not None and gb.sem is not None and ga.sem is gb.sem:
            wait(eng, ga.sem, max(ga.val, gb.val))
        else:
            gwait(eng, ga)
            gwait(eng, gb)

    vop("memset", abf.ap[0:1, o_negbf:o_negbf + 128], -1.0, s=ve)
    wait("tensor", ve, ve.v)

    for h in range(HPC):
        if h == 1:
            wait("sync", te, te.v)
            for z in range(3):
                wait("sync", s_st[z], s_st[z].v)
            for z, dst in enumerate((o_qt, o_kt, o_vt)):
                dma("sync", bfv(dst, N), qkv1_buf.ap()[:, z * N:(z + 1) * N],
                    s_rel)
            wait("tensor", s_rel, s_rel.v)
            wait("vector", s_rel, s_rel.v)
        for GG in range(16):
            gwait("vector", dmg[GG % 2])
            vop("tensor_tensor", bfv(o_dmul + (GG % 2) * 512, 512),
                bfv(o_qt + GG * 512, 512), bfv(o_kt + GG * 512, 512),
                op=ALU.mult, s=ve)
            wait("tensor", ve, ve.v)
            gwait("tensor", pbmg)
            mm(pbm[0:1, 0:512], abf.ap[:, o_onesbf:o_onesbf + 1],
               bfv(o_dmul + (GG % 2) * 512, 512), True, True, s=te)
            dmg[GG % 2].set(te, te.v)
            wait("vector", te, te.v)
            vop("tensor_copy", abf.ap[0:1, o_drow + GG * 512:
                                      o_drow + (GG + 1) * 512],
                pbm[0:1, 0:512], s=ve)
            pbmg.set(ve, ve.v)

        for G in range(16):
            nkj = 4 * G + 4
            avbank = pb[4 + (G % 2)]
            dend = pb[6 if G % 2 == 0 else 3]
            gwait("tensor", pbmg)
            wait("tensor", ve, ve.v)
            mm(pbm[:, 0:512], abf.ap[0:1, o_negbf:o_negbf + 128],
               abf.ap[0:1, o_drow + G * 512:o_drow + (G + 1) * 512],
               True, True, s=te)
            wait("vector", te, te.v)
            vop("tensor_copy", fv(o_bias2, 512), pbm[:, 0:512], s=ve)
            pbmg.set(ve, ve.v)
            gwait("tensor", den_gate[G % 2])
            gwait("tensor", av_gate[G % 2])
            smm_ts = {}
            exp_ts = {}
            for pj in range(nkj // 2 + 1):
                if pj < nkj // 2:
                    c0, c1 = 2 * pj, 2 * pj + 1
                    gwait2("tensor", pbg[c0 % 3], pbg[c1 % 3])
                    for c in (c0, c1):
                        smm_ts[c] = mm(pb[c % 3][:, 0:512],
                                       bfv(o_kt + c * 128, 128),
                                       bfv(o_qt + G * 512, 512), True, True,
                                       s=te)
                if pj == 1 and pending_norm[0] is not None:
                    pending_norm[0]()
                    pending_norm[0] = None
                if pj >= 1:
                    p0, p1 = 2 * pj - 2, 2 * pj - 1
                    wait("tensor", se, exp_ts[p1])
                    for p in (p0, p1):
                        mm(avbank[:, 0:512], bfv(o_vt + p * 128, 128),
                           bfv(o_pt[p % 3], 512), p == 0, p == nkj - 1)
                        mm(dend[0:1, 0:512],
                           abf.ap[:, o_onesbf:o_onesbf + 1],
                           bfv(o_pt[p % 3], 512), p == 0, p == nkj - 1, s=te)
                        pt_gate[p % 3].set(te, te.v)
                if pj < nkj // 2:
                    wait("vector", te, smm_ts[c1])
                    for c in (c0, c1):
                        sl = ve if c == c1 and c < 4 * G else None
                        vop("tensor_tensor", pb[c % 3][:, 0:512],
                            pb[c % 3][:, 0:512], fv(o_bias2, 512), op=ALU.add,
                            s=sl)
                        if c >= 4 * G:
                            vop("tensor_tensor", pb[c % 3][:, 0:512],
                                pb[c % 3][:, 0:512],
                                bfv(o_trimt4 + (c - 4 * G) * 512, 512),
                                op=ALU.add, s=ve if c == c1 else None)
                    wait("scalar", ve, ve.v)
                    gwait2("scalar", pt_gate[c0 % 3], pt_gate[c1 % 3])
                    for c in (c0, c1):
                        act(bfv(o_pt[c % 3], 512), pb[c % 3][:, 0:512],
                            AF.Exp, scale=SCALE, s=se)
                        exp_ts[c] = se.v
                        pbg[c % 3].set(se, se.v)
            av_done = te.v

            def mknorm(_h=h, _G=G, _av=avbank, _dend=dend, _done=av_done):
                def emit():
                    wait("vector", te, _done)
                    vop("reciprocal", af.ap[0:1, o_ir1:o_ir1 + 512],
                        _dend[0:1, 0:512], s=ve)
                    den_gate[_G % 2].set(ve, ve.v)
                    wait("tensor", ve, ve.v)
                    gwait("tensor", pbmg)
                    mm(pbm[:, 0:512], af.ap[0:1, o_onesf:o_onesf + 128],
                       af.ap[0:1, o_ir1:o_ir1 + 512], True, True, s=te)
                    wait("vector", te, te.v)
                    vop("tensor_copy", fv(o_invr, 512), pbm[:, 0:512], s=ve)
                    pbmg.set(ve, ve.v)
                    vop("scalar_tensor_tensor",
                        bfv(o_aoutT + _h * N + _G * 512, 512),
                        _av[:, 0:512], 1.0, fv(o_invr, 512),
                        op0=ALU.mult, op1=ALU.mult, s=ve)
                    av_gate[_G % 2].set(ve, ve.v)
                return emit
            pending_norm[0] = mknorm()
        pending_norm[0]()
        pending_norm[0] = None
        wait("sync", ve, ve.v)
        dma("sync", cc_ao_in[h].ap(), bfv(o_aoutT + h * N, N), s_ao)
        wait("gpsimd", s_ao, s_ao.v)
        coll("AllGather", ALU.bypass, [cc_ao_in[h].ap().opt()],
             [cc_ao_out[h].ap().rearrange("(f g) x -> f (g x)", g=8).opt()],
             cao)
    P2_VE = ve.v
    P2_SE = se.v
    for t in range(3):
        pbg[t].set(se, P2_SE)
    pbg[3].set(ve, P2_VE)
    pbg[4].set(ve, P2_VE)
    pbg[5].set(ve, P2_VE)
    pbg[6].set(ve, P2_VE)

    abf.pop()
    abf.pop()

    af.push()
    o_sst = [af.take(512) for _ in range(4)]
    sst_gate = [Gate() for _ in range(4)]
    abf.push()
    o_alh = abf.take(16 * 1024)
    o_wo = [abf.take(16 * 512) for _ in range(2)]
    wo_gate = [Gate(), Gate()]
    wo_ld = {}

    def emit_wo_load(cb):
        b = cb % 2
        gwait("gpsimd", wo_gate[b])
        src = woutFi[:, cb * 512:(cb + 1) * 512] \
            .rearrange("(a p) c -> p a c", p=128)
        dma("gpsimd", bfv(o_wo[b], 16 * 512), src, s_wo[b])
        wo_ld[cb] = s_wo[b].v

    wait("gpsimd", cao, 2)
    for a in range(16):
        def g(e, _a=a):
            with e.register(f"ba{_a}") as r:
                e.reg_mov(r, NCORES * 128 * 8 - 1)
                e.indirect_dma_start(
                    out=bfv(o_alh + _a * 1024, 1024), out_offset=None,
                    in_=cc_ao_out[_a % 2].ap(),
                    in_offset=IndirectOffsetOnAxis(ivv(o_aoix + _a, 1),
                                                   axis=0),
                    bounds_check=r, oob_is_err=False).then_inc(s_alh.h, 16)
        op("gpsimd", g)
        s_alh.inc(16)
    emit_wo_load(0)
    emit_wo_load(1)

    wait("tensor", s_alh, s_alh.v)
    wait("tensor", ve, P2_VE)
    for cb in range(4):
        b = cb % 2
        wait("tensor", s_wo[b], wo_ld[cb])
        for k in range(8):
            bank = (cb * 8 + k) % 4
            gwait("tensor", pbg[bank])
            for a in range(16):
                mm(pb[bank][:, 0:512], bfv(o_alh + a * 1024 + k * 128, 128),
                   bfv(o_wo[b] + a * 512, 512), a == 0, a == 15,
                   s=te if a == 15 else None)
            if k == 7:
                wo_gate[b].set(te, te.v)
            wait("vector", te, te.v)
            gwait("vector", sst_gate[bank])
            vop("tensor_copy", fv(o_sst[bank], 512), pb[bank][:, 0:512], s=ve)
            pbg[bank].set(ve, ve.v)
            wait("sync", ve, ve.v)
            dma("sync", cc_sown.ap()[k * 128:(k + 1) * 128,
                                     cb * 512:(cb + 1) * 512],
                fv(o_sst[bank], 512), s_sst)
            sst_gate[bank].set(s_sst, s_sst.v)
        if cb + 2 < 4:
            emit_wo_load(cb + 2)
    P3_TE = te.v
    abf.pop()
    af.pop()

    abf.push()
    o_w1b = [abf.take(2048) for _ in range(2)]
    o_w2sl = [abf.take(16 * 512) for _ in range(2)]
    o_ht = abf.take(16 * 1024)
    o_yacc = abf.take(8 * 2048)
    o_yst = [abf.take(512) for _ in range(2)]
    w1_gate = [Gate(), Gate()]
    w2_gate = [Gate(), Gate()]
    hps_gate = [Gate(), Gate()]
    hps_gate[0].set(ve, P2_VE)
    hps_gate[1].set(pbmg.sem, pbmg.val)
    ht_gate = Gate()
    yst_gate = [Gate(), Gate()]
    w1_ld = {}
    w2_ld = {}

    def emit_w1_load(q, ht):
        hb = ht % 2
        gwait("gpsimd", w1_gate[hb])
        src = w1i[:, (q * 16 + ht) * 128:(q * 16 + ht + 1) * 128] \
            .rearrange("(a p) h -> p a h", p=128)
        dma("gpsimd", bfv(o_w1b[hb], 2048), src, s_w1[hb])
        w1_ld[(q, ht)] = (hb, s_w1[hb].v)

    def emit_w2_load(q, cb):
        wb = cb % 2
        gwait("gpsimd", w2_gate[wb])
        src = w2i[q * 2048:(q + 1) * 2048, cb * 512:(cb + 1) * 512] \
            .rearrange("(a p) c -> p a c", p=128)
        dma("gpsimd", bfv(o_w2sl[wb], 16 * 512), src, s_w2[wb])
        w2_ld[(q, cb)] = (wb, s_w2[wb].v)

    wait("gpsimd", te, P3_TE)
    emit_w1_load(0, 0)
    emit_w1_load(0, 1)
    for q in range(4):
        gwait("scalar", ht_gate)
        for ht in range(16):
            hb = ht % 2
            _hb, _v = w1_ld[(q, ht)]
            wait("tensor", s_w1[_hb], _v)
            for sb in range(2):
                bank = pb[4] if sb == 0 else pbm
                gwait("tensor", hps_gate[sb])
                for d16 in range(16):
                    mm(bank[:, 0:512], bfv(o_w1b[hb] + d16 * 128, 128),
                       bfv(o_xgt + d16 * 1024 + sb * 512, 512),
                       d16 == 0, d16 == 15, s=te if d16 == 15 else None)
                if sb == 1:
                    w1_gate[hb].set(te, te.v)
                wait("scalar", te, te.v)
                act(bfv(o_ht + ht * 1024 + sb * 512, 512), bank[:, 0:512],
                    AF.Gelu, s=se)
                hps_gate[sb].set(se, se.v)
            nq, nht = (q, ht + 2) if ht + 2 < 16 else (q + 1, ht + 2 - 16)
            if nq < 4:
                emit_w1_load(nq, nht)
        se_h = se.v
        if q == 0:
            emit_w2_load(0, 0)
            emit_w2_load(0, 1)
        for cb in range(4):
            wb = cb % 2
            _wb, _v = w2_ld[(q, cb)]
            wait("tensor", s_w2[_wb], _v)
            wait("tensor", se, se_h)
            for stp in range(2):
                for st in range(4):
                    gwait("tensor", pbg[st])
                for ht2 in range(16):
                    for st in range(4):
                        sg = stp * 4 + st
                        mm(pb[st][:, 0:512],
                           bfv(o_ht + ht2 * 1024 + sg * 128, 128),
                           bfv(o_w2sl[wb] + ht2 * 512, 512),
                           ht2 == 0, ht2 == 15,
                           s=te if (ht2 == 15 and st == 3) else None)
                if cb == 3 and stp == 1:
                    ht_gate.set(te, te.v)
                if stp == 1:
                    w2_gate[wb].set(te, te.v)
                wait("vector", te, te.v)
                for st in range(4):
                    sg = stp * 4 + st
                    ya = bfv(o_yacc + sg * 2048 + cb * 512, 512)
                    if q == 0:
                        vop("tensor_copy", ya, pb[st][:, 0:512],
                            s=ve if st == 3 else None)
                    else:
                        vop("tensor_tensor", ya, pb[st][:, 0:512], ya,
                            op=ALU.add, s=ve if st == 3 else None)
                    if st == 3:
                        for st2 in range(4):
                            pbg[st2].set(ve, ve.v)
            nq, ncb = (q, cb + 2) if cb + 2 < 4 else (q + 1, cb + 2 - 4)
            if nq < 4:
                emit_w2_load(nq, ncb)
    for sg in range(8):
        for cb in range(4):
            b = (sg * 4 + cb) % 2
            gwait("vector", yst_gate[b])
            vop("tensor_scalar", bfv(o_yst[b], 512),
                bfv(o_yacc + sg * 2048 + cb * 512, 512),
                fv(o_ptil + sg, 1), None, op0=ALU.mult, s=ve)
            wait("sync", ve, ve.v)
            dma("sync",
                cc_y_in.ap()[sg * 128:(sg + 1) * 128, cb * 512:(cb + 1) * 512],
                bfv(o_yst[b], 512), s_yst[b])
            yst_gate[b].set(s_yst[b], s_yst[b].v)
    wait("gpsimd", s_yst[0], s_yst[0].v)
    wait("gpsimd", s_yst[1], s_yst[1].v)
    coll("AllGather", ALU.bypass, [cc_y_in.ap().opt()],
         [cc_y_out.ap().opt()], cy)
    abf.pop()
    abf.pop()

    abf.push()
    o_ya = abf.take(2048)
    o_yb = abf.take(2048)
    af.push()
    o_g = af.take(2048)
    o_xt = af.take(2048)
    o_st5 = af.take(2048)

    wait("gpsimd", cy, 1)
    wait("sync", s_sst, s_sst.v)
    vop("memset", bfv(o_ya, 2048), 0.0)
    vop("memset", bfv(o_yb, 2048), 0.0, s=ve)
    for k in range(8):
        wait("gpsimd", ve, ve.v)

        def gath(e, _k=k):
            with e.register(f"by{_k}") as r:
                e.reg_mov(r, NCORES * HALF - 1)
                e.indirect_dma_start(
                    out=bfv(o_ya, 2048), out_offset=None, in_=cc_y_out.ap(),
                    in_offset=IndirectOffsetOnAxis(ivv(o_rowA8 + _k, 1),
                                                   axis=0),
                    bounds_check=r, oob_is_err=False).then_inc(s_y5.h, 16)
                e.indirect_dma_start(
                    out=bfv(o_yb, 2048), out_offset=None, in_=cc_y_out.ap(),
                    in_offset=IndirectOffsetOnAxis(ivv(o_rowB8 + _k, 1),
                                                   axis=0),
                    bounds_check=r, oob_is_err=False).then_inc(s_y5.h, 16)
        op("gpsimd", gath)
        s_y5.inc(32)
        if k >= 1:
            wait("sync", s_out, s_out.v)
        dma("sync", fv(o_xt, 2048), x_sl[k * 128:(k + 1) * 128, :], s_p5x)
        if k >= 1:
            wait("sync", ve, ve.v)
        dma("sync", fv(o_st5, 2048),
            cc_sown.ap()[k * 128:(k + 1) * 128, :], s_p5s)
        wait("vector", s_y5, s_y5.v)
        vop("tensor_add", bfv(o_ya, 2048), bfv(o_ya, 2048), bfv(o_yb, 2048),
            s=ve)
        wait("scalar", ve, ve.v)
        act(fv(o_g, 2048), bfv(o_ya, 2048), AF.Silu, s=se)
        wait("vector", se, se.v)
        wait("vector", s_p5x, s_p5x.v)
        wait("vector", s_p5s, s_p5s.v)
        vop("tensor_mul", fv(o_g, 2048), fv(o_g, 2048), fv(o_st5, 2048),
            s=ve)
        wait("vector", ve, ve.v)
        vop("tensor_add", fv(o_xt, 2048), fv(o_xt, 2048), fv(o_g, 2048), s=ve)
        wait("sync", ve, ve.v)
        dma("sync", out_sl[k * 128:(k + 1) * 128, :], fv(o_xt, 2048), s_out)
        if k < 7:
            vop("memset", bfv(o_ya, 2048), 0.0)
            vop("memset", bfv(o_yb, 2048), 0.0, s=ve)
    if debug:
        s_dbg = sem("s_dbg")
        wait("gpsimd", s_out, s_out.v)
        dma("gpsimd", dbg_o["d_xn"], cc_xn_out.ap(), s_dbg)
        dma("gpsimd", dbg_o["d_lg"], fv(o_logits, 256), s_dbg)
        dma("gpsimd", dbg_o["d_rt"][:, 0:256], fv(o_sel, 256), s_dbg)
        dma("gpsimd", dbg_o["d_rt"][:, 256:512], fv(o_rank, 256), s_dbg)
        dma("gpsimd", dbg_o["d_rt"][:, 512:768], fv(o_p, 256), s_dbg)
        dma("gpsimd", dbg_o["d_rt"][:, 768:832], fv(o_mysel, 64), s_dbg)
        dma("gpsimd", dbg_o["d_rt"][:, 832:896], fv(o_pif, 64), s_dbg)
        dma("gpsimd", dbg_o["d_rt"][:, 896:900], fv(o_th, 4), s_dbg)
        dma("gpsimd", dbg_o["d_rt"][:, 900:904], fv(o_ptil + 0, 4), s_dbg)
        dma("gpsimd", dbg_o["d_rab"], rowab_buf.ap(), s_dbg)
        dma("gpsimd", dbg_o["d_idx"], idxp_buf.ap(), s_dbg)
        dma("gpsimd", dbg_o["d_qkv1"], qkv1_buf.ap(), s_dbg)
        dma("gpsimd", dbg_o["d_s"], cc_sown.ap(), s_dbg)
        dma("gpsimd", dbg_o["d_y"], cc_y_out.ap(), s_dbg)
        wait("gpsimd", s_dbg, s_dbg.v)
    wait("sync", s_out, s_out.v)

    nc._dbg = {k: v for k, v in locals().items()
               if isinstance(v, int) and k.startswith("o_")}
    with nc.Block() as block:
        @block.sync
        def _(e):
            for f in prog["sync"]:
                f(e)

        @block.tensor
        def _(e):
            for f in prog["tensor"]:
                f(e)

        @block.vector
        def _(e):
            for f in prog["vector"]:
                f(e)

        @block.scalar
        def _(e):
            for f in prog["scalar"]:
                f(e)

        @block.gpsimd
        def _(e):
            for f in prog["gpsimd"]:
                f(e)

    return nc


def _host_inputs(inputs):
    x = np.asarray(inputs["L0_attn_x"]).reshape(N, D).astype(np.float32)
    w_qkv = np.asarray(inputs["L0_attn_w_qkv"]).astype(np.float32)
    w_out = np.asarray(inputs["L0_attn_w_out"]).astype(np.float32)
    Wg = np.asarray(inputs["L0_ffn_Wg"]).astype(np.float32)
    bg = np.asarray(inputs["L0_ffn_bg"]).astype(np.float32)
    W1 = np.asarray(inputs["L0_ffn_W1"]).astype(np.float32)
    W2 = np.asarray(inputs["L0_ffn_W2"]).astype(np.float32)

    invf = (np.float32(1.0) / np.float32(10000.0) **
            (np.arange(0, 32, 2, dtype=np.float32) / np.float32(32)))
    freqs = np.arange(N, dtype=np.float32)[:, None] * invf[None, :]
    cos = np.cos(freqs.astype(np.float64)).astype(np.float32)
    sin = np.sin(freqs.astype(np.float64)).astype(np.float32)
    rotA = np.zeros((64, N), np.float32)
    rotB = np.zeros((64, N), np.float32)
    rotA[0:16] = cos.T
    rotA[16:32] = 1.0
    rotA[32:48] = -sin.T
    rotB[0:16] = sin.T
    rotB[32:48] = cos.T
    rotB[48:64] = 1.0

    trimask = np.where(np.triu(np.ones((128, 128), bool), 1),
                       np.float32(-1e30), np.float32(0.0)).astype(np.float32)
    trimT4 = np.zeros((128, 2048), np.float32)
    for t in range(4):
        trimT4[:, t * 512:t * 512 + t * 128] = -1e30
        trimT4[:, t * 512 + t * 128:t * 512 + (t + 1) * 128] = trimask.T

    common = {
        "rotAi": rotA.astype(bfdt),
        "rotBi": rotB.astype(bfdt),
        "trimT4i": trimT4.astype(bfdt),
        "identbfi": np.eye(128).astype(bfdt),
        "identf32i": np.eye(128).astype(np.float32),
        "utrii": (np.arange(128)[:, None] <= np.arange(128)[None, :])
        .astype(bfdt),
        "onesbfi": np.ones((128, 128), bfdt),
        "onesf32i": np.ones((128, 128), np.float32),
        "wgi": Wg,
        "bgi": bg.reshape(1, E),
    }

    in_maps = []
    for c in range(NCORES):
        h0 = HPC * c
        rows = []
        for blk in range(3):
            for hh in range(HPC):
                r = w_qkv[blk * 2048 + (h0 + hh) * DH:
                          blk * 2048 + (h0 + hh + 1) * DH]
                if blk < 2:
                    r = r[PERM]
                rows.append(r)
        wqkvT_c = np.ascontiguousarray(np.concatenate(rows, 0).T)

        e = c // 2
        wsel_c = np.zeros((128, E), np.float32)
        wsel_c[:, e] = 1.0
        q = np.arange(128)[:, None]
        kk = np.arange(8)[None, :]
        in_maps.append(dict(common, **{
            "x_sl": np.ascontiguousarray(x[c * TPC:(c + 1) * TPC]),
            "wqkvTi": wqkvT_c,
            "woutFi": np.ascontiguousarray(w_out.T),
            "aoixi": (((np.arange(16)[None, :] // 2) * 128 +
                       np.arange(128)[:, None]) * 8 + c).astype(np.int32),
            "w1i": np.ascontiguousarray(W1[e]),
            "w2i": np.ascontiguousarray(W2[e]),
            "wseli": wsel_c,
            "hbasei": np.full((128, 1), 1024.0 * (c % 2), np.float32),
            "ownpmi": (q * 64 + 8 * c + kk).astype(np.int32),
        }))
    return in_maps


def kernel(**inputs):
    key = ("ncd" if DEBUG else "nc")
    if key not in _CACHE:
        _CACHE[key] = build_program(debug=DEBUG)
    nc = _CACHE[key]
    in_maps = _host_inputs(inputs)
    res = run_bass_kernel_spmd(nc, in_maps, core_ids=list(range(NCORES)))
    if DEBUG:
        _CACHE["dbg_results"] = res.results
    out = np.concatenate([r["out_sl"] for r in res.results], axis=0)
    return out.reshape(1, N, D).astype(np.float32)


# version 09
# speedup vs baseline: 1.0878x; 1.0878x over previous
import numpy as np
import ml_dtypes

import concourse.bass as bass
import concourse.mybir as mybir
from concourse.bass import IndirectOffsetOnAxis
from concourse.bass_utils import run_bass_kernel_spmd

F32 = mybir.dt.float32
BF16 = mybir.dt.bfloat16
I32 = mybir.dt.int32
AF = mybir.ActivationFunctionType
ALU = mybir.AluOpType
AX = mybir.AxisListType
bfdt = ml_dtypes.bfloat16

NCORES = 8
N = 8192
D = 2048
DH = 128
HPC = 2
TPC = N // NCORES
E = 4
CAP = 2048
HALF = CAP // 2
HDIM = 8192
SCALE = float(DH) ** -0.5
BIG = 99999.0
NBLK = 16
NTILE = 64

PERM = np.concatenate([np.arange(0, 16), np.arange(32, 48),
                       np.arange(16, 32), np.arange(48, 128)])

_CACHE = {}
DEBUG = False


class Sem:
    def __init__(self, h):
        self.h = h
        self.v = 0

    def inc(self, n=1):
        self.v += n
        return self.v


class Gate:
    def __init__(self):
        self.sem = None
        self.val = 0

    def set(self, sem, val):
        self.sem = sem
        self.val = val


class Arena:
    def __init__(self, ap, cols, name):
        self.ap = ap
        self.cols = cols
        self.off = 0
        self.marks = []
        self.name = name
        self.peak = 0

    def take(self, w, align=16):
        if self.off % align:
            self.off += align - (self.off % align)
        o = self.off
        self.off += w
        self.peak = max(self.peak, self.off)
        assert self.off <= self.cols, f"{self.name} overflow {self.off}>{self.cols}"
        return o

    def push(self):
        self.marks.append(self.off)

    def pop(self):
        self.off = self.marks.pop()


def build_program(debug=False):
    nc = bass.Bass("TRN2", target_bir_lowering=False)

    def ext(name, shape, dt):
        return nc.dram_tensor(name, shape, dt, kind="ExternalInput").ap()

    x_sl = ext("x_sl", [TPC, D], F32)
    wqkvTi = ext("wqkvTi", [D, 768], F32)
    woutFi = ext("woutFi", [D, D], F32)
    aoixi = ext("aoixi", [128, 16], I32)
    wgi = ext("wgi", [D, E], F32)
    bgi = ext("bgi", [1, E], F32)
    w1i = ext("w1i", [D, HDIM], F32)
    w2i = ext("w2i", [HDIM, D], F32)
    rotAi = ext("rotAi", [64, N], BF16)
    rotBi = ext("rotBi", [64, N], BF16)
    trimT4i = ext("trimT4i", [128, 2048], BF16)
    identbfi = ext("identbfi", [128, 128], BF16)
    identf32i = ext("identf32i", [128, 128], F32)
    utrii = ext("utrii", [128, 128], BF16)
    onesbfi = ext("onesbfi", [128, 128], BF16)
    onesf32i = ext("onesf32i", [128, 128], F32)
    wseli = ext("wseli", [128, E], F32)
    hbasei = ext("hbasei", [128, 1], F32)
    ownpmi = ext("ownpmi", [128, 8], I32)

    out_sl = nc.dram_tensor("out_sl", [TPC, D], F32, kind="ExternalOutput").ap()

    cc_xn_in = nc.dram_tensor("cc_xn_in", [TPC, D], BF16)
    cc_xn_out = nc.dram_tensor("cc_xn_out", [N, D], BF16)
    cc_ao_in = [nc.dram_tensor(f"cc_ao_in{h}", [128, N], BF16)
                for h in range(2)]
    cc_ao_out = [nc.dram_tensor(f"cc_ao_out{h}", [NCORES * 128 * 8, N // 8],
                                BF16) for h in range(2)]
    cc_sown = nc.dram_tensor("cc_sown", [TPC, D], F32)
    cc_y_in = nc.dram_tensor("cc_y_in", [HALF, D], BF16)
    cc_y_out = nc.dram_tensor("cc_y_out", [NCORES * HALF, D], BF16)
    idxp_buf = nc.dram_tensor("idxp_buf", [HALF, 2], F32)
    rowab_buf = nc.dram_tensor("rowab_buf", [N, 2], F32)
    qkv1_buf = nc.dram_tensor("qkv1_buf", [128, 3 * N], BF16)
    if debug:
        dbg_o = {
            "d_xn": nc.dram_tensor("d_xn", [N, D], BF16,
                                   kind="ExternalOutput").ap(),
            "d_lg": nc.dram_tensor("d_lg", [128, 256], F32,
                                   kind="ExternalOutput").ap(),
            "d_rt": nc.dram_tensor("d_rt", [128, 1024], F32,
                                   kind="ExternalOutput").ap(),
            "d_rab": nc.dram_tensor("d_rab", [N, 2], F32,
                                    kind="ExternalOutput").ap(),
            "d_idx": nc.dram_tensor("d_idx", [HALF, 2], F32,
                                    kind="ExternalOutput").ap(),

            "d_qkv1": nc.dram_tensor("d_qkv1", [128, 3 * N], BF16,
                                     kind="ExternalOutput").ap(),
            "d_s": nc.dram_tensor("d_s", [1024, D], F32,
                                  kind="ExternalOutput").ap(),
            "d_y": nc.dram_tensor("d_y", [NCORES * HALF, D], BF16,
                                  kind="ExternalOutput").ap(),
        }

    import contextlib
    es = contextlib.ExitStack()
    nc._keepalive = es

    prog = {k: [] for k in ("sync", "tensor", "vector", "scalar", "gpsimd")}

    def op(eng, fn):
        prog[eng].append(fn)

    def sem(name):
        return Sem(es.enter_context(nc.semaphore(name)))

    te = sem("te")
    ve = sem("ve")
    se = sem("sescal")
    geio = sem("geio")
    cxn = sem("cxn")
    cy = sem("cy")
    s_const = sem("s_const")
    s_lnx = sem("s_lnx")
    s_xni = sem("s_xni")
    s_w0 = sem("s_w0")
    s_xnt = [sem("s_xnt0"), sem("s_xnt1")]
    s_rot = [sem("s_rot0"), sem("s_rot1")]
    s_st = [sem("s_st0"), sem("s_st1"), sem("s_st2")]
    s_rel = sem("s_rel")
    s_ao = sem("s_ao")
    cao = sem("cao")
    s_alh = sem("s_alh")
    s_wo = [sem("s_wo0"), sem("s_wo1")]
    s_sst = sem("s_sst")
    s_rab = sem("s_rab")
    s_scat = sem("s_scat")
    s_idx = sem("s_idx")
    s_rabg = sem("s_rabg")
    s_xg = [sem("s_xg0"), sem("s_xg1")]
    s_wout = sem("s_wout")
    s_w1 = [sem("s_w10"), sem("s_w11")]
    s_w2 = [sem("s_w20"), sem("s_w21")]
    s_yst = [sem("s_yst0"), sem("s_yst1")]
    s_y5 = sem("s_y5")
    s_p5x = sem("s_p5x")
    s_p5s = sem("s_p5s")
    s_out = sem("s_out")

    def wait(eng, s, v):
        if v <= 0:
            return
        op(eng, lambda e, _h=s.h, _v=v: e.wait_ge(_h, _v))

    def gwait(eng, gate):
        if gate.sem is not None and gate.val > 0:
            wait(eng, gate.sem, gate.val)

    def dma(eng, out, in_, s, transpose=False):
        def f(e, _o=out, _i=in_, _h=s.h, _t=transpose):
            if _t:
                e.dma_start(out=_o, in_=_i, transpose=True).then_inc(_h, 16)
            else:
                e.dma_start(out=_o, in_=_i).then_inc(_h, 16)
        op(eng, f)
        return s.inc(16)

    def mm(out, lhsT, rhs, start, stop, s=None):
        def f(e, _o=out, _l=lhsT, _r=rhs, _a=start, _b=stop, _s=s):
            i = e.matmul(_o, _l, _r, start=_a, stop=_b)
            if _s is not None:
                i.then_inc(_s.h, 1)
        op("tensor", f)
        return s.inc() if s is not None else None

    def petr(out, in_, ident, s=None):
        def f(e, _o=out, _i=in_, _d=ident, _s=s):
            i = e.transpose(_o, _i, _d)
            if _s is not None:
                i.then_inc(_s.h, 1)
        op("tensor", f)
        return s.inc() if s is not None else None

    EDGE_ALL = [False]

    def vop(meth, *args, s=None, eng="vector", **kw):
        if EDGE_ALL[0] and eng == "vector" and s is None:
            s = ve

        def f(e, _m=meth, _a=args, _k=kw, _s=s):
            i = getattr(e, _m)(*_a, **_k)
            if _s is not None:
                i.then_inc(_s.h, 1)
        op(eng, f)
        r = s.inc() if s is not None else None
        if EDGE_ALL[0] and eng == "vector":
            wait("vector", ve, ve.v)
        return r

    def act(out, in_, func, bias=0.0, scale=1.0, accum=None, s=None):
        def f(e, _o=out, _i=in_, _f=func, _b=bias, _c=scale, _ac=accum, _s=s):
            if _ac is not None:
                i = e.activation(_o, _i, _f, bias=_b, scale=_c, accum_out=_ac)
            else:
                i = e.activation(_o, _i, _f, bias=_b, scale=_c)
            if _s is not None:
                i.then_inc(_s.h, 1)
        op("scalar", f)
        return s.inc() if s is not None else None

    def coll(kind, alu, ins, outs, s):
        def f(e, _k=kind, _a=alu, _i=ins, _o=outs, _h=s.h):
            e.collective_compute(_k, _a, replica_groups=[list(range(NCORES))],
                                 ins=_i, outs=_o).then_inc(_h, 1)
        op("gpsimd", f)
        return s.inc()

    ABF, AFC, AIC = 81056, 12048, 608
    abf_t = nc.alloc_sbuf_tensor("abf", [128, ABF], BF16)
    af_t = nc.alloc_sbuf_tensor("af32", [128, AFC], F32)
    ai_t = nc.alloc_sbuf_tensor("ai32", [128, AIC], I32)
    abf = Arena(abf_t.ap(), ABF, "bf")
    af = Arena(af_t.ap(), AFC, "f32")
    ai = Arena(ai_t.ap(), AIC, "i32")

    def bfv(o, w):
        return abf.ap[:, o:o + w]

    def fv(o, w):
        return af.ap[:, o:o + w]

    def ivv(o, w):
        return ai.ap[:, o:o + w]

    pb = [es.enter_context(nc.psum_tensor(f"pb{i}", [128, 512], F32))
          for i in range(7)]
    pbm = es.enter_context(nc.psum_tensor("pbm", [128, 512], F32))
    pbg = [Gate() for _ in range(7)]
    pbmg = Gate()

    o_identbf = abf.take(128)
    o_utri = abf.take(128)
    o_onesbf = abf.take(128)
    o_selbf = abf.take(64)
    o_identf = af.take(128)
    o_onesf = af.take(128)
    o_wsel = af.take(E)
    o_hbase = af.take(1)
    o_iota = af.take(64)
    o_logits = af.take(256)
    o_t1 = af.take(64)
    o_cnt = af.take(256)
    o_p = af.take(256)
    o_score = af.take(256)
    o_eqm = af.take(256)
    o_rank = af.take(256)
    o_sel = af.take(256)
    o_r1 = af.take(256)
    o_r2 = af.take(256)
    o_cnt4 = af.take(4)
    o_lo = af.take(4)
    o_hi = af.take(4)
    o_mid = af.take(4)
    o_bc = af.take(4)
    o_ge4 = af.take(4)
    o_th = af.take(4)
    o_thbc = af.take(4)
    o_rr = af.take(4)
    o_colsum = af.take(64)
    o_colp = af.take(64)
    o_cps = af.take(64)
    o_capc = af.take(1)
    o_myrec = af.take(128)
    o_rowa = af.take(64)
    o_rowb = af.take(64)
    o_rowi = af.take(128)
    o_mysel = af.take(64)
    o_myrank = af.take(64)
    o_myp = af.take(64)
    o_pif = af.take(64)
    o_rab = af.take(16)
    o_rab2 = af.take(16)
    o_ptil = af.take(8)
    o_invr = af.take(512)
    o_ir1 = af.take(512)
    o_eps = af.take(1)
    o_ss4 = af.take(4)
    o_ss64 = af.take(64)
    o_ownpm = ai.take(8)
    o_aoix = ai.take(16)
    o_pi = ai.take(64)
    o_iadj = ai.take(64)
    o_ish = ai.take(64)
    o_tok = ai.take(8)
    o_rowA8 = ai.take(8)
    o_rowB8 = ai.take(8)
    o_im = ai.take(64)

    abf.push()
    o_xg = abf.take(2 * 2048)
    o_xgt = abf.take(16 * 1024)
    abf.push()
    o_qt = abf.take(N)
    o_kt = abf.take(N)
    o_vt = abf.take(N)
    abf.push()
    o_wqkvT = abf.take(16 * 768)
    o_wg = abf.take(16 * E)
    o_bg = abf.take(E)
    o_xnt = [abf.take(16 * 512) for _ in range(2)]
    o_xnstage = abf.take(2048)
    o_rot = [abf.take(1024) for _ in range(2)]
    o_rsc = abf.take(1024)
    o_qkv1st = abf.take(3 * 512)
    af.push()
    o_xsb = af.take(2048)
    o_stat = af.take(8)

    dma("sync", bfv(o_identbf, 128), identbfi, s_const)
    dma("sync", bfv(o_utri, 128), utrii, s_const)
    dma("sync", bfv(o_onesbf, 128), onesbfi, s_const)
    dma("sync", fv(o_identf, 128), identf32i, s_const)
    dma("sync", fv(o_onesf, 128), onesf32i, s_const)
    dma("sync", fv(o_wsel, E), wseli, s_const)
    dma("sync", fv(o_hbase, 1), hbasei, s_const)
    dma("sync", ivv(o_ownpm, 8), ownpmi, s_const)
    dma("sync", ivv(o_aoix, 16), aoixi, s_const)

    dma("gpsimd", bfv(o_wqkvT, 16 * 768),
        wqkvTi.rearrange("(a p) f -> p a f", p=128), s_w0)
    dma("gpsimd", bfv(o_wg, 16 * E),
        wgi.rearrange("(a p) e -> p a e", p=128), s_w0)
    dma("gpsimd", abf.ap[0:1, o_bg:o_bg + E], bgi, s_w0)
    vop("iota", ivv(o_pi, 64), pattern=[[128, 64]], base=0,
        channel_multiplier=1, eng="gpsimd", s=geio)

    wait("vector", geio, 1)
    vop("memset", fv(o_eps, 1), 1e-05)
    vop("tensor_copy", fv(o_iota, 64), ivv(o_pi, 64), s=ve)

    for k in range(8):
        if k >= 1:
            wait("sync", ve, ln_last_ve)
        dma("sync", fv(o_xsb, 2048), x_sl[k * 128:(k + 1) * 128, :], s_lnx)
        wait("vector", s_lnx, s_lnx.v)
        vop("reduce_sum", fv(o_stat, 1), fv(o_xsb, 2048), axis=AX.X, s=ve)
        wait("vector", ve, ve.v)
        vop("tensor_scalar", fv(o_stat + 1, 1), fv(o_stat, 1), 1.0 / D, None,
            op0=ALU.mult, s=ve)
        wait("vector", ve, ve.v)
        vop("tensor_scalar", fv(o_xsb, 2048), fv(o_xsb, 2048),
            fv(o_stat + 1, 1), None, op0=ALU.subtract, s=ve)
        wait("scalar", ve, ve.v)
        if k >= 1:
            wait("scalar", s_xni, ln_last_gq)
        act(bfv(o_xnstage, 2048), fv(o_xsb, 2048), AF.Square,
            accum=fv(o_stat + 2, 1), s=se)
        wait("scalar", se, se.v)
        act(fv(o_stat + 3, 1), fv(o_stat + 2, 1), AF.Ln, bias=fv(o_eps, 1),
            scale=1.0 / D, s=se)
        wait("scalar", se, se.v)
        act(fv(o_stat + 4, 1), fv(o_stat + 3, 1), AF.Exp, scale=-0.5, s=se)
        wait("vector", se, se.v)
        vop("tensor_scalar", bfv(o_xnstage, 2048), fv(o_xsb, 2048),
            fv(o_stat + 4, 1), None, op0=ALU.mult, s=ve)
        ln_last_ve = ve.v
        wait("gpsimd", ve, ve.v)
        dma("gpsimd", cc_xn_in.ap()[k * 128:(k + 1) * 128, :],
            bfv(o_xnstage, 2048), s_xni)
        ln_last_gq = s_xni.v
    wait("gpsimd", s_xni, s_xni.v)
    coll("AllGather", ALU.bypass, [cc_xn_in.ap().opt()],
         [cc_xn_out.ap().opt()], cxn)

    xnt_gate = [Gate(), Gate()]
    rot_gate = [Gate(), Gate()]
    st_gate = [Gate(), Gate(), Gate()]
    xnt_ld = {}
    rot_ld = {}

    def emit_xnt_load(j):
        buf = j % 2
        gwait("sync", xnt_gate[buf])
        for d16 in range(16):
            src = cc_xn_out.ap()[j * 512:(j + 1) * 512,
                                 d16 * 128:(d16 + 1) * 128]
            dma("sync", bfv(o_xnt[buf] + d16 * 512, 512), src, s_xnt[buf],
                transpose=True)
        xnt_ld[j] = s_xnt[buf].v

    def emit_rot_load(j):
        buf = j % 2
        gwait("sync", rot_gate[buf])
        dma("sync", abf.ap[0:64, o_rot[buf]:o_rot[buf] + 512],
            rotAi[:, j * 512:(j + 1) * 512], s_rot[buf])
        dma("sync", abf.ap[0:64, o_rot[buf] + 512:o_rot[buf] + 1024],
            rotBi[:, j * 512:(j + 1) * 512], s_rot[buf])
        rot_ld[j] = s_rot[buf].v

    wait("sync", cxn, 1)
    emit_xnt_load(0)
    emit_rot_load(0)
    emit_xnt_load(1)
    emit_rot_load(1)

    wait("tensor", s_w0, s_w0.v)
    wait("tensor", s_const, s_const.v)
    for j in range(NBLK):
        buf = j % 2
        wait("tensor", s_xnt[buf], xnt_ld[j])
        for F in range(4):
            gwait("tensor", pbg[F])
            for d16 in range(16):
                mm(pb[F][:, 0:512], bfv(o_wqkvT + d16 * 768 + F * 128, 128),
                   bfv(o_xnt[buf] + d16 * 512, 512),
                   d16 == 0, d16 == 15, s=te if d16 == 15 else None)
        gwait("tensor", pbmg)
        vpair_te = {}
        for pair in range(2):
            gwait("tensor", pbg[4])
            for tt in (2 * pair, 2 * pair + 1):
                colo = (tt % 2) * 256
                for d16 in range(16):
                    lhs = bfv(o_xnt[buf] + d16 * 512 + tt * 128, 128)
                    mm(pb[4][:, colo:colo + 256], lhs,
                       bfv(o_wqkvT + d16 * 768 + 512, 256),
                       d16 == 0, d16 == 15, s=te if d16 == 15 else None)
                    mm(pbm[:, tt * 4:tt * 4 + 4], lhs, bfv(o_wg + d16 * 4, 4),
                       d16 == 0, False)
                mm(pbm[:, tt * 4:tt * 4 + 4],
                   abf.ap[0:1, o_onesbf:o_onesbf + 128],
                   abf.ap[0:1, o_bg:o_bg + E], False, True,
                   s=te if tt == 3 else None)
            vpair_te[pair] = te.v
            wait("vector", te, te.v)
            for tt in (2 * pair, 2 * pair + 1):
                colo = (tt % 2) * 256
                vop("tensor_copy", bfv(o_vt + (j * 4 + tt) * 128, 128),
                    pb[4][:, colo:colo + 128])
                gwait("vector", st_gate[2])
                vop("tensor_copy", bfv(o_qkv1st + 1024 + tt * 128, 128),
                    pb[4][:, colo + 128:colo + 256], s=ve)
            pbg[4].set(ve, ve.v)
        te_blk = te.v
        xnt_gate[buf].set(te, te_blk)
        if j + 2 < NBLK:
            emit_xnt_load(j + 2)

        wait("vector", te, te_blk)
        wait("vector", s_rot[buf], rot_ld[j])
        for F in range(4):
            h = F % 2
            if h == 0:
                dsto = (o_qt if F == 0 else o_kt) + j * 512
            else:
                dsto = o_qkv1st + (F // 2) * 512
                gwait("vector", st_gate[F // 2])
            vop("tensor_copy", bfv(dsto, 512), pb[F][:, 0:512],
                s=ve if F == 3 else None)
        for F in range(4):
            pbg[F].set(ve, ve.v)
        for F in range(4):
            h = F % 2
            dsto = ((o_qt if F == 0 else o_kt) + j * 512) if h == 0 \
                else (o_qkv1st + (F // 2) * 512)
            in0 = abf.ap[0:32, dsto:dsto + 512]
            in32 = abf.ap[32:64, dsto:dsto + 512]
            cA = abf.ap[0:32, o_rot[buf]:o_rot[buf] + 512]
            sA = abf.ap[32:64, o_rot[buf]:o_rot[buf] + 512]
            sB = abf.ap[0:32, o_rot[buf] + 512:o_rot[buf] + 1024]
            cB = abf.ap[32:64, o_rot[buf] + 512:o_rot[buf] + 1024]
            t1 = abf.ap[0:32, o_rsc:o_rsc + 512]
            t2 = abf.ap[0:32, o_rsc + 512:o_rsc + 1024]
            t3 = abf.ap[32:64, o_rsc:o_rsc + 512]
            t4 = abf.ap[32:64, o_rsc + 512:o_rsc + 1024]
            vop("tensor_mul", t1, in0, cA)
            vop("tensor_mul", t2, in32, sA)
            vop("tensor_mul", t3, in32, cB)
            vop("tensor_mul", t4, in0, sB, s=ve)
            wait("vector", ve, ve.v)
            vop("tensor_add", in0, t1, t2)
            vop("tensor_add", in32, t3, t4, s=ve)
        rot_gate[buf].set(ve, ve.v)
        if j + 2 < NBLK:
            emit_rot_load(j + 2)
        vop("tensor_copy", fv(o_logits + j * 16, 16), pbm[:, 0:16], s=ve)
        pbmg.set(ve, ve.v)
        wait("sync", ve, ve.v)
        dma("sync", qkv1_buf.ap()[:, j * 512:(j + 1) * 512],
            bfv(o_qkv1st, 512), s_st[0])
        st_gate[0].set(s_st[0], s_st[0].v)
        dma("sync", qkv1_buf.ap()[:, N + j * 512:N + (j + 1) * 512],
            bfv(o_qkv1st + 512, 512), s_st[1])
        st_gate[1].set(s_st[1], s_st[1].v)
        dma("sync", qkv1_buf.ap()[:, 2 * N + j * 512:2 * N + (j + 1) * 512],
            bfv(o_qkv1st + 1024, 512), s_st[2])
        st_gate[2].set(s_st[2], s_st[2].v)
    P1_VE = ve.v
    P1_TE = te.v
    abf.pop()
    af.pop()

    EDGE_ALL[0] = True

    def eview(base, e):
        return af.ap[:, base:base + 256].rearrange("p (i e) -> p e i",
                                                   e=E)[:, e, :]

    vop("tensor_tensor", fv(o_t1, 64), eview(o_logits, 0),
        eview(o_logits, 1), op=ALU.max)
    vop("tensor_tensor", fv(o_r1, 64), eview(o_logits, 2),
        eview(o_logits, 3), op=ALU.max)
    vop("tensor_tensor", fv(o_t1, 64), fv(o_t1, 64), fv(o_r1, 64), op=ALU.max)
    for e in range(E):
        ce = eview(o_cnt, e)
        vop("memset", ce, 0.0)
        for ep in range(E):
            if ep == e:
                continue
            opk = ALU.is_gt if ep > e else ALU.is_ge
            vop("tensor_tensor", fv(o_r1, 64), eview(o_logits, ep),
                eview(o_logits, e), op=opk)
            vop("tensor_add", ce, ce, fv(o_r1, 64))
    for e in range(E):
        vop("tensor_tensor", eview(o_r1, e), eview(o_logits, e),
            fv(o_t1, 64), op=ALU.subtract, s=ve if e == E - 1 else None)
    wait("scalar", ve, ve.v)
    act(fv(o_p, 256), fv(o_r1, 256), AF.Exp, s=se)
    wait("vector", se, se.v)
    vop("tensor_scalar", fv(o_r2, 256), fv(o_cnt, 256), 1.5, None,
        op0=ALU.is_le)
    vop("tensor_mul", fv(o_p, 256), fv(o_p, 256), fv(o_r2, 256))
    vop("tensor_tensor", fv(o_t1, 64), eview(o_p, 0), eview(o_p, 1),
        op=ALU.add)
    vop("tensor_tensor", fv(o_r1, 64), eview(o_p, 2), eview(o_p, 3),
        op=ALU.add)
    vop("tensor_add", fv(o_t1, 64), fv(o_t1, 64), fv(o_r1, 64))
    vop("reciprocal", fv(o_r1, 64), fv(o_t1, 64), s=ve)
    wait("vector", ve, ve.v)
    for e in range(E):
        vop("tensor_tensor", eview(o_p, e), eview(o_p, e), fv(o_r1, 64),
            op=ALU.mult)
    vop("tensor_scalar", fv(o_score, 256), fv(o_r2, 256), 1.0, None,
        op0=ALU.subtract)
    vop("tensor_mul", fv(o_r2, 256), fv(o_p, 256), fv(o_r2, 256))
    vop("tensor_add", fv(o_score, 256), fv(o_score, 256), fv(o_r2, 256))
    vop("memset", fv(o_lo, 4), -2.0)
    vop("memset", fv(o_hi, 4), 2.0)
    vop("memset", fv(o_capc, 1), float(CAP) - 0.5, s=ve)

    def count4(cmp_op, scal_off, iota_mode=False):
        for e in range(E):
            tgt = eview(o_r1, e)
            if iota_mode:
                vop("tensor_scalar", tgt, fv(o_iota, 64),
                    fv(scal_off + e, 1), None, op0=cmp_op)
                vop("tensor_mul", tgt, tgt, eview(o_eqm, e))
            else:
                vop("tensor_scalar", tgt, eview(o_score, e),
                    fv(scal_off + e, 1), None, op0=cmp_op)
            vop("reduce_sum", fv(o_cnt4 + e, 1), tgt, axis=AX.X, s=ve)
            wait("vector", ve, ve.v)
        wait("tensor", ve, ve.v)
        gwait("tensor", pbmg)
        mm(pbm[:, 0:4], fv(o_onesf, 128), fv(o_cnt4, 4), True, True, s=te)
        wait("vector", te, te.v)
        vop("tensor_copy", fv(o_bc, 4), pbm[:, 0:4], s=ve)
        pbmg.set(ve, ve.v)

    for it in range(34):
        vop("tensor_tensor", fv(o_mid, 4), fv(o_lo, 4), fv(o_hi, 4),
            op=ALU.add)
        vop("tensor_scalar", fv(o_mid, 4), fv(o_mid, 4), 0.5, None,
            op0=ALU.mult, s=ve)
        count4(ALU.is_gt, o_mid)
        vop("tensor_scalar", fv(o_ge4, 4), fv(o_bc, 4), fv(o_capc, 1),
            None, op0=ALU.is_ge)
        vop("tensor_copy", ivv(o_im, 4), fv(o_ge4, 4))
        vop("select", fv(o_ss4, 4), ivv(o_im, 4), fv(o_mid, 4), fv(o_lo, 4))
        vop("tensor_copy", fv(o_lo, 4), fv(o_ss4, 4))
        vop("select", fv(o_ss4, 4), ivv(o_im, 4), fv(o_hi, 4), fv(o_mid, 4))
        vop("tensor_copy", fv(o_hi, 4), fv(o_ss4, 4), s=ve)
    vop("tensor_copy", fv(o_th, 4), fv(o_hi, 4), s=ve)
    count4(ALU.is_gt, o_th)
    vop("tensor_scalar", fv(o_rr, 4), fv(o_bc, 4), -1.0, float(CAP),
        op0=ALU.mult, op1=ALU.add, s=ve)
    for e in range(E):
        vop("tensor_scalar", eview(o_eqm, e), eview(o_score, e),
            fv(o_th + e, 1), None, op0=ALU.is_equal)
    vop("memset", fv(o_lo, 4), -1.0)
    vop("memset", fv(o_hi, 4), float(N), s=ve)
    for it in range(16):
        vop("tensor_tensor", fv(o_mid, 4), fv(o_lo, 4), fv(o_hi, 4),
            op=ALU.add)
        vop("tensor_scalar", fv(o_mid, 4), fv(o_mid, 4), 0.5, None,
            op0=ALU.mult, s=ve)
        count4(ALU.is_le, o_mid, iota_mode=True)
        vop("tensor_tensor", fv(o_ge4, 4), fv(o_bc, 4), fv(o_rr, 4),
            op=ALU.is_ge)
        vop("tensor_copy", ivv(o_im, 4), fv(o_ge4, 4))
        vop("select", fv(o_ss4, 4), ivv(o_im, 4), fv(o_mid, 4), fv(o_hi, 4))
        vop("tensor_copy", fv(o_hi, 4), fv(o_ss4, 4))
        vop("select", fv(o_ss4, 4), ivv(o_im, 4), fv(o_lo, 4), fv(o_mid, 4))
        vop("tensor_copy", fv(o_lo, 4), fv(o_ss4, 4), s=ve)
    for e in range(E):
        vop("tensor_scalar", eview(o_r1, e), eview(o_score, e),
            fv(o_th + e, 1), None, op0=ALU.is_gt)
        vop("tensor_scalar", eview(o_r2, e), fv(o_iota, 64),
            fv(o_hi + e, 1), None, op0=ALU.is_le)
        vop("tensor_mul", eview(o_r2, e), eview(o_r2, e), eview(o_eqm, e))
        vop("tensor_tensor", eview(o_sel, e), eview(o_r1, e), eview(o_r2, e),
            op=ALU.add)
    vop("tensor_mul", fv(o_p, 256), fv(o_p, 256), fv(o_sel, 256), s=ve)

    for e in range(E):
        vop("tensor_copy", bfv(o_selbf, 64), eview(o_sel, e), s=ve)
        wait("tensor", ve, ve.v)
        gwait("tensor", pbmg)
        mm(pbm[:, 64:128], bfv(o_utri, 128), bfv(o_selbf, 64), True, True)
        mm(pbm[0:1, 192:256], abf.ap[:, o_onesbf:o_onesbf + 1],
           bfv(o_selbf, 64), True, True, s=te)
        wait("vector", te, te.v)
        vop("tensor_copy", af.ap[0:1, o_colsum:o_colsum + 64],
            pbm[0:1, 192:256])
        vop("tensor_copy", af.ap[0:1, o_colp:o_colp + 64], pbm[0:1, 192:256])
        cur, oth = o_colp, o_cps
        sh = 1
        while sh < 64:
            vop("tensor_copy", af.ap[0:1, oth:oth + sh],
                af.ap[0:1, cur:cur + sh])
            vop("tensor_tensor", af.ap[0:1, oth + sh:oth + 64],
                af.ap[0:1, cur + sh:cur + 64],
                af.ap[0:1, cur:cur + 64 - sh], op=ALU.add)
            cur, oth = oth, cur
            sh *= 2
        vop("tensor_tensor", af.ap[0:1, o_colp:o_colp + 64],
            af.ap[0:1, cur:cur + 64],
            af.ap[0:1, o_colsum:o_colsum + 64], op=ALU.subtract, s=ve)
        wait("tensor", ve, ve.v)
        mm(pbm[:, 128:192], af.ap[0:1, o_onesf:o_onesf + 128],
           af.ap[0:1, o_colp:o_colp + 64], True, True, s=te)
        wait("vector", te, te.v)
        vop("tensor_copy", fv(o_r1, 64), pbm[:, 128:192])
        vop("tensor_tensor", eview(o_rank, e), pbm[:, 64:128],
            eview(o_sel, e), op=ALU.subtract)
        vop("tensor_tensor", eview(o_rank, e), eview(o_rank, e),
            fv(o_r1, 64), op=ALU.add, s=ve)
        pbmg.set(ve, ve.v)

    for src_b, dst_o in ((o_sel, o_mysel), (o_rank, o_myrank), (o_p, o_myp)):
        vop("memset", fv(dst_o, 64), 0.0)
        for e in range(E):
            vop("tensor_scalar", fv(o_r1, 64), eview(src_b, e),
                fv(o_wsel + e, 1), None, op0=ALU.mult)
            vop("tensor_add", fv(dst_o, 64), fv(dst_o, 64), fv(o_r1, 64))
    vop("tensor_scalar", fv(o_r1, 64), fv(o_myrank, 64), fv(o_hbase, 1),
        None, op0=ALU.subtract)
    vop("tensor_scalar", fv(o_r2, 64), fv(o_r1, 64), -0.5, None, op0=ALU.is_ge)
    vop("tensor_mul", fv(o_r2, 64), fv(o_r2, 64), fv(o_mysel, 64))
    vop("tensor_scalar", fv(o_mysel, 64), fv(o_r1, 64), float(HALF) - 0.5,
        None, op0=ALU.is_le)
    vop("tensor_mul", fv(o_r2, 64), fv(o_r2, 64), fv(o_mysel, 64))
    vop("tensor_copy", ivv(o_iadj, 64), fv(o_r1, 64))
    vop("memset", ivv(o_pi, 64), 127)
    vop("tensor_tensor", ivv(o_pi, 64), ivv(o_iadj, 64), ivv(o_pi, 64),
        op=ALU.bitwise_and)
    vop("memset", ivv(o_ish, 64), 3)
    vop("tensor_tensor", ivv(o_pi, 64), ivv(o_pi, 64), ivv(o_ish, 64),
        op=ALU.logical_shift_left)
    vop("memset", ivv(o_ish, 64), 7)
    vop("tensor_tensor", ivv(o_ish, 64), ivv(o_iadj, 64), ivv(o_ish, 64),
        op=ALU.logical_shift_right)
    vop("tensor_tensor", ivv(o_pi, 64), ivv(o_pi, 64), ivv(o_ish, 64),
        op=ALU.add)
    vop("tensor_copy", fv(o_pif, 64), ivv(o_pi, 64))
    vop("tensor_mul", fv(o_pif, 64), fv(o_pif, 64), fv(o_r2, 64))
    vop("tensor_scalar", fv(o_r1, 64), fv(o_r2, 64), -BIG, BIG, op0=ALU.mult,
        op1=ALU.add)
    vop("tensor_add", fv(o_pif, 64), fv(o_pif, 64), fv(o_r1, 64))
    vop("tensor_copy", ivv(o_pi, 64), fv(o_pif, 64))
    rec = af.ap[:, o_myrec:o_myrec + 128].rearrange("p (i z) -> p z i", z=2)
    vop("tensor_copy", rec[:, 0, :], fv(o_iota, 64))
    vop("tensor_copy", rec[:, 1, :], fv(o_myp, 64), s=ve)
    vop("memset", fv(o_rowa, 64), BIG)
    vop("memset", fv(o_rowb, 64), BIG)
    for e in range(E - 1, -1, -1):
        vop("tensor_scalar", fv(o_r1, 64), eview(o_rank, e), 1.0,
            float(CAP * e), op0=ALU.mult, op1=ALU.add)
        vop("tensor_copy", ivv(o_im, 64), eview(o_sel, e))
        vop("select", fv(o_ss64, 64), ivv(o_im, 64), fv(o_r1, 64),
            fv(o_rowa, 64))
        vop("tensor_copy", fv(o_rowa, 64), fv(o_ss64, 64))
    for e in range(E):
        vop("tensor_scalar", fv(o_r1, 64), eview(o_rank, e), 1.0,
            float(CAP * e), op0=ALU.mult, op1=ALU.add)
        vop("tensor_copy", ivv(o_im, 64), eview(o_sel, e))
        vop("select", fv(o_ss64, 64), ivv(o_im, 64), fv(o_r1, 64),
            fv(o_rowb, 64))
        vop("tensor_copy", fv(o_rowb, 64), fv(o_ss64, 64))
    vop("tensor_tensor", fv(o_r1, 64), eview(o_sel, 0), eview(o_sel, 1),
        op=ALU.add)
    vop("tensor_tensor", fv(o_r2, 64), eview(o_sel, 2), eview(o_sel, 3),
        op=ALU.add)
    vop("tensor_add", fv(o_r1, 64), fv(o_r1, 64), fv(o_r2, 64))
    vop("tensor_scalar", fv(o_r2, 64), fv(o_r1, 64), 1.5, None, op0=ALU.is_ge)
    vop("tensor_copy", ivv(o_im, 64), fv(o_r2, 64))
    vop("memset", fv(o_r1, 64), BIG)
    vop("select", fv(o_ss64, 64), ivv(o_im, 64), fv(o_rowb, 64), fv(o_r1, 64))
    vop("tensor_copy", fv(o_rowb, 64), fv(o_ss64, 64))
    rowi = af.ap[:, o_rowi:o_rowi + 128].rearrange("p (i z) -> p z i", z=2)
    vop("tensor_copy", rowi[:, 0, :], fv(o_rowa, 64))
    vop("tensor_copy", rowi[:, 1, :], fv(o_rowb, 64), s=ve)
    EDGE_ALL[0] = False
    VE_ROUTE = ve.v

    wait("gpsimd", ve, VE_ROUTE)
    dma("gpsimd", rowab_buf.ap().rearrange("(p i) z -> p (i z)", p=128),
        fv(o_rowi, 128), s_rab)

    def scat(e):
        with e.register("b1023") as r:
            e.reg_mov(r, HALF - 1)
            for i in range(64):
                e.indirect_dma_start(
                    out=idxp_buf.ap(),
                    out_offset=IndirectOffsetOnAxis(ivv(o_pi + i, 1), axis=0),
                    in_=af.ap[:, o_myrec + i * 2:o_myrec + i * 2 + 2],
                    in_offset=None, bounds_check=r,
                    oob_is_err=False).then_inc(s_scat.h, 16)
    op("gpsimd", scat)
    s_scat.inc(16 * 64)
    wait("gpsimd", s_scat, s_scat.v)
    dma("gpsimd", fv(o_rab, 16),
        idxp_buf.ap().rearrange("(p k) z -> p (k z)", p=128), s_idx)

    def rowabg(e):
        with e.register("brb") as r:
            e.reg_mov(r, N - 1)
            for k in range(8):
                e.indirect_dma_start(
                    out=af.ap[:, o_rab2 + k * 2:o_rab2 + k * 2 + 2],
                    out_offset=None, in_=rowab_buf.ap(),
                    in_offset=IndirectOffsetOnAxis(ivv(o_ownpm + k, 1), axis=0),
                    bounds_check=r, oob_is_err=False).then_inc(s_rabg.h, 16)
    op("gpsimd", rowabg)
    s_rabg.inc(16 * 8)
    wait("gpsimd", s_rab, s_rab.v)
    EDGE_ALL[0] = True
    wait("vector", s_idx, s_idx.v)
    wait("vector", s_rabg, s_rabg.v)
    rab = af.ap[:, o_rab:o_rab + 16].rearrange("p (k z) -> p z k", z=2)
    vop("tensor_copy", ivv(o_tok, 8), rab[:, 0, :])
    vop("tensor_copy", fv(o_ptil, 8), rab[:, 1, :])
    rab2 = af.ap[:, o_rab2:o_rab2 + 16].rearrange("p (k z) -> p z k", z=2)
    vop("tensor_copy", ivv(o_rowA8, 8), rab2[:, 0, :])
    vop("tensor_copy", ivv(o_rowB8, 8), rab2[:, 1, :], s=ve)
    EDGE_ALL[0] = False
    VE_TOK = ve.v

    wait("gpsimd", ve, VE_TOK)
    xg_gate = [Gate(), Gate()]
    xgs_gate = [Gate(), Gate()]
    gq_xg = {}
    o_xgf = af.take(256)
    for k in range(8):
        gwait("gpsimd", xg_gate[k % 2])

        def gat(e, _k=k):
            with e.register(f"bx{_k}") as r:
                e.reg_mov(r, N - 1)
                e.indirect_dma_start(
                    out=bfv(o_xg + (_k % 2) * 2048, 2048), out_offset=None,
                    in_=cc_xn_out.ap(),
                    in_offset=IndirectOffsetOnAxis(ivv(o_tok + _k, 1), axis=0),
                    bounds_check=r, oob_is_err=False).then_inc(
                        s_xg[_k % 2].h, 16)
        op("gpsimd", gat)
        gq_xg[k] = s_xg[k % 2].inc(16)
        wait("vector", s_xg[k % 2], gq_xg[k])
        for d16 in range(16):
            sl = d16 % 2
            gwait("vector", xgs_gate[sl])
            vop("tensor_copy", fv(o_xgf + sl * 128, 128),
                bfv(o_xg + (k % 2) * 2048 + d16 * 128, 128), s=ve)
            if d16 == 15:
                xg_gate[k % 2].set(ve, ve.v)
            wait("tensor", ve, ve.v)
            gwait("tensor", pbg[sl])
            petr(pb[sl][:, 0:128], fv(o_xgf + sl * 128, 128),
                 fv(o_identf, 128), s=te)
            xgs_gate[sl].set(te, te.v)
            wait("vector", te, te.v)
            vop("tensor_copy", bfv(o_xgt + d16 * 1024 + k * 128, 128),
                pb[sl][:, 0:128], s=ve)
            pbg[sl].set(ve, ve.v)
    VE_XGT = ve.v

    abf.push()
    o_pt = [abf.take(512) for _ in range(4)]
    o_pts = [abf.take(512) for _ in range(2)]
    o_drow = abf.take(N)
    o_dmul = abf.take(1024)
    o_negbf = abf.take(128)
    o_aoutT = abf.take(HPC * N)
    o_trimt4 = abf.take(2048)
    o_bias2 = af.take(512)
    s_tm4 = sem("s_tm4")
    dma("sync", bfv(o_trimt4, 2048), trimT4i, s_tm4)
    wait("vector", s_tm4, s_tm4.v)
    pt_gate = [Gate() for _ in range(4)]
    av_gate = [Gate(), Gate()]
    av_gate[0].set(pbg[4].sem, pbg[4].val)
    av_gate[1].set(pbg[5].sem, pbg[5].val)
    den_gate = Gate()
    pts_gate = [Gate(), Gate()]
    dmg = [Gate(), Gate()]
    pending_norm = [None]

    def gwait2(eng, ga, gb):
        if ga.sem is not None and gb.sem is not None and ga.sem is gb.sem:
            wait(eng, ga.sem, max(ga.val, gb.val))
        else:
            gwait(eng, ga)
            gwait(eng, gb)

    vop("memset", abf.ap[0:1, o_negbf:o_negbf + 128], -1.0, s=ve)
    wait("tensor", ve, ve.v)

    for h in range(HPC):
        if h == 1:
            wait("sync", te, te.v)
            for z in range(3):
                wait("sync", s_st[z], s_st[z].v)
            for z, dst in enumerate((o_qt, o_kt, o_vt)):
                dma("sync", bfv(dst, N), qkv1_buf.ap()[:, z * N:(z + 1) * N],
                    s_rel)
            wait("tensor", s_rel, s_rel.v)
            wait("vector", s_rel, s_rel.v)
        for GG in range(16):
            gwait("vector", dmg[GG % 2])
            vop("tensor_tensor", bfv(o_dmul + (GG % 2) * 512, 512),
                bfv(o_qt + GG * 512, 512), bfv(o_kt + GG * 512, 512),
                op=ALU.mult, s=ve)
            wait("tensor", ve, ve.v)
            gwait("tensor", pbmg)
            mm(pbm[0:1, 0:512], abf.ap[:, o_onesbf:o_onesbf + 1],
               bfv(o_dmul + (GG % 2) * 512, 512), True, True, s=te)
            dmg[GG % 2].set(te, te.v)
            wait("vector", te, te.v)
            vop("tensor_copy", abf.ap[0:1, o_drow + GG * 512:
                                      o_drow + (GG + 1) * 512],
                pbm[0:1, 0:512], s=ve)
            pbmg.set(ve, ve.v)

        for G in range(16):
            nkj = 4 * G + 4
            P = nkj // 2
            avbank = pb[4 + (G % 2)]
            dend = pb[6]
            gwait("tensor", pbmg)
            wait("tensor", ve, ve.v)
            mm(pbm[:, 0:512], abf.ap[0:1, o_negbf:o_negbf + 128],
               abf.ap[0:1, o_drow + G * 512:o_drow + (G + 1) * 512],
               True, True, s=te)
            wait("vector", te, te.v)
            vop("tensor_copy", fv(o_bias2, 512), pbm[:, 0:512], s=ve)
            pbmg.set(ve, ve.v)
            gwait("tensor", den_gate)
            gwait("tensor", av_gate[G % 2])
            smm_ts = {}
            exp_ts = {}
            pts_ts = {}
            for pj in range(P + 2):
                if pj < P:
                    c0, c1 = 2 * pj, 2 * pj + 1
                    gwait2("tensor", pbg[c0 % 4], pbg[c1 % 4])
                    for c in (c0, c1):
                        smm_ts[c] = mm(pb[c % 4][:, 0:512],
                                       bfv(o_kt + c * 128, 128),
                                       bfv(o_qt + G * 512, 512), True, True,
                                       s=te)
                if pj == 1 and pending_norm[0] is not None:
                    pending_norm[0]()
                    pending_norm[0] = None
                if 1 <= pj <= P:
                    p0, p1 = 2 * pj - 2, 2 * pj - 1
                    wait("tensor", se, exp_ts[p1])
                    mm(avbank[:, 0:512], bfv(o_vt + p0 * 128, 128),
                       bfv(o_pt[p0 % 4], 512), p0 == 0, False)
                    mm(avbank[:, 0:512], bfv(o_vt + p1 * 128, 128),
                       bfv(o_pt[p1 % 4], 512), False, p1 == nkj - 1, s=te)
                    pt_gate[p0 % 4].set(te, te.v)
                    pt_gate[p1 % 4].set(te, te.v)
                if pj >= 2:
                    q0 = 2 * pj - 4
                    wait("tensor", ve, pts_ts[pj - 2])
                    mm(dend[0:1, 0:512], abf.ap[:, o_onesbf:o_onesbf + 1],
                       bfv(o_pts[(pj - 2) % 2], 512),
                       q0 == 0, q0 + 1 == nkj - 1, s=te)
                    pts_gate[(pj - 2) % 2].set(te, te.v)
                if pj < P:
                    wait("vector", te, smm_ts[c1])
                    for c in (c0, c1):
                        sl = ve if c == c1 and c < 4 * G else None
                        vop("tensor_tensor", pb[c % 4][:, 0:512],
                            pb[c % 4][:, 0:512], fv(o_bias2, 512), op=ALU.add,
                            s=sl)
                        if c >= 4 * G:
                            vop("tensor_tensor", pb[c % 4][:, 0:512],
                                pb[c % 4][:, 0:512],
                                bfv(o_trimt4 + (c - 4 * G) * 512, 512),
                                op=ALU.add, s=ve if c == c1 else None)
                    add_inc = ve.v
                if 1 <= pj <= P:
                    wait("vector", se, exp_ts[2 * pj - 1])
                    gwait("vector", pts_gate[(pj - 1) % 2])
                    vop("tensor_tensor", bfv(o_pts[(pj - 1) % 2], 512),
                        bfv(o_pt[(2 * pj - 2) % 4], 512),
                        bfv(o_pt[(2 * pj - 1) % 4], 512), op=ALU.add, s=ve)
                    pts_ts[pj - 1] = ve.v
                if pj < P:
                    wait("scalar", ve, add_inc)
                    gwait2("scalar", pt_gate[c0 % 4], pt_gate[c1 % 4])
                    for c in (c0, c1):
                        act(bfv(o_pt[c % 4], 512), pb[c % 4][:, 0:512],
                            AF.Exp, scale=SCALE, s=se)
                        exp_ts[c] = se.v
                        pbg[c % 4].set(se, se.v)
            av_done = te.v

            def mknorm(_h=h, _G=G, _av=avbank, _done=av_done):
                def emit():
                    wait("vector", te, _done)
                    vop("reciprocal", af.ap[0:1, o_ir1:o_ir1 + 512],
                        pb[6][0:1, 0:512], s=ve)
                    den_gate.set(ve, ve.v)
                    wait("tensor", ve, ve.v)
                    gwait("tensor", pbmg)
                    mm(pbm[:, 0:512], af.ap[0:1, o_onesf:o_onesf + 128],
                       af.ap[0:1, o_ir1:o_ir1 + 512], True, True, s=te)
                    wait("vector", te, te.v)
                    vop("tensor_copy", fv(o_invr, 512), pbm[:, 0:512], s=ve)
                    pbmg.set(ve, ve.v)
                    vop("scalar_tensor_tensor",
                        bfv(o_aoutT + _h * N + _G * 512, 512),
                        _av[:, 0:512], 1.0, fv(o_invr, 512),
                        op0=ALU.mult, op1=ALU.mult, s=ve)
                    av_gate[_G % 2].set(ve, ve.v)
                return emit
            pending_norm[0] = mknorm()
        pending_norm[0]()
        pending_norm[0] = None
        wait("sync", ve, ve.v)
        dma("sync", cc_ao_in[h].ap(), bfv(o_aoutT + h * N, N), s_ao)
        wait("gpsimd", s_ao, s_ao.v)
        coll("AllGather", ALU.bypass, [cc_ao_in[h].ap().opt()],
             [cc_ao_out[h].ap().rearrange("(f g) x -> f (g x)", g=8).opt()],
             cao)
    P2_VE = ve.v
    P2_SE = se.v
    for t in range(4):
        pbg[t].set(se, P2_SE)
    pbg[4].set(ve, P2_VE)
    pbg[5].set(ve, P2_VE)
    pbg[6].set(ve, P2_VE)

    abf.pop()
    abf.pop()

    af.push()
    o_sst = [af.take(512) for _ in range(4)]
    sst_gate = [Gate() for _ in range(4)]
    abf.push()
    o_alh = abf.take(16 * 1024)
    o_wo = [abf.take(16 * 512) for _ in range(2)]
    wo_gate = [Gate(), Gate()]
    wo_ld = {}

    def emit_wo_load(cb):
        b = cb % 2
        gwait("gpsimd", wo_gate[b])
        src = woutFi[:, cb * 512:(cb + 1) * 512] \
            .rearrange("(a p) c -> p a c", p=128)
        dma("gpsimd", bfv(o_wo[b], 16 * 512), src, s_wo[b])
        wo_ld[cb] = s_wo[b].v

    wait("gpsimd", cao, 2)
    for a in range(16):
        def g(e, _a=a):
            with e.register(f"ba{_a}") as r:
                e.reg_mov(r, NCORES * 128 * 8 - 1)
                e.indirect_dma_start(
                    out=bfv(o_alh + _a * 1024, 1024), out_offset=None,
                    in_=cc_ao_out[_a % 2].ap(),
                    in_offset=IndirectOffsetOnAxis(ivv(o_aoix + _a, 1),
                                                   axis=0),
                    bounds_check=r, oob_is_err=False).then_inc(s_alh.h, 16)
        op("gpsimd", g)
        s_alh.inc(16)
    emit_wo_load(0)
    emit_wo_load(1)

    wait("tensor", s_alh, s_alh.v)
    wait("tensor", ve, P2_VE)
    for cb in range(4):
        b = cb % 2
        wait("tensor", s_wo[b], wo_ld[cb])
        for k in range(8):
            bank = (cb * 8 + k) % 4
            gwait("tensor", pbg[bank])
            for a in range(16):
                mm(pb[bank][:, 0:512], bfv(o_alh + a * 1024 + k * 128, 128),
                   bfv(o_wo[b] + a * 512, 512), a == 0, a == 15,
                   s=te if a == 15 else None)
            if k == 7:
                wo_gate[b].set(te, te.v)
            wait("vector", te, te.v)
            gwait("vector", sst_gate[bank])
            vop("tensor_copy", fv(o_sst[bank], 512), pb[bank][:, 0:512], s=ve)
            pbg[bank].set(ve, ve.v)
            wait("sync", ve, ve.v)
            dma("sync", cc_sown.ap()[k * 128:(k + 1) * 128,
                                     cb * 512:(cb + 1) * 512],
                fv(o_sst[bank], 512), s_sst)
            sst_gate[bank].set(s_sst, s_sst.v)
        if cb + 2 < 4:
            emit_wo_load(cb + 2)
    P3_TE = te.v
    abf.pop()
    af.pop()

    abf.push()
    o_w1b = [abf.take(2048) for _ in range(2)]
    o_w2sl = [abf.take(16 * 512) for _ in range(2)]
    o_ht = abf.take(16 * 1024)
    o_yacc = abf.take(8 * 2048)
    o_yst = [abf.take(512) for _ in range(2)]
    w1_gate = [Gate(), Gate()]
    w2_gate = [Gate(), Gate()]
    hps_gate = [Gate(), Gate()]
    hps_gate[0].set(ve, P2_VE)
    hps_gate[1].set(pbmg.sem, pbmg.val)
    ht_gate = Gate()
    yst_gate = [Gate(), Gate()]
    w1_ld = {}
    w2_ld = {}

    def emit_w1_load(q, ht):
        hb = ht % 2
        gwait("gpsimd", w1_gate[hb])
        src = w1i[:, (q * 16 + ht) * 128:(q * 16 + ht + 1) * 128] \
            .rearrange("(a p) h -> p a h", p=128)
        dma("gpsimd", bfv(o_w1b[hb], 2048), src, s_w1[hb])
        w1_ld[(q, ht)] = (hb, s_w1[hb].v)

    def emit_w2_load(q, cb):
        wb = cb % 2
        gwait("gpsimd", w2_gate[wb])
        src = w2i[q * 2048:(q + 1) * 2048, cb * 512:(cb + 1) * 512] \
            .rearrange("(a p) c -> p a c", p=128)
        dma("gpsimd", bfv(o_w2sl[wb], 16 * 512), src, s_w2[wb])
        w2_ld[(q, cb)] = (wb, s_w2[wb].v)

    wait("gpsimd", te, P3_TE)
    emit_w1_load(0, 0)
    emit_w1_load(0, 1)
    for q in range(4):
        gwait("scalar", ht_gate)
        for ht in range(16):
            hb = ht % 2
            _hb, _v = w1_ld[(q, ht)]
            wait("tensor", s_w1[_hb], _v)
            for sb in range(2):
                bank = pb[4] if sb == 0 else pbm
                gwait("tensor", hps_gate[sb])
                for d16 in range(16):
                    mm(bank[:, 0:512], bfv(o_w1b[hb] + d16 * 128, 128),
                       bfv(o_xgt + d16 * 1024 + sb * 512, 512),
                       d16 == 0, d16 == 15, s=te if d16 == 15 else None)
                if sb == 1:
                    w1_gate[hb].set(te, te.v)
                wait("scalar", te, te.v)
                act(bfv(o_ht + ht * 1024 + sb * 512, 512), bank[:, 0:512],
                    AF.Gelu, s=se)
                hps_gate[sb].set(se, se.v)
            nq, nht = (q, ht + 2) if ht + 2 < 16 else (q + 1, ht + 2 - 16)
            if nq < 4:
                emit_w1_load(nq, nht)
        se_h = se.v
        if q == 0:
            emit_w2_load(0, 0)
            emit_w2_load(0, 1)
        for cb in range(4):
            wb = cb % 2
            _wb, _v = w2_ld[(q, cb)]
            wait("tensor", s_w2[_wb], _v)
            wait("tensor", se, se_h)
            for stp in range(2):
                for st in range(4):
                    gwait("tensor", pbg[st])
                for ht2 in range(16):
                    for st in range(4):
                        sg = stp * 4 + st
                        mm(pb[st][:, 0:512],
                           bfv(o_ht + ht2 * 1024 + sg * 128, 128),
                           bfv(o_w2sl[wb] + ht2 * 512, 512),
                           ht2 == 0, ht2 == 15,
                           s=te if (ht2 == 15 and st == 3) else None)
                if cb == 3 and stp == 1:
                    ht_gate.set(te, te.v)
                if stp == 1:
                    w2_gate[wb].set(te, te.v)
                wait("vector", te, te.v)
                for st in range(4):
                    sg = stp * 4 + st
                    ya = bfv(o_yacc + sg * 2048 + cb * 512, 512)
                    if q == 0:
                        vop("tensor_copy", ya, pb[st][:, 0:512],
                            s=ve if st == 3 else None)
                    else:
                        vop("tensor_tensor", ya, pb[st][:, 0:512], ya,
                            op=ALU.add, s=ve if st == 3 else None)
                    if st == 3:
                        for st2 in range(4):
                            pbg[st2].set(ve, ve.v)
            nq, ncb = (q, cb + 2) if cb + 2 < 4 else (q + 1, cb + 2 - 4)
            if nq < 4:
                emit_w2_load(nq, ncb)
    for sg in range(8):
        for cb in range(4):
            b = (sg * 4 + cb) % 2
            gwait("vector", yst_gate[b])
            vop("tensor_scalar", bfv(o_yst[b], 512),
                bfv(o_yacc + sg * 2048 + cb * 512, 512),
                fv(o_ptil + sg, 1), None, op0=ALU.mult, s=ve)
            wait("sync", ve, ve.v)
            dma("sync",
                cc_y_in.ap()[sg * 128:(sg + 1) * 128, cb * 512:(cb + 1) * 512],
                bfv(o_yst[b], 512), s_yst[b])
            yst_gate[b].set(s_yst[b], s_yst[b].v)
    wait("gpsimd", s_yst[0], s_yst[0].v)
    wait("gpsimd", s_yst[1], s_yst[1].v)
    coll("AllGather", ALU.bypass, [cc_y_in.ap().opt()],
         [cc_y_out.ap().opt()], cy)
    abf.pop()
    abf.pop()

    abf.push()
    o_ya = abf.take(2048)
    o_yb = abf.take(2048)
    af.push()
    o_g = af.take(2048)
    o_xt = af.take(2048)
    o_st5 = af.take(2048)

    wait("gpsimd", cy, 1)
    wait("sync", s_sst, s_sst.v)
    vop("memset", bfv(o_ya, 2048), 0.0)
    vop("memset", bfv(o_yb, 2048), 0.0, s=ve)
    for k in range(8):
        wait("gpsimd", ve, ve.v)

        def gath(e, _k=k):
            with e.register(f"by{_k}") as r:
                e.reg_mov(r, NCORES * HALF - 1)
                e.indirect_dma_start(
                    out=bfv(o_ya, 2048), out_offset=None, in_=cc_y_out.ap(),
                    in_offset=IndirectOffsetOnAxis(ivv(o_rowA8 + _k, 1),
                                                   axis=0),
                    bounds_check=r, oob_is_err=False).then_inc(s_y5.h, 16)
                e.indirect_dma_start(
                    out=bfv(o_yb, 2048), out_offset=None, in_=cc_y_out.ap(),
                    in_offset=IndirectOffsetOnAxis(ivv(o_rowB8 + _k, 1),
                                                   axis=0),
                    bounds_check=r, oob_is_err=False).then_inc(s_y5.h, 16)
        op("gpsimd", gath)
        s_y5.inc(32)
        if k >= 1:
            wait("sync", s_out, s_out.v)
        dma("sync", fv(o_xt, 2048), x_sl[k * 128:(k + 1) * 128, :], s_p5x)
        if k >= 1:
            wait("sync", ve, ve.v)
        dma("sync", fv(o_st5, 2048),
            cc_sown.ap()[k * 128:(k + 1) * 128, :], s_p5s)
        wait("vector", s_y5, s_y5.v)
        vop("tensor_add", bfv(o_ya, 2048), bfv(o_ya, 2048), bfv(o_yb, 2048),
            s=ve)
        wait("scalar", ve, ve.v)
        act(fv(o_g, 2048), bfv(o_ya, 2048), AF.Silu, s=se)
        wait("vector", se, se.v)
        wait("vector", s_p5x, s_p5x.v)
        wait("vector", s_p5s, s_p5s.v)
        vop("tensor_mul", fv(o_g, 2048), fv(o_g, 2048), fv(o_st5, 2048),
            s=ve)
        wait("vector", ve, ve.v)
        vop("tensor_add", fv(o_xt, 2048), fv(o_xt, 2048), fv(o_g, 2048), s=ve)
        wait("sync", ve, ve.v)
        dma("sync", out_sl[k * 128:(k + 1) * 128, :], fv(o_xt, 2048), s_out)
        if k < 7:
            vop("memset", bfv(o_ya, 2048), 0.0)
            vop("memset", bfv(o_yb, 2048), 0.0, s=ve)
    if debug:
        s_dbg = sem("s_dbg")
        wait("gpsimd", s_out, s_out.v)
        dma("gpsimd", dbg_o["d_xn"], cc_xn_out.ap(), s_dbg)
        dma("gpsimd", dbg_o["d_lg"], fv(o_logits, 256), s_dbg)
        dma("gpsimd", dbg_o["d_rt"][:, 0:256], fv(o_sel, 256), s_dbg)
        dma("gpsimd", dbg_o["d_rt"][:, 256:512], fv(o_rank, 256), s_dbg)
        dma("gpsimd", dbg_o["d_rt"][:, 512:768], fv(o_p, 256), s_dbg)
        dma("gpsimd", dbg_o["d_rt"][:, 768:832], fv(o_mysel, 64), s_dbg)
        dma("gpsimd", dbg_o["d_rt"][:, 832:896], fv(o_pif, 64), s_dbg)
        dma("gpsimd", dbg_o["d_rt"][:, 896:900], fv(o_th, 4), s_dbg)
        dma("gpsimd", dbg_o["d_rt"][:, 900:904], fv(o_ptil + 0, 4), s_dbg)
        dma("gpsimd", dbg_o["d_rab"], rowab_buf.ap(), s_dbg)
        dma("gpsimd", dbg_o["d_idx"], idxp_buf.ap(), s_dbg)
        dma("gpsimd", dbg_o["d_qkv1"], qkv1_buf.ap(), s_dbg)
        dma("gpsimd", dbg_o["d_s"], cc_sown.ap(), s_dbg)
        dma("gpsimd", dbg_o["d_y"], cc_y_out.ap(), s_dbg)
        wait("gpsimd", s_dbg, s_dbg.v)
    wait("sync", s_out, s_out.v)

    nc._dbg = {k: v for k, v in locals().items()
               if isinstance(v, int) and k.startswith("o_")}
    with nc.Block() as block:
        @block.sync
        def _(e):
            for f in prog["sync"]:
                f(e)

        @block.tensor
        def _(e):
            for f in prog["tensor"]:
                f(e)

        @block.vector
        def _(e):
            for f in prog["vector"]:
                f(e)

        @block.scalar
        def _(e):
            for f in prog["scalar"]:
                f(e)

        @block.gpsimd
        def _(e):
            for f in prog["gpsimd"]:
                f(e)

    return nc


def _host_inputs(inputs):
    x = np.asarray(inputs["L0_attn_x"]).reshape(N, D).astype(np.float32)
    w_qkv = np.asarray(inputs["L0_attn_w_qkv"]).astype(np.float32)
    w_out = np.asarray(inputs["L0_attn_w_out"]).astype(np.float32)
    Wg = np.asarray(inputs["L0_ffn_Wg"]).astype(np.float32)
    bg = np.asarray(inputs["L0_ffn_bg"]).astype(np.float32)
    W1 = np.asarray(inputs["L0_ffn_W1"]).astype(np.float32)
    W2 = np.asarray(inputs["L0_ffn_W2"]).astype(np.float32)

    invf = (np.float32(1.0) / np.float32(10000.0) **
            (np.arange(0, 32, 2, dtype=np.float32) / np.float32(32)))
    freqs = np.arange(N, dtype=np.float32)[:, None] * invf[None, :]
    cos = np.cos(freqs.astype(np.float64)).astype(np.float32)
    sin = np.sin(freqs.astype(np.float64)).astype(np.float32)
    rotA = np.zeros((64, N), np.float32)
    rotB = np.zeros((64, N), np.float32)
    rotA[0:16] = cos.T
    rotA[16:32] = 1.0
    rotA[32:48] = -sin.T
    rotB[0:16] = sin.T
    rotB[32:48] = cos.T
    rotB[48:64] = 1.0

    trimask = np.where(np.triu(np.ones((128, 128), bool), 1),
                       np.float32(-1e30), np.float32(0.0)).astype(np.float32)
    trimT4 = np.zeros((128, 2048), np.float32)
    for t in range(4):
        trimT4[:, t * 512:t * 512 + t * 128] = -1e30
        trimT4[:, t * 512 + t * 128:t * 512 + (t + 1) * 128] = trimask.T

    common = {
        "rotAi": rotA.astype(bfdt),
        "rotBi": rotB.astype(bfdt),
        "trimT4i": trimT4.astype(bfdt),
        "identbfi": np.eye(128).astype(bfdt),
        "identf32i": np.eye(128).astype(np.float32),
        "utrii": (np.arange(128)[:, None] <= np.arange(128)[None, :])
        .astype(bfdt),
        "onesbfi": np.ones((128, 128), bfdt),
        "onesf32i": np.ones((128, 128), np.float32),
        "wgi": Wg,
        "bgi": bg.reshape(1, E),
    }

    in_maps = []
    for c in range(NCORES):
        h0 = HPC * c
        rows = []
        for blk in range(3):
            for hh in range(HPC):
                r = w_qkv[blk * 2048 + (h0 + hh) * DH:
                          blk * 2048 + (h0 + hh + 1) * DH]
                if blk < 2:
                    r = r[PERM]
                rows.append(r)
        wqkvT_c = np.ascontiguousarray(np.concatenate(rows, 0).T)

        e = c // 2
        wsel_c = np.zeros((128, E), np.float32)
        wsel_c[:, e] = 1.0
        q = np.arange(128)[:, None]
        kk = np.arange(8)[None, :]
        in_maps.append(dict(common, **{
            "x_sl": np.ascontiguousarray(x[c * TPC:(c + 1) * TPC]),
            "wqkvTi": wqkvT_c,
            "woutFi": np.ascontiguousarray(w_out.T),
            "aoixi": (((np.arange(16)[None, :] // 2) * 128 +
                       np.arange(128)[:, None]) * 8 + c).astype(np.int32),
            "w1i": np.ascontiguousarray(W1[e]),
            "w2i": np.ascontiguousarray(W2[e]),
            "wseli": wsel_c,
            "hbasei": np.full((128, 1), 1024.0 * (c % 2), np.float32),
            "ownpmi": (q * 64 + 8 * c + kk).astype(np.int32),
        }))
    return in_maps


def kernel(**inputs):
    key = ("ncd" if DEBUG else "nc")
    if key not in _CACHE:
        _CACHE[key] = build_program(debug=DEBUG)
    nc = _CACHE[key]
    in_maps = _host_inputs(inputs)
    res = run_bass_kernel_spmd(nc, in_maps, core_ids=list(range(NCORES)))
    if DEBUG:
        _CACHE["dbg_results"] = res.results
    out = np.concatenate([r["out_sl"] for r in res.results], axis=0)
    return out.reshape(1, N, D).astype(np.float32)


# version 11
# speedup vs baseline: 1.1027x; 1.0137x over previous
import numpy as np
import ml_dtypes

import concourse.bass as bass
import concourse.mybir as mybir
from concourse.bass import IndirectOffsetOnAxis
from concourse.bass_utils import run_bass_kernel_spmd

F32 = mybir.dt.float32
BF16 = mybir.dt.bfloat16
I32 = mybir.dt.int32
AF = mybir.ActivationFunctionType
ALU = mybir.AluOpType
AX = mybir.AxisListType
bfdt = ml_dtypes.bfloat16

NCORES = 8
N = 8192
D = 2048
DH = 128
HPC = 2
TPC = N // NCORES
E = 4
CAP = 2048
HALF = CAP // 2
HDIM = 8192
SCALE = float(DH) ** -0.5
BIG = 99999.0
NBLK = 16
NTILE = 64

PERM = np.concatenate([np.arange(0, 16), np.arange(32, 48),
                       np.arange(16, 32), np.arange(48, 128)])

_CACHE = {}
DEBUG = False


class Sem:
    def __init__(self, h):
        self.h = h
        self.v = 0

    def inc(self, n=1):
        self.v += n
        return self.v


class Gate:
    def __init__(self):
        self.sem = None
        self.val = 0

    def set(self, sem, val):
        self.sem = sem
        self.val = val


class Arena:
    def __init__(self, ap, cols, name):
        self.ap = ap
        self.cols = cols
        self.off = 0
        self.marks = []
        self.name = name
        self.peak = 0

    def take(self, w, align=16):
        if self.off % align:
            self.off += align - (self.off % align)
        o = self.off
        self.off += w
        self.peak = max(self.peak, self.off)
        assert self.off <= self.cols, f"{self.name} overflow {self.off}>{self.cols}"
        return o

    def push(self):
        self.marks.append(self.off)

    def pop(self):
        self.off = self.marks.pop()


def build_program(debug=False):
    nc = bass.Bass("TRN2", target_bir_lowering=False)

    def ext(name, shape, dt):
        return nc.dram_tensor(name, shape, dt, kind="ExternalInput").ap()

    x_sl = ext("x_sl", [TPC, D], F32)
    wqkvTi = ext("wqkvTi", [D, 768], BF16)
    woutFi = ext("woutFi", [D, D], BF16)
    aoixi = ext("aoixi", [128, 16], I32)
    wgi = ext("wgi", [D, E], F32)
    bgi = ext("bgi", [1, E], F32)
    w1i = ext("w1i", [D, HDIM], BF16)
    w2i = ext("w2i", [HDIM, D], BF16)
    rotAi = ext("rotAi", [64, N], BF16)
    rotBi = ext("rotBi", [64, N], BF16)
    trimT4i = ext("trimT4i", [128, 2048], BF16)
    identbfi = ext("identbfi", [128, 128], BF16)
    identf32i = ext("identf32i", [128, 128], F32)
    utrii = ext("utrii", [128, 128], BF16)
    onesbfi = ext("onesbfi", [128, 128], BF16)
    onesf32i = ext("onesf32i", [128, 128], F32)
    wseli = ext("wseli", [128, E], F32)
    hbasei = ext("hbasei", [128, 1], F32)
    ownpmi = ext("ownpmi", [128, 8], I32)

    out_sl = nc.dram_tensor("out_sl", [TPC, D], F32, kind="ExternalOutput").ap()

    cc_xn_in = nc.dram_tensor("cc_xn_in", [TPC, D], BF16)
    cc_xn_out = nc.dram_tensor("cc_xn_out", [N, D], BF16)
    cc_ao_in = [nc.dram_tensor(f"cc_ao_in{h}", [128, N], BF16)
                for h in range(2)]
    cc_ao_out = [nc.dram_tensor(f"cc_ao_out{h}", [NCORES * 128 * 8, N // 8],
                                BF16) for h in range(2)]
    cc_sown = nc.dram_tensor("cc_sown", [TPC, D], F32)
    cc_y_in = nc.dram_tensor("cc_y_in", [HALF, D], BF16)
    cc_y_out = nc.dram_tensor("cc_y_out", [NCORES * HALF, D], BF16)
    idxp_buf = nc.dram_tensor("idxp_buf", [HALF, 2], F32)
    rowab_buf = nc.dram_tensor("rowab_buf", [N, 2], F32)
    qkv1_buf = nc.dram_tensor("qkv1_buf", [128, 3 * N], BF16)
    if debug:
        dbg_o = {
            "d_xn": nc.dram_tensor("d_xn", [N, D], BF16,
                                   kind="ExternalOutput").ap(),
            "d_lg": nc.dram_tensor("d_lg", [128, 256], F32,
                                   kind="ExternalOutput").ap(),
            "d_rt": nc.dram_tensor("d_rt", [128, 1024], F32,
                                   kind="ExternalOutput").ap(),
            "d_rab": nc.dram_tensor("d_rab", [N, 2], F32,
                                    kind="ExternalOutput").ap(),
            "d_idx": nc.dram_tensor("d_idx", [HALF, 2], F32,
                                    kind="ExternalOutput").ap(),

            "d_qkv1": nc.dram_tensor("d_qkv1", [128, 3 * N], BF16,
                                     kind="ExternalOutput").ap(),
            "d_s": nc.dram_tensor("d_s", [1024, D], F32,
                                  kind="ExternalOutput").ap(),
            "d_y": nc.dram_tensor("d_y", [NCORES * HALF, D], BF16,
                                  kind="ExternalOutput").ap(),
        }

    import contextlib
    es = contextlib.ExitStack()
    nc._keepalive = es

    prog = {k: [] for k in ("sync", "tensor", "vector", "scalar", "gpsimd")}

    def op(eng, fn):
        prog[eng].append(fn)

    def sem(name):
        return Sem(es.enter_context(nc.semaphore(name)))

    te = sem("te")
    ve = sem("ve")
    se = sem("sescal")
    geio = sem("geio")
    cxn = sem("cxn")
    cy = sem("cy")
    s_const = sem("s_const")
    s_lnx = sem("s_lnx")
    s_xni = sem("s_xni")
    s_w0 = sem("s_w0")
    s_xnt = [sem("s_xnt0"), sem("s_xnt1")]
    s_rot = [sem("s_rot0"), sem("s_rot1")]
    s_st = [sem("s_st0"), sem("s_st1"), sem("s_st2")]
    s_rel = sem("s_rel")
    s_ao = sem("s_ao")
    cao = sem("cao")
    s_alh = sem("s_alh")
    s_wo = [sem("s_wo0"), sem("s_wo1")]
    s_sst = sem("s_sst")
    s_rab = sem("s_rab")
    s_scat = sem("s_scat")
    s_idx = sem("s_idx")
    s_rabg = sem("s_rabg")
    s_xg = [sem("s_xg0"), sem("s_xg1")]
    s_wout = sem("s_wout")
    s_w1 = [sem("s_w10"), sem("s_w11")]
    s_w2 = [sem("s_w20"), sem("s_w21")]
    s_yst = [sem("s_yst0"), sem("s_yst1")]
    s_y5 = sem("s_y5")
    s_p5x = sem("s_p5x")
    s_p5s = sem("s_p5s")
    s_out = sem("s_out")

    def wait(eng, s, v):
        if v <= 0:
            return
        op(eng, lambda e, _h=s.h, _v=v: e.wait_ge(_h, _v))

    def gwait(eng, gate):
        if gate.sem is not None and gate.val > 0:
            wait(eng, gate.sem, gate.val)

    def dma(eng, out, in_, s, transpose=False):
        def f(e, _o=out, _i=in_, _h=s.h, _t=transpose):
            if _t:
                e.dma_start(out=_o, in_=_i, transpose=True).then_inc(_h, 16)
            else:
                e.dma_start(out=_o, in_=_i).then_inc(_h, 16)
        op(eng, f)
        return s.inc(16)

    def mm(out, lhsT, rhs, start, stop, s=None):
        def f(e, _o=out, _l=lhsT, _r=rhs, _a=start, _b=stop, _s=s):
            i = e.matmul(_o, _l, _r, start=_a, stop=_b)
            if _s is not None:
                i.then_inc(_s.h, 1)
        op("tensor", f)
        return s.inc() if s is not None else None

    def petr(out, in_, ident, s=None):
        def f(e, _o=out, _i=in_, _d=ident, _s=s):
            i = e.transpose(_o, _i, _d)
            if _s is not None:
                i.then_inc(_s.h, 1)
        op("tensor", f)
        return s.inc() if s is not None else None

    EDGE_ALL = [False]

    def vop(meth, *args, s=None, eng="vector", **kw):
        if EDGE_ALL[0] and eng == "vector" and s is None:
            s = ve

        def f(e, _m=meth, _a=args, _k=kw, _s=s):
            i = getattr(e, _m)(*_a, **_k)
            if _s is not None:
                i.then_inc(_s.h, 1)
        op(eng, f)
        r = s.inc() if s is not None else None
        if EDGE_ALL[0] and eng == "vector":
            wait("vector", ve, ve.v)
        return r

    def act(out, in_, func, bias=0.0, scale=1.0, accum=None, s=None):
        def f(e, _o=out, _i=in_, _f=func, _b=bias, _c=scale, _ac=accum, _s=s):
            if _ac is not None:
                i = e.activation(_o, _i, _f, bias=_b, scale=_c, accum_out=_ac)
            else:
                i = e.activation(_o, _i, _f, bias=_b, scale=_c)
            if _s is not None:
                i.then_inc(_s.h, 1)
        op("scalar", f)
        return s.inc() if s is not None else None

    def coll(kind, alu, ins, outs, s):
        def f(e, _k=kind, _a=alu, _i=ins, _o=outs, _h=s.h):
            e.collective_compute(_k, _a, replica_groups=[list(range(NCORES))],
                                 ins=_i, outs=_o).then_inc(_h, 1)
        op("gpsimd", f)
        return s.inc()

    ABF, AFC, AIC = 81056, 12048, 608
    abf_t = nc.alloc_sbuf_tensor("abf", [128, ABF], BF16)
    af_t = nc.alloc_sbuf_tensor("af32", [128, AFC], F32)
    ai_t = nc.alloc_sbuf_tensor("ai32", [128, AIC], I32)
    abf = Arena(abf_t.ap(), ABF, "bf")
    af = Arena(af_t.ap(), AFC, "f32")
    ai = Arena(ai_t.ap(), AIC, "i32")

    def bfv(o, w):
        return abf.ap[:, o:o + w]

    def fv(o, w):
        return af.ap[:, o:o + w]

    def ivv(o, w):
        return ai.ap[:, o:o + w]

    pb = [es.enter_context(nc.psum_tensor(f"pb{i}", [128, 512], F32))
          for i in range(7)]
    pbm = es.enter_context(nc.psum_tensor("pbm", [128, 512], F32))
    pbg = [Gate() for _ in range(7)]
    pbmg = Gate()

    o_identbf = abf.take(128)
    o_utri = abf.take(128)
    o_onesbf = abf.take(128)
    o_selbf = abf.take(64)
    o_identf = af.take(128)
    o_onesf = af.take(128)
    o_wsel = af.take(E)
    o_hbase = af.take(1)
    o_iota = af.take(64)
    o_logits = af.take(256)
    o_t1 = af.take(64)
    o_cnt = af.take(256)
    o_p = af.take(256)
    o_score = af.take(256)
    o_eqm = af.take(256)
    o_rank = af.take(256)
    o_sel = af.take(256)
    o_r1 = af.take(256)
    o_r2 = af.take(256)
    o_cnt4 = af.take(4)
    o_lo = af.take(4)
    o_hi = af.take(4)
    o_mid = af.take(4)
    o_bc = af.take(4)
    o_ge4 = af.take(4)
    o_th = af.take(4)
    o_thbc = af.take(4)
    o_rr = af.take(4)
    o_colsum = af.take(64)
    o_colp = af.take(64)
    o_cps = af.take(64)
    o_capc = af.take(1)
    o_myrec = af.take(128)
    o_rowa = af.take(64)
    o_rowb = af.take(64)
    o_rowi = af.take(128)
    o_mysel = af.take(64)
    o_myrank = af.take(64)
    o_myp = af.take(64)
    o_pif = af.take(64)
    o_rab = af.take(16)
    o_rab2 = af.take(16)
    o_ptil = af.take(8)
    o_invr = af.take(512)
    o_ir1 = af.take(512)
    o_eps = af.take(1)
    o_ss4 = af.take(4)
    o_ss64 = af.take(64)
    o_ownpm = ai.take(8)
    o_aoix = ai.take(16)
    o_pi = ai.take(64)
    o_iadj = ai.take(64)
    o_ish = ai.take(64)
    o_tok = ai.take(8)
    o_rowA8 = ai.take(8)
    o_rowB8 = ai.take(8)
    o_im = ai.take(64)

    abf.push()
    o_xg = abf.take(2 * 2048)
    o_xgt = abf.take(16 * 1024)
    abf.push()
    o_qt = abf.take(N)
    o_kt = abf.take(N)
    o_vt = abf.take(N)
    abf.push()
    o_wqkvT = abf.take(16 * 768)
    o_wg = abf.take(16 * E)
    o_bg = abf.take(E)
    o_xnt = [abf.take(16 * 512) for _ in range(2)]
    o_xnstage = abf.take(2048)
    o_rot = [abf.take(1024) for _ in range(2)]
    o_rsc = abf.take(1024)
    o_qkv1st = abf.take(3 * 512)
    af.push()
    o_xsb = af.take(2048)
    o_stat = af.take(8)

    dma("sync", bfv(o_identbf, 128), identbfi, s_const)
    dma("sync", bfv(o_utri, 128), utrii, s_const)
    dma("sync", bfv(o_onesbf, 128), onesbfi, s_const)
    dma("sync", fv(o_identf, 128), identf32i, s_const)
    dma("sync", fv(o_onesf, 128), onesf32i, s_const)
    dma("sync", fv(o_wsel, E), wseli, s_const)
    dma("sync", fv(o_hbase, 1), hbasei, s_const)
    dma("sync", ivv(o_ownpm, 8), ownpmi, s_const)
    dma("sync", ivv(o_aoix, 16), aoixi, s_const)

    dma("gpsimd", bfv(o_wqkvT, 16 * 768),
        wqkvTi.rearrange("(a p) f -> p a f", p=128), s_w0)
    dma("gpsimd", bfv(o_wg, 16 * E),
        wgi.rearrange("(a p) e -> p a e", p=128), s_w0)
    dma("gpsimd", abf.ap[0:1, o_bg:o_bg + E], bgi, s_w0)
    vop("iota", ivv(o_pi, 64), pattern=[[128, 64]], base=0,
        channel_multiplier=1, eng="gpsimd", s=geio)

    wait("vector", geio, 1)
    vop("memset", fv(o_eps, 1), 1e-05)
    vop("tensor_copy", fv(o_iota, 64), ivv(o_pi, 64), s=ve)

    for k in range(8):
        if k >= 1:
            wait("sync", ve, ln_last_ve)
        dma("sync", fv(o_xsb, 2048), x_sl[k * 128:(k + 1) * 128, :], s_lnx)
        wait("vector", s_lnx, s_lnx.v)
        vop("reduce_sum", fv(o_stat, 1), fv(o_xsb, 2048), axis=AX.X, s=ve)
        wait("vector", ve, ve.v)
        vop("tensor_scalar", fv(o_stat + 1, 1), fv(o_stat, 1), 1.0 / D, None,
            op0=ALU.mult, s=ve)
        wait("vector", ve, ve.v)
        vop("tensor_scalar", fv(o_xsb, 2048), fv(o_xsb, 2048),
            fv(o_stat + 1, 1), None, op0=ALU.subtract, s=ve)
        wait("scalar", ve, ve.v)
        if k >= 1:
            wait("scalar", s_xni, ln_last_gq)
        act(bfv(o_xnstage, 2048), fv(o_xsb, 2048), AF.Square,
            accum=fv(o_stat + 2, 1), s=se)
        wait("scalar", se, se.v)
        act(fv(o_stat + 3, 1), fv(o_stat + 2, 1), AF.Ln, bias=fv(o_eps, 1),
            scale=1.0 / D, s=se)
        wait("scalar", se, se.v)
        act(fv(o_stat + 4, 1), fv(o_stat + 3, 1), AF.Exp, scale=-0.5, s=se)
        wait("vector", se, se.v)
        vop("tensor_scalar", bfv(o_xnstage, 2048), fv(o_xsb, 2048),
            fv(o_stat + 4, 1), None, op0=ALU.mult, s=ve)
        ln_last_ve = ve.v
        wait("gpsimd", ve, ve.v)
        dma("gpsimd", cc_xn_in.ap()[k * 128:(k + 1) * 128, :],
            bfv(o_xnstage, 2048), s_xni)
        ln_last_gq = s_xni.v
    wait("gpsimd", s_xni, s_xni.v)
    coll("AllGather", ALU.bypass, [cc_xn_in.ap().opt()],
         [cc_xn_out.ap().opt()], cxn)

    xnt_gate = [Gate(), Gate()]
    rot_gate = [Gate(), Gate()]
    st_gate = [Gate(), Gate(), Gate()]
    xnt_ld = {}
    rot_ld = {}

    def emit_xnt_load(j):
        buf = j % 2
        gwait("scalar", xnt_gate[buf])
        for d16 in range(16):
            src = cc_xn_out.ap()[j * 512:(j + 1) * 512,
                                 d16 * 128:(d16 + 1) * 128]
            dma("scalar", bfv(o_xnt[buf] + d16 * 512, 512), src, s_xnt[buf],
                transpose=True)
        xnt_ld[j] = s_xnt[buf].v

    def emit_rot_load(j):
        buf = j % 2
        gwait("sync", rot_gate[buf])
        dma("sync", abf.ap[0:64, o_rot[buf]:o_rot[buf] + 512],
            rotAi[:, j * 512:(j + 1) * 512], s_rot[buf])
        dma("sync", abf.ap[0:64, o_rot[buf] + 512:o_rot[buf] + 1024],
            rotBi[:, j * 512:(j + 1) * 512], s_rot[buf])
        rot_ld[j] = s_rot[buf].v

    wait("sync", cxn, 1)
    wait("scalar", cxn, 1)
    emit_xnt_load(0)
    emit_rot_load(0)
    emit_xnt_load(1)
    emit_rot_load(1)

    wait("tensor", s_w0, s_w0.v)
    wait("tensor", s_const, s_const.v)
    for j in range(NBLK):
        buf = j % 2
        wait("tensor", s_xnt[buf], xnt_ld[j])
        for F in range(4):
            gwait("tensor", pbg[F])
            for d16 in range(16):
                mm(pb[F][:, 0:512], bfv(o_wqkvT + d16 * 768 + F * 128, 128),
                   bfv(o_xnt[buf] + d16 * 512, 512),
                   d16 == 0, d16 == 15, s=te if d16 == 15 else None)
        gwait("tensor", pbmg)
        vpair_te = {}
        for pair in range(2):
            gwait("tensor", pbg[4])
            for tt in (2 * pair, 2 * pair + 1):
                colo = (tt % 2) * 256
                for d16 in range(16):
                    lhs = bfv(o_xnt[buf] + d16 * 512 + tt * 128, 128)
                    mm(pb[4][:, colo:colo + 256], lhs,
                       bfv(o_wqkvT + d16 * 768 + 512, 256),
                       d16 == 0, d16 == 15, s=te if d16 == 15 else None)
                    mm(pbm[:, tt * 4:tt * 4 + 4], lhs, bfv(o_wg + d16 * 4, 4),
                       d16 == 0, False)
                mm(pbm[:, tt * 4:tt * 4 + 4],
                   abf.ap[0:1, o_onesbf:o_onesbf + 128],
                   abf.ap[0:1, o_bg:o_bg + E], False, True,
                   s=te if tt == 3 else None)
            vpair_te[pair] = te.v
            wait("vector", te, te.v)
            for tt in (2 * pair, 2 * pair + 1):
                colo = (tt % 2) * 256
                vop("tensor_copy", bfv(o_vt + (j * 4 + tt) * 128, 128),
                    pb[4][:, colo:colo + 128])
                gwait("vector", st_gate[2])
                vop("tensor_copy", bfv(o_qkv1st + 1024 + tt * 128, 128),
                    pb[4][:, colo + 128:colo + 256], s=ve)
            pbg[4].set(ve, ve.v)
        te_blk = te.v
        xnt_gate[buf].set(te, te_blk)
        if j + 2 < NBLK:
            emit_xnt_load(j + 2)

        wait("vector", te, te_blk)
        wait("vector", s_rot[buf], rot_ld[j])
        for F in range(4):
            h = F % 2
            if h == 0:
                dsto = (o_qt if F == 0 else o_kt) + j * 512
            else:
                dsto = o_qkv1st + (F // 2) * 512
                gwait("vector", st_gate[F // 2])
            vop("tensor_copy", bfv(dsto, 512), pb[F][:, 0:512],
                s=ve if F == 3 else None)
        for F in range(4):
            pbg[F].set(ve, ve.v)
        for F in range(4):
            h = F % 2
            dsto = ((o_qt if F == 0 else o_kt) + j * 512) if h == 0 \
                else (o_qkv1st + (F // 2) * 512)
            in0 = abf.ap[0:32, dsto:dsto + 512]
            in32 = abf.ap[32:64, dsto:dsto + 512]
            cA = abf.ap[0:32, o_rot[buf]:o_rot[buf] + 512]
            sA = abf.ap[32:64, o_rot[buf]:o_rot[buf] + 512]
            sB = abf.ap[0:32, o_rot[buf] + 512:o_rot[buf] + 1024]
            cB = abf.ap[32:64, o_rot[buf] + 512:o_rot[buf] + 1024]
            t1 = abf.ap[0:32, o_rsc:o_rsc + 512]
            t2 = abf.ap[0:32, o_rsc + 512:o_rsc + 1024]
            t3 = abf.ap[32:64, o_rsc:o_rsc + 512]
            t4 = abf.ap[32:64, o_rsc + 512:o_rsc + 1024]
            vop("tensor_mul", t1, in0, cA)
            vop("tensor_mul", t2, in32, sA)
            vop("tensor_mul", t3, in32, cB)
            vop("tensor_mul", t4, in0, sB, s=ve)
            wait("vector", ve, ve.v)
            vop("tensor_add", in0, t1, t2)
            vop("tensor_add", in32, t3, t4, s=ve)
        rot_gate[buf].set(ve, ve.v)
        if j + 2 < NBLK:
            emit_rot_load(j + 2)
        vop("tensor_copy", fv(o_logits + j * 16, 16), pbm[:, 0:16], s=ve)
        pbmg.set(ve, ve.v)
        wait("sync", ve, ve.v)
        dma("sync", qkv1_buf.ap()[:, j * 512:(j + 1) * 512],
            bfv(o_qkv1st, 512), s_st[0])
        st_gate[0].set(s_st[0], s_st[0].v)
        dma("sync", qkv1_buf.ap()[:, N + j * 512:N + (j + 1) * 512],
            bfv(o_qkv1st + 512, 512), s_st[1])
        st_gate[1].set(s_st[1], s_st[1].v)
        dma("sync", qkv1_buf.ap()[:, 2 * N + j * 512:2 * N + (j + 1) * 512],
            bfv(o_qkv1st + 1024, 512), s_st[2])
        st_gate[2].set(s_st[2], s_st[2].v)
    P1_VE = ve.v
    P1_TE = te.v
    abf.pop()
    af.pop()

    EDGE_ALL[0] = True

    def eview(base, e):
        return af.ap[:, base:base + 256].rearrange("p (i e) -> p e i",
                                                   e=E)[:, e, :]

    vop("tensor_tensor", fv(o_t1, 64), eview(o_logits, 0),
        eview(o_logits, 1), op=ALU.max)
    vop("tensor_tensor", fv(o_r1, 64), eview(o_logits, 2),
        eview(o_logits, 3), op=ALU.max)
    vop("tensor_tensor", fv(o_t1, 64), fv(o_t1, 64), fv(o_r1, 64), op=ALU.max)
    for e in range(E):
        ce = eview(o_cnt, e)
        vop("memset", ce, 0.0)
        for ep in range(E):
            if ep == e:
                continue
            opk = ALU.is_gt if ep > e else ALU.is_ge
            vop("tensor_tensor", fv(o_r1, 64), eview(o_logits, ep),
                eview(o_logits, e), op=opk)
            vop("tensor_add", ce, ce, fv(o_r1, 64))
    for e in range(E):
        vop("tensor_tensor", eview(o_r1, e), eview(o_logits, e),
            fv(o_t1, 64), op=ALU.subtract, s=ve if e == E - 1 else None)
    wait("scalar", ve, ve.v)
    act(fv(o_p, 256), fv(o_r1, 256), AF.Exp, s=se)
    wait("vector", se, se.v)
    vop("tensor_scalar", fv(o_r2, 256), fv(o_cnt, 256), 1.5, None,
        op0=ALU.is_le)
    vop("tensor_mul", fv(o_p, 256), fv(o_p, 256), fv(o_r2, 256))
    vop("tensor_tensor", fv(o_t1, 64), eview(o_p, 0), eview(o_p, 1),
        op=ALU.add)
    vop("tensor_tensor", fv(o_r1, 64), eview(o_p, 2), eview(o_p, 3),
        op=ALU.add)
    vop("tensor_add", fv(o_t1, 64), fv(o_t1, 64), fv(o_r1, 64))
    vop("reciprocal", fv(o_r1, 64), fv(o_t1, 64), s=ve)
    wait("vector", ve, ve.v)
    for e in range(E):
        vop("tensor_tensor", eview(o_p, e), eview(o_p, e), fv(o_r1, 64),
            op=ALU.mult)
    vop("tensor_scalar", fv(o_score, 256), fv(o_r2, 256), 1.0, None,
        op0=ALU.subtract)
    vop("tensor_mul", fv(o_r2, 256), fv(o_p, 256), fv(o_r2, 256))
    vop("tensor_add", fv(o_score, 256), fv(o_score, 256), fv(o_r2, 256))
    vop("memset", fv(o_lo, 4), -2.0)
    vop("memset", fv(o_hi, 4), 2.0)
    vop("memset", fv(o_capc, 1), float(CAP) - 0.5, s=ve)

    def count4(cmp_op, scal_off, iota_mode=False):
        for e in range(E):
            tgt = eview(o_r1, e)
            if iota_mode:
                vop("tensor_scalar", tgt, fv(o_iota, 64),
                    fv(scal_off + e, 1), None, op0=cmp_op)
                vop("tensor_mul", tgt, tgt, eview(o_eqm, e))
            else:
                vop("tensor_scalar", tgt, eview(o_score, e),
                    fv(scal_off + e, 1), None, op0=cmp_op)
            vop("reduce_sum", fv(o_cnt4 + e, 1), tgt, axis=AX.X, s=ve)
            wait("vector", ve, ve.v)
        wait("tensor", ve, ve.v)
        gwait("tensor", pbmg)
        mm(pbm[:, 0:4], fv(o_onesf, 128), fv(o_cnt4, 4), True, True, s=te)
        wait("vector", te, te.v)
        vop("tensor_copy", fv(o_bc, 4), pbm[:, 0:4], s=ve)
        pbmg.set(ve, ve.v)

    for it in range(34):
        vop("tensor_tensor", fv(o_mid, 4), fv(o_lo, 4), fv(o_hi, 4),
            op=ALU.add)
        vop("tensor_scalar", fv(o_mid, 4), fv(o_mid, 4), 0.5, None,
            op0=ALU.mult, s=ve)
        count4(ALU.is_gt, o_mid)
        vop("tensor_scalar", fv(o_ge4, 4), fv(o_bc, 4), fv(o_capc, 1),
            None, op0=ALU.is_ge)
        vop("tensor_copy", ivv(o_im, 4), fv(o_ge4, 4))
        vop("select", fv(o_ss4, 4), ivv(o_im, 4), fv(o_mid, 4), fv(o_lo, 4))
        vop("tensor_copy", fv(o_lo, 4), fv(o_ss4, 4))
        vop("select", fv(o_ss4, 4), ivv(o_im, 4), fv(o_hi, 4), fv(o_mid, 4))
        vop("tensor_copy", fv(o_hi, 4), fv(o_ss4, 4), s=ve)
    vop("tensor_copy", fv(o_th, 4), fv(o_hi, 4), s=ve)
    count4(ALU.is_gt, o_th)
    vop("tensor_scalar", fv(o_rr, 4), fv(o_bc, 4), -1.0, float(CAP),
        op0=ALU.mult, op1=ALU.add, s=ve)
    for e in range(E):
        vop("tensor_scalar", eview(o_eqm, e), eview(o_score, e),
            fv(o_th + e, 1), None, op0=ALU.is_equal)
    vop("memset", fv(o_lo, 4), -1.0)
    vop("memset", fv(o_hi, 4), float(N), s=ve)
    for it in range(16):
        vop("tensor_tensor", fv(o_mid, 4), fv(o_lo, 4), fv(o_hi, 4),
            op=ALU.add)
        vop("tensor_scalar", fv(o_mid, 4), fv(o_mid, 4), 0.5, None,
            op0=ALU.mult, s=ve)
        count4(ALU.is_le, o_mid, iota_mode=True)
        vop("tensor_tensor", fv(o_ge4, 4), fv(o_bc, 4), fv(o_rr, 4),
            op=ALU.is_ge)
        vop("tensor_copy", ivv(o_im, 4), fv(o_ge4, 4))
        vop("select", fv(o_ss4, 4), ivv(o_im, 4), fv(o_mid, 4), fv(o_hi, 4))
        vop("tensor_copy", fv(o_hi, 4), fv(o_ss4, 4))
        vop("select", fv(o_ss4, 4), ivv(o_im, 4), fv(o_lo, 4), fv(o_mid, 4))
        vop("tensor_copy", fv(o_lo, 4), fv(o_ss4, 4), s=ve)
    for e in range(E):
        vop("tensor_scalar", eview(o_r1, e), eview(o_score, e),
            fv(o_th + e, 1), None, op0=ALU.is_gt)
        vop("tensor_scalar", eview(o_r2, e), fv(o_iota, 64),
            fv(o_hi + e, 1), None, op0=ALU.is_le)
        vop("tensor_mul", eview(o_r2, e), eview(o_r2, e), eview(o_eqm, e))
        vop("tensor_tensor", eview(o_sel, e), eview(o_r1, e), eview(o_r2, e),
            op=ALU.add)
    vop("tensor_mul", fv(o_p, 256), fv(o_p, 256), fv(o_sel, 256), s=ve)

    for e in range(E):
        vop("tensor_copy", bfv(o_selbf, 64), eview(o_sel, e), s=ve)
        wait("tensor", ve, ve.v)
        gwait("tensor", pbmg)
        mm(pbm[:, 64:128], bfv(o_utri, 128), bfv(o_selbf, 64), True, True)
        mm(pbm[0:1, 192:256], abf.ap[:, o_onesbf:o_onesbf + 1],
           bfv(o_selbf, 64), True, True, s=te)
        wait("vector", te, te.v)
        vop("tensor_copy", af.ap[0:1, o_colsum:o_colsum + 64],
            pbm[0:1, 192:256])
        vop("tensor_copy", af.ap[0:1, o_colp:o_colp + 64], pbm[0:1, 192:256])
        cur, oth = o_colp, o_cps
        sh = 1
        while sh < 64:
            vop("tensor_copy", af.ap[0:1, oth:oth + sh],
                af.ap[0:1, cur:cur + sh])
            vop("tensor_tensor", af.ap[0:1, oth + sh:oth + 64],
                af.ap[0:1, cur + sh:cur + 64],
                af.ap[0:1, cur:cur + 64 - sh], op=ALU.add)
            cur, oth = oth, cur
            sh *= 2
        vop("tensor_tensor", af.ap[0:1, o_colp:o_colp + 64],
            af.ap[0:1, cur:cur + 64],
            af.ap[0:1, o_colsum:o_colsum + 64], op=ALU.subtract, s=ve)
        wait("tensor", ve, ve.v)
        mm(pbm[:, 128:192], af.ap[0:1, o_onesf:o_onesf + 128],
           af.ap[0:1, o_colp:o_colp + 64], True, True, s=te)
        wait("vector", te, te.v)
        vop("tensor_copy", fv(o_r1, 64), pbm[:, 128:192])
        vop("tensor_tensor", eview(o_rank, e), pbm[:, 64:128],
            eview(o_sel, e), op=ALU.subtract)
        vop("tensor_tensor", eview(o_rank, e), eview(o_rank, e),
            fv(o_r1, 64), op=ALU.add, s=ve)
        pbmg.set(ve, ve.v)

    for src_b, dst_o in ((o_sel, o_mysel), (o_rank, o_myrank), (o_p, o_myp)):
        vop("memset", fv(dst_o, 64), 0.0)
        for e in range(E):
            vop("tensor_scalar", fv(o_r1, 64), eview(src_b, e),
                fv(o_wsel + e, 1), None, op0=ALU.mult)
            vop("tensor_add", fv(dst_o, 64), fv(dst_o, 64), fv(o_r1, 64))
    vop("tensor_scalar", fv(o_r1, 64), fv(o_myrank, 64), fv(o_hbase, 1),
        None, op0=ALU.subtract)
    vop("tensor_scalar", fv(o_r2, 64), fv(o_r1, 64), -0.5, None, op0=ALU.is_ge)
    vop("tensor_mul", fv(o_r2, 64), fv(o_r2, 64), fv(o_mysel, 64))
    vop("tensor_scalar", fv(o_mysel, 64), fv(o_r1, 64), float(HALF) - 0.5,
        None, op0=ALU.is_le)
    vop("tensor_mul", fv(o_r2, 64), fv(o_r2, 64), fv(o_mysel, 64))
    vop("tensor_copy", ivv(o_iadj, 64), fv(o_r1, 64))
    vop("memset", ivv(o_pi, 64), 127)
    vop("tensor_tensor", ivv(o_pi, 64), ivv(o_iadj, 64), ivv(o_pi, 64),
        op=ALU.bitwise_and)
    vop("memset", ivv(o_ish, 64), 3)
    vop("tensor_tensor", ivv(o_pi, 64), ivv(o_pi, 64), ivv(o_ish, 64),
        op=ALU.logical_shift_left)
    vop("memset", ivv(o_ish, 64), 7)
    vop("tensor_tensor", ivv(o_ish, 64), ivv(o_iadj, 64), ivv(o_ish, 64),
        op=ALU.logical_shift_right)
    vop("tensor_tensor", ivv(o_pi, 64), ivv(o_pi, 64), ivv(o_ish, 64),
        op=ALU.add)
    vop("tensor_copy", fv(o_pif, 64), ivv(o_pi, 64))
    vop("tensor_mul", fv(o_pif, 64), fv(o_pif, 64), fv(o_r2, 64))
    vop("tensor_scalar", fv(o_r1, 64), fv(o_r2, 64), -BIG, BIG, op0=ALU.mult,
        op1=ALU.add)
    vop("tensor_add", fv(o_pif, 64), fv(o_pif, 64), fv(o_r1, 64))
    vop("tensor_copy", ivv(o_pi, 64), fv(o_pif, 64))
    rec = af.ap[:, o_myrec:o_myrec + 128].rearrange("p (i z) -> p z i", z=2)
    vop("tensor_copy", rec[:, 0, :], fv(o_iota, 64))
    vop("tensor_copy", rec[:, 1, :], fv(o_myp, 64), s=ve)
    vop("memset", fv(o_rowa, 64), BIG)
    vop("memset", fv(o_rowb, 64), BIG)
    for e in range(E - 1, -1, -1):
        vop("tensor_scalar", fv(o_r1, 64), eview(o_rank, e), 1.0,
            float(CAP * e), op0=ALU.mult, op1=ALU.add)
        vop("tensor_copy", ivv(o_im, 64), eview(o_sel, e))
        vop("select", fv(o_ss64, 64), ivv(o_im, 64), fv(o_r1, 64),
            fv(o_rowa, 64))
        vop("tensor_copy", fv(o_rowa, 64), fv(o_ss64, 64))
    for e in range(E):
        vop("tensor_scalar", fv(o_r1, 64), eview(o_rank, e), 1.0,
            float(CAP * e), op0=ALU.mult, op1=ALU.add)
        vop("tensor_copy", ivv(o_im, 64), eview(o_sel, e))
        vop("select", fv(o_ss64, 64), ivv(o_im, 64), fv(o_r1, 64),
            fv(o_rowb, 64))
        vop("tensor_copy", fv(o_rowb, 64), fv(o_ss64, 64))
    vop("tensor_tensor", fv(o_r1, 64), eview(o_sel, 0), eview(o_sel, 1),
        op=ALU.add)
    vop("tensor_tensor", fv(o_r2, 64), eview(o_sel, 2), eview(o_sel, 3),
        op=ALU.add)
    vop("tensor_add", fv(o_r1, 64), fv(o_r1, 64), fv(o_r2, 64))
    vop("tensor_scalar", fv(o_r2, 64), fv(o_r1, 64), 1.5, None, op0=ALU.is_ge)
    vop("tensor_copy", ivv(o_im, 64), fv(o_r2, 64))
    vop("memset", fv(o_r1, 64), BIG)
    vop("select", fv(o_ss64, 64), ivv(o_im, 64), fv(o_rowb, 64), fv(o_r1, 64))
    vop("tensor_copy", fv(o_rowb, 64), fv(o_ss64, 64))
    rowi = af.ap[:, o_rowi:o_rowi + 128].rearrange("p (i z) -> p z i", z=2)
    vop("tensor_copy", rowi[:, 0, :], fv(o_rowa, 64))
    vop("tensor_copy", rowi[:, 1, :], fv(o_rowb, 64), s=ve)
    EDGE_ALL[0] = False
    VE_ROUTE = ve.v

    wait("gpsimd", ve, VE_ROUTE)
    dma("gpsimd", rowab_buf.ap().rearrange("(p i) z -> p (i z)", p=128),
        fv(o_rowi, 128), s_rab)

    def scat(e):
        with e.register("b1023") as r:
            e.reg_mov(r, HALF - 1)
            for i in range(64):
                e.indirect_dma_start(
                    out=idxp_buf.ap(),
                    out_offset=IndirectOffsetOnAxis(ivv(o_pi + i, 1), axis=0),
                    in_=af.ap[:, o_myrec + i * 2:o_myrec + i * 2 + 2],
                    in_offset=None, bounds_check=r,
                    oob_is_err=False).then_inc(s_scat.h, 16)
    op("gpsimd", scat)
    s_scat.inc(16 * 64)
    wait("gpsimd", s_scat, s_scat.v)
    dma("gpsimd", fv(o_rab, 16),
        idxp_buf.ap().rearrange("(p k) z -> p (k z)", p=128), s_idx)

    def rowabg(e):
        with e.register("brb") as r:
            e.reg_mov(r, N - 1)
            for k in range(8):
                e.indirect_dma_start(
                    out=af.ap[:, o_rab2 + k * 2:o_rab2 + k * 2 + 2],
                    out_offset=None, in_=rowab_buf.ap(),
                    in_offset=IndirectOffsetOnAxis(ivv(o_ownpm + k, 1), axis=0),
                    bounds_check=r, oob_is_err=False).then_inc(s_rabg.h, 16)
    op("gpsimd", rowabg)
    s_rabg.inc(16 * 8)
    wait("gpsimd", s_rab, s_rab.v)
    EDGE_ALL[0] = True
    wait("vector", s_idx, s_idx.v)
    wait("vector", s_rabg, s_rabg.v)
    rab = af.ap[:, o_rab:o_rab + 16].rearrange("p (k z) -> p z k", z=2)
    vop("tensor_copy", ivv(o_tok, 8), rab[:, 0, :])
    vop("tensor_copy", fv(o_ptil, 8), rab[:, 1, :])
    rab2 = af.ap[:, o_rab2:o_rab2 + 16].rearrange("p (k z) -> p z k", z=2)
    vop("tensor_copy", ivv(o_rowA8, 8), rab2[:, 0, :])
    vop("tensor_copy", ivv(o_rowB8, 8), rab2[:, 1, :], s=ve)
    EDGE_ALL[0] = False
    VE_TOK = ve.v

    wait("gpsimd", ve, VE_TOK)
    xg_gate = [Gate(), Gate()]
    xgs_gate = [Gate(), Gate()]
    gq_xg = {}
    o_xgf = af.take(256)
    for k in range(8):
        gwait("gpsimd", xg_gate[k % 2])

        def gat(e, _k=k):
            with e.register(f"bx{_k}") as r:
                e.reg_mov(r, N - 1)
                e.indirect_dma_start(
                    out=bfv(o_xg + (_k % 2) * 2048, 2048), out_offset=None,
                    in_=cc_xn_out.ap(),
                    in_offset=IndirectOffsetOnAxis(ivv(o_tok + _k, 1), axis=0),
                    bounds_check=r, oob_is_err=False).then_inc(
                        s_xg[_k % 2].h, 16)
        op("gpsimd", gat)
        gq_xg[k] = s_xg[k % 2].inc(16)
        wait("vector", s_xg[k % 2], gq_xg[k])
        for d16 in range(16):
            sl = d16 % 2
            gwait("vector", xgs_gate[sl])
            vop("tensor_copy", fv(o_xgf + sl * 128, 128),
                bfv(o_xg + (k % 2) * 2048 + d16 * 128, 128), s=ve)
            if d16 == 15:
                xg_gate[k % 2].set(ve, ve.v)
            wait("tensor", ve, ve.v)
            gwait("tensor", pbg[sl])
            petr(pb[sl][:, 0:128], fv(o_xgf + sl * 128, 128),
                 fv(o_identf, 128), s=te)
            xgs_gate[sl].set(te, te.v)
            wait("vector", te, te.v)
            vop("tensor_copy", bfv(o_xgt + d16 * 1024 + k * 128, 128),
                pb[sl][:, 0:128], s=ve)
            pbg[sl].set(ve, ve.v)
    VE_XGT = ve.v

    abf.push()
    o_pt = [abf.take(512) for _ in range(4)]
    o_pts = [abf.take(512) for _ in range(2)]
    o_drow = abf.take(N)
    o_dmul = abf.take(1024)
    o_negbf = abf.take(128)
    o_aoutT = abf.take(HPC * N)
    o_trimt4 = abf.take(2048)
    o_bias2 = af.take(512)
    s_tm4 = sem("s_tm4")
    dma("sync", bfv(o_trimt4, 2048), trimT4i, s_tm4)
    wait("vector", s_tm4, s_tm4.v)
    pt_gate = [Gate() for _ in range(4)]
    av_gate = [Gate(), Gate()]
    av_gate[0].set(pbg[4].sem, pbg[4].val)
    av_gate[1].set(pbg[5].sem, pbg[5].val)
    den_gate = Gate()
    pts_gate = [Gate(), Gate()]
    dmg = [Gate(), Gate()]
    pending_norm = [None]

    def gwait2(eng, ga, gb):
        if ga.sem is not None and gb.sem is not None and ga.sem is gb.sem:
            wait(eng, ga.sem, max(ga.val, gb.val))
        else:
            gwait(eng, ga)
            gwait(eng, gb)

    vop("memset", abf.ap[0:1, o_negbf:o_negbf + 128], -1.0, s=ve)
    wait("tensor", ve, ve.v)

    for h in range(HPC):
        if h == 1:
            wait("sync", te, te.v)
            for z in range(3):
                wait("sync", s_st[z], s_st[z].v)
            for z, dst in enumerate((o_qt, o_kt, o_vt)):
                dma("sync", bfv(dst, N), qkv1_buf.ap()[:, z * N:(z + 1) * N],
                    s_rel)
            wait("tensor", s_rel, s_rel.v)
            wait("vector", s_rel, s_rel.v)
        for GG in range(16):
            gwait("vector", dmg[GG % 2])
            vop("tensor_tensor", bfv(o_dmul + (GG % 2) * 512, 512),
                bfv(o_qt + GG * 512, 512), bfv(o_kt + GG * 512, 512),
                op=ALU.mult, s=ve)
            wait("tensor", ve, ve.v)
            gwait("tensor", pbmg)
            mm(pbm[0:1, 0:512], abf.ap[:, o_onesbf:o_onesbf + 1],
               bfv(o_dmul + (GG % 2) * 512, 512), True, True, s=te)
            dmg[GG % 2].set(te, te.v)
            wait("vector", te, te.v)
            vop("tensor_copy", abf.ap[0:1, o_drow + GG * 512:
                                      o_drow + (GG + 1) * 512],
                pbm[0:1, 0:512], s=ve)
            pbmg.set(ve, ve.v)

        for G in range(16):
            nkj = 4 * G + 4
            P = nkj // 2
            avbank = pb[4 + (G % 2)]
            dend = pb[6]
            gwait("tensor", pbmg)
            wait("tensor", ve, ve.v)
            mm(pbm[:, 0:512], abf.ap[0:1, o_negbf:o_negbf + 128],
               abf.ap[0:1, o_drow + G * 512:o_drow + (G + 1) * 512],
               True, True, s=te)
            wait("vector", te, te.v)
            vop("tensor_copy", fv(o_bias2, 512), pbm[:, 0:512], s=ve)
            pbmg.set(ve, ve.v)
            gwait("tensor", den_gate)
            gwait("tensor", av_gate[G % 2])
            smm_ts = {}
            exp_ts = {}
            pts_ts = {}
            for pj in range(P + 2):
                if pj < P:
                    c0, c1 = 2 * pj, 2 * pj + 1
                    gwait2("tensor", pbg[c0 % 4], pbg[c1 % 4])
                    for c in (c0, c1):
                        smm_ts[c] = mm(pb[c % 4][:, 0:512],
                                       bfv(o_kt + c * 128, 128),
                                       bfv(o_qt + G * 512, 512), True, True,
                                       s=te)
                if pj == 1 and pending_norm[0] is not None:
                    pending_norm[0]()
                    pending_norm[0] = None
                if 1 <= pj <= P:
                    p0, p1 = 2 * pj - 2, 2 * pj - 1
                    wait("tensor", se, exp_ts[p1])
                    mm(avbank[:, 0:512], bfv(o_vt + p0 * 128, 128),
                       bfv(o_pt[p0 % 4], 512), p0 == 0, False)
                    mm(avbank[:, 0:512], bfv(o_vt + p1 * 128, 128),
                       bfv(o_pt[p1 % 4], 512), False, p1 == nkj - 1, s=te)
                    pt_gate[p0 % 4].set(te, te.v)
                    pt_gate[p1 % 4].set(te, te.v)
                if pj >= 2:
                    q0 = 2 * pj - 4
                    wait("tensor", ve, pts_ts[pj - 2])
                    mm(dend[0:1, 0:512], abf.ap[:, o_onesbf:o_onesbf + 1],
                       bfv(o_pts[(pj - 2) % 2], 512),
                       q0 == 0, q0 + 1 == nkj - 1, s=te)
                    pts_gate[(pj - 2) % 2].set(te, te.v)
                if pj < P:
                    wait("vector", te, smm_ts[c1])
                    for c in (c0, c1):
                        sl = ve if c == c1 and c < 4 * G else None
                        vop("tensor_tensor", pb[c % 4][:, 0:512],
                            pb[c % 4][:, 0:512], fv(o_bias2, 512), op=ALU.add,
                            s=sl)
                        if c >= 4 * G:
                            vop("tensor_tensor", pb[c % 4][:, 0:512],
                                pb[c % 4][:, 0:512],
                                bfv(o_trimt4 + (c - 4 * G) * 512, 512),
                                op=ALU.add, s=ve if c == c1 else None)
                    add_inc = ve.v
                if 1 <= pj <= P:
                    wait("vector", se, exp_ts[2 * pj - 1])
                    gwait("vector", pts_gate[(pj - 1) % 2])
                    vop("tensor_tensor", bfv(o_pts[(pj - 1) % 2], 512),
                        bfv(o_pt[(2 * pj - 2) % 4], 512),
                        bfv(o_pt[(2 * pj - 1) % 4], 512), op=ALU.add, s=ve)
                    pts_ts[pj - 1] = ve.v
                if pj < P:
                    wait("scalar", ve, add_inc)
                    gwait2("scalar", pt_gate[c0 % 4], pt_gate[c1 % 4])
                    for c in (c0, c1):
                        act(bfv(o_pt[c % 4], 512), pb[c % 4][:, 0:512],
                            AF.Exp, scale=SCALE, s=se)
                        exp_ts[c] = se.v
                        pbg[c % 4].set(se, se.v)
            av_done = te.v

            def mknorm(_h=h, _G=G, _av=avbank, _done=av_done):
                def emit():
                    wait("vector", te, _done)
                    vop("reciprocal", af.ap[0:1, o_ir1:o_ir1 + 512],
                        pb[6][0:1, 0:512], s=ve)
                    den_gate.set(ve, ve.v)
                    wait("tensor", ve, ve.v)
                    gwait("tensor", pbmg)
                    mm(pbm[:, 0:512], af.ap[0:1, o_onesf:o_onesf + 128],
                       af.ap[0:1, o_ir1:o_ir1 + 512], True, True, s=te)
                    wait("vector", te, te.v)
                    vop("tensor_copy", fv(o_invr, 512), pbm[:, 0:512], s=ve)
                    pbmg.set(ve, ve.v)
                    vop("scalar_tensor_tensor",
                        bfv(o_aoutT + _h * N + _G * 512, 512),
                        _av[:, 0:512], 1.0, fv(o_invr, 512),
                        op0=ALU.mult, op1=ALU.mult, s=ve)
                    av_gate[_G % 2].set(ve, ve.v)
                return emit
            pending_norm[0] = mknorm()
        pending_norm[0]()
        pending_norm[0] = None
        wait("sync", ve, ve.v)
        dma("sync", cc_ao_in[h].ap(), bfv(o_aoutT + h * N, N), s_ao)
        wait("gpsimd", s_ao, s_ao.v)
        coll("AllGather", ALU.bypass, [cc_ao_in[h].ap().opt()],
             [cc_ao_out[h].ap().rearrange("(f g) x -> f (g x)", g=8).opt()],
             cao)
    P2_VE = ve.v
    P2_SE = se.v
    for t in range(4):
        pbg[t].set(se, P2_SE)
    pbg[4].set(ve, P2_VE)
    pbg[5].set(ve, P2_VE)
    pbg[6].set(ve, P2_VE)

    abf.pop()
    abf.pop()

    af.push()
    o_sst = [af.take(512) for _ in range(4)]
    sst_gate = [Gate() for _ in range(4)]
    abf.push()
    o_alh = abf.take(16 * 1024)
    o_wo = [abf.take(16 * 512) for _ in range(2)]
    wo_gate = [Gate(), Gate()]
    wo_ld = {}

    def emit_wo_load(cb):
        b = cb % 2
        gwait("gpsimd", wo_gate[b])
        src = woutFi[:, cb * 512:(cb + 1) * 512] \
            .rearrange("(a p) c -> p a c", p=128)
        dma("gpsimd", bfv(o_wo[b], 16 * 512), src, s_wo[b])
        wo_ld[cb] = s_wo[b].v

    wait("gpsimd", cao, 2)
    for a in range(16):
        def g(e, _a=a):
            with e.register(f"ba{_a}") as r:
                e.reg_mov(r, NCORES * 128 * 8 - 1)
                e.indirect_dma_start(
                    out=bfv(o_alh + _a * 1024, 1024), out_offset=None,
                    in_=cc_ao_out[_a % 2].ap(),
                    in_offset=IndirectOffsetOnAxis(ivv(o_aoix + _a, 1),
                                                   axis=0),
                    bounds_check=r, oob_is_err=False).then_inc(s_alh.h, 16)
        op("gpsimd", g)
        s_alh.inc(16)
    emit_wo_load(0)
    emit_wo_load(1)

    wait("tensor", s_alh, s_alh.v)
    wait("tensor", ve, P2_VE)
    for cb in range(4):
        b = cb % 2
        wait("tensor", s_wo[b], wo_ld[cb])
        for k in range(8):
            bank = (cb * 8 + k) % 4
            gwait("tensor", pbg[bank])
            for a in range(16):
                mm(pb[bank][:, 0:512], bfv(o_alh + a * 1024 + k * 128, 128),
                   bfv(o_wo[b] + a * 512, 512), a == 0, a == 15,
                   s=te if a == 15 else None)
            if k == 7:
                wo_gate[b].set(te, te.v)
            wait("vector", te, te.v)
            gwait("vector", sst_gate[bank])
            vop("tensor_copy", fv(o_sst[bank], 512), pb[bank][:, 0:512], s=ve)
            pbg[bank].set(ve, ve.v)
            wait("sync", ve, ve.v)
            dma("sync", cc_sown.ap()[k * 128:(k + 1) * 128,
                                     cb * 512:(cb + 1) * 512],
                fv(o_sst[bank], 512), s_sst)
            sst_gate[bank].set(s_sst, s_sst.v)
        if cb + 2 < 4:
            emit_wo_load(cb + 2)
    P3_TE = te.v
    abf.pop()
    af.pop()

    abf.push()
    o_w1b = [abf.take(2048) for _ in range(2)]
    o_w2sl = [abf.take(16 * 512) for _ in range(2)]
    o_ht = abf.take(16 * 1024)
    o_yacc = abf.take(8 * 2048)
    o_yst = [abf.take(512) for _ in range(2)]
    w1_gate = [Gate(), Gate()]
    w2_gate = [Gate(), Gate()]
    hps_gate = [Gate(), Gate()]
    hps_gate[0].set(ve, P2_VE)
    hps_gate[1].set(pbmg.sem, pbmg.val)
    ht_gate = Gate()
    yst_gate = [Gate(), Gate()]
    w1_ld = {}
    w2_ld = {}

    def emit_w1_load(q, ht):
        hb = ht % 2
        gwait("gpsimd", w1_gate[hb])
        src = w1i[:, (q * 16 + ht) * 128:(q * 16 + ht + 1) * 128] \
            .rearrange("(a p) h -> p a h", p=128)
        dma("gpsimd", bfv(o_w1b[hb], 2048), src, s_w1[hb])
        w1_ld[(q, ht)] = (hb, s_w1[hb].v)

    def emit_w2_load(q, cb):
        wb = cb % 2
        gwait("sync", w2_gate[wb])
        src = w2i[q * 2048:(q + 1) * 2048, cb * 512:(cb + 1) * 512] \
            .rearrange("(a p) c -> p a c", p=128)
        dma("sync", bfv(o_w2sl[wb], 16 * 512), src, s_w2[wb])
        w2_ld[(q, cb)] = (wb, s_w2[wb].v)

    wait("gpsimd", te, P3_TE)
    emit_w1_load(0, 0)
    emit_w1_load(0, 1)
    for q in range(4):
        gwait("scalar", ht_gate)
        for ht in range(16):
            hb = ht % 2
            _hb, _v = w1_ld[(q, ht)]
            wait("tensor", s_w1[_hb], _v)
            for sb in range(2):
                bank = pb[4] if sb == 0 else pbm
                gwait("tensor", hps_gate[sb])
                for d16 in range(16):
                    mm(bank[:, 0:512], bfv(o_w1b[hb] + d16 * 128, 128),
                       bfv(o_xgt + d16 * 1024 + sb * 512, 512),
                       d16 == 0, d16 == 15, s=te if d16 == 15 else None)
                if sb == 1:
                    w1_gate[hb].set(te, te.v)
                wait("scalar", te, te.v)
                act(bfv(o_ht + ht * 1024 + sb * 512, 512), bank[:, 0:512],
                    AF.Gelu, s=se)
                hps_gate[sb].set(se, se.v)
            nq, nht = (q, ht + 2) if ht + 2 < 16 else (q + 1, ht + 2 - 16)
            if nq < 4:
                emit_w1_load(nq, nht)
        se_h = se.v
        if q == 0:
            emit_w2_load(0, 0)
            emit_w2_load(0, 1)
        for cb in range(4):
            wb = cb % 2
            _wb, _v = w2_ld[(q, cb)]
            wait("tensor", s_w2[_wb], _v)
            wait("tensor", se, se_h)
            for stp in range(2):
                for st in range(4):
                    gwait("tensor", pbg[st])
                for ht2 in range(16):
                    for st in range(4):
                        sg = stp * 4 + st
                        mm(pb[st][:, 0:512],
                           bfv(o_ht + ht2 * 1024 + sg * 128, 128),
                           bfv(o_w2sl[wb] + ht2 * 512, 512),
                           ht2 == 0, ht2 == 15,
                           s=te if (ht2 == 15 and st == 3) else None)
                if cb == 3 and stp == 1:
                    ht_gate.set(te, te.v)
                if stp == 1:
                    w2_gate[wb].set(te, te.v)
                wait("vector", te, te.v)
                for st in range(4):
                    sg = stp * 4 + st
                    ya = bfv(o_yacc + sg * 2048 + cb * 512, 512)
                    if q == 0:
                        vop("tensor_copy", ya, pb[st][:, 0:512],
                            s=ve if st == 3 else None)
                    else:
                        vop("tensor_tensor", ya, pb[st][:, 0:512], ya,
                            op=ALU.add, s=ve if st == 3 else None)
                    if st == 3:
                        for st2 in range(4):
                            pbg[st2].set(ve, ve.v)
            nq, ncb = (q, cb + 2) if cb + 2 < 4 else (q + 1, cb + 2 - 4)
            if nq < 4:
                emit_w2_load(nq, ncb)
    for sg in range(8):
        for cb in range(4):
            b = (sg * 4 + cb) % 2
            gwait("vector", yst_gate[b])
            vop("tensor_scalar", bfv(o_yst[b], 512),
                bfv(o_yacc + sg * 2048 + cb * 512, 512),
                fv(o_ptil + sg, 1), None, op0=ALU.mult, s=ve)
            wait("sync", ve, ve.v)
            dma("sync",
                cc_y_in.ap()[sg * 128:(sg + 1) * 128, cb * 512:(cb + 1) * 512],
                bfv(o_yst[b], 512), s_yst[b])
            yst_gate[b].set(s_yst[b], s_yst[b].v)
    wait("gpsimd", s_yst[0], s_yst[0].v)
    wait("gpsimd", s_yst[1], s_yst[1].v)
    coll("AllGather", ALU.bypass, [cc_y_in.ap().opt()],
         [cc_y_out.ap().opt()], cy)
    abf.pop()
    abf.pop()

    abf.push()
    o_ya = abf.take(2048)
    o_yb = abf.take(2048)
    af.push()
    o_g = af.take(2048)
    o_xt = af.take(2048)
    o_st5 = af.take(2048)

    wait("gpsimd", cy, 1)
    wait("sync", s_sst, s_sst.v)
    vop("memset", bfv(o_ya, 2048), 0.0)
    vop("memset", bfv(o_yb, 2048), 0.0, s=ve)
    for k in range(8):
        wait("gpsimd", ve, ve.v)

        def gath(e, _k=k):
            with e.register(f"by{_k}") as r:
                e.reg_mov(r, NCORES * HALF - 1)
                e.indirect_dma_start(
                    out=bfv(o_ya, 2048), out_offset=None, in_=cc_y_out.ap(),
                    in_offset=IndirectOffsetOnAxis(ivv(o_rowA8 + _k, 1),
                                                   axis=0),
                    bounds_check=r, oob_is_err=False).then_inc(s_y5.h, 16)
                e.indirect_dma_start(
                    out=bfv(o_yb, 2048), out_offset=None, in_=cc_y_out.ap(),
                    in_offset=IndirectOffsetOnAxis(ivv(o_rowB8 + _k, 1),
                                                   axis=0),
                    bounds_check=r, oob_is_err=False).then_inc(s_y5.h, 16)
        op("gpsimd", gath)
        s_y5.inc(32)
        if k >= 1:
            wait("sync", s_out, s_out.v)
        dma("sync", fv(o_xt, 2048), x_sl[k * 128:(k + 1) * 128, :], s_p5x)
        if k >= 1:
            wait("sync", ve, ve.v)
        dma("sync", fv(o_st5, 2048),
            cc_sown.ap()[k * 128:(k + 1) * 128, :], s_p5s)
        wait("vector", s_y5, s_y5.v)
        vop("tensor_add", bfv(o_ya, 2048), bfv(o_ya, 2048), bfv(o_yb, 2048),
            s=ve)
        wait("scalar", ve, ve.v)
        act(fv(o_g, 2048), bfv(o_ya, 2048), AF.Silu, s=se)
        wait("vector", se, se.v)
        wait("vector", s_p5x, s_p5x.v)
        wait("vector", s_p5s, s_p5s.v)
        vop("tensor_mul", fv(o_g, 2048), fv(o_g, 2048), fv(o_st5, 2048),
            s=ve)
        wait("vector", ve, ve.v)
        vop("tensor_add", fv(o_xt, 2048), fv(o_xt, 2048), fv(o_g, 2048), s=ve)
        wait("sync", ve, ve.v)
        dma("sync", out_sl[k * 128:(k + 1) * 128, :], fv(o_xt, 2048), s_out)
        if k < 7:
            vop("memset", bfv(o_ya, 2048), 0.0)
            vop("memset", bfv(o_yb, 2048), 0.0, s=ve)
    if debug:
        s_dbg = sem("s_dbg")
        wait("gpsimd", s_out, s_out.v)
        dma("gpsimd", dbg_o["d_xn"], cc_xn_out.ap(), s_dbg)
        dma("gpsimd", dbg_o["d_lg"], fv(o_logits, 256), s_dbg)
        dma("gpsimd", dbg_o["d_rt"][:, 0:256], fv(o_sel, 256), s_dbg)
        dma("gpsimd", dbg_o["d_rt"][:, 256:512], fv(o_rank, 256), s_dbg)
        dma("gpsimd", dbg_o["d_rt"][:, 512:768], fv(o_p, 256), s_dbg)
        dma("gpsimd", dbg_o["d_rt"][:, 768:832], fv(o_mysel, 64), s_dbg)
        dma("gpsimd", dbg_o["d_rt"][:, 832:896], fv(o_pif, 64), s_dbg)
        dma("gpsimd", dbg_o["d_rt"][:, 896:900], fv(o_th, 4), s_dbg)
        dma("gpsimd", dbg_o["d_rt"][:, 900:904], fv(o_ptil + 0, 4), s_dbg)
        dma("gpsimd", dbg_o["d_rab"], rowab_buf.ap(), s_dbg)
        dma("gpsimd", dbg_o["d_idx"], idxp_buf.ap(), s_dbg)
        dma("gpsimd", dbg_o["d_qkv1"], qkv1_buf.ap(), s_dbg)
        dma("gpsimd", dbg_o["d_s"], cc_sown.ap(), s_dbg)
        dma("gpsimd", dbg_o["d_y"], cc_y_out.ap(), s_dbg)
        wait("gpsimd", s_dbg, s_dbg.v)
    wait("sync", s_out, s_out.v)

    nc._dbg = {k: v for k, v in locals().items()
               if isinstance(v, int) and k.startswith("o_")}
    with nc.Block() as block:
        @block.sync
        def _(e):
            for f in prog["sync"]:
                f(e)

        @block.tensor
        def _(e):
            for f in prog["tensor"]:
                f(e)

        @block.vector
        def _(e):
            for f in prog["vector"]:
                f(e)

        @block.scalar
        def _(e):
            for f in prog["scalar"]:
                f(e)

        @block.gpsimd
        def _(e):
            for f in prog["gpsimd"]:
                f(e)

    return nc


def _host_inputs(inputs):
    x = np.asarray(inputs["L0_attn_x"]).reshape(N, D).astype(np.float32)
    w_qkv = np.asarray(inputs["L0_attn_w_qkv"]).astype(np.float32)
    w_out = np.asarray(inputs["L0_attn_w_out"]).astype(np.float32)
    Wg = np.asarray(inputs["L0_ffn_Wg"]).astype(np.float32)
    bg = np.asarray(inputs["L0_ffn_bg"]).astype(np.float32)
    W1 = np.asarray(inputs["L0_ffn_W1"]).astype(np.float32)
    W2 = np.asarray(inputs["L0_ffn_W2"]).astype(np.float32)

    invf = (np.float32(1.0) / np.float32(10000.0) **
            (np.arange(0, 32, 2, dtype=np.float32) / np.float32(32)))
    freqs = np.arange(N, dtype=np.float32)[:, None] * invf[None, :]
    cos = np.cos(freqs.astype(np.float64)).astype(np.float32)
    sin = np.sin(freqs.astype(np.float64)).astype(np.float32)
    rotA = np.zeros((64, N), np.float32)
    rotB = np.zeros((64, N), np.float32)
    rotA[0:16] = cos.T
    rotA[16:32] = 1.0
    rotA[32:48] = -sin.T
    rotB[0:16] = sin.T
    rotB[32:48] = cos.T
    rotB[48:64] = 1.0

    trimask = np.where(np.triu(np.ones((128, 128), bool), 1),
                       np.float32(-1e30), np.float32(0.0)).astype(np.float32)
    trimT4 = np.zeros((128, 2048), np.float32)
    for t in range(4):
        trimT4[:, t * 512:t * 512 + t * 128] = -1e30
        trimT4[:, t * 512 + t * 128:t * 512 + (t + 1) * 128] = trimask.T

    common = {
        "rotAi": rotA.astype(bfdt),
        "rotBi": rotB.astype(bfdt),
        "trimT4i": trimT4.astype(bfdt),
        "identbfi": np.eye(128).astype(bfdt),
        "identf32i": np.eye(128).astype(np.float32),
        "utrii": (np.arange(128)[:, None] <= np.arange(128)[None, :])
        .astype(bfdt),
        "onesbfi": np.ones((128, 128), bfdt),
        "onesf32i": np.ones((128, 128), np.float32),
        "wgi": Wg,
        "bgi": bg.reshape(1, E),
    }

    in_maps = []
    for c in range(NCORES):
        h0 = HPC * c
        rows = []
        for blk in range(3):
            for hh in range(HPC):
                r = w_qkv[blk * 2048 + (h0 + hh) * DH:
                          blk * 2048 + (h0 + hh + 1) * DH]
                if blk < 2:
                    r = r[PERM]
                rows.append(r)
        wqkvT_c = np.ascontiguousarray(np.concatenate(rows, 0).T).astype(bfdt)

        e = c // 2
        wsel_c = np.zeros((128, E), np.float32)
        wsel_c[:, e] = 1.0
        q = np.arange(128)[:, None]
        kk = np.arange(8)[None, :]
        in_maps.append(dict(common, **{
            "x_sl": np.ascontiguousarray(x[c * TPC:(c + 1) * TPC]),
            "wqkvTi": wqkvT_c,
            "woutFi": np.ascontiguousarray(w_out.T).astype(bfdt),
            "aoixi": (((np.arange(16)[None, :] // 2) * 128 +
                       np.arange(128)[:, None]) * 8 + c).astype(np.int32),
            "w1i": np.ascontiguousarray(W1[e]).astype(bfdt),
            "w2i": np.ascontiguousarray(W2[e]).astype(bfdt),
            "wseli": wsel_c,
            "hbasei": np.full((128, 1), 1024.0 * (c % 2), np.float32),
            "ownpmi": (q * 64 + 8 * c + kk).astype(np.int32),
        }))
    return in_maps


def kernel(**inputs):
    key = ("ncd" if DEBUG else "nc")
    if key not in _CACHE:
        _CACHE[key] = build_program(debug=DEBUG)
    nc = _CACHE[key]
    in_maps = _host_inputs(inputs)
    res = run_bass_kernel_spmd(nc, in_maps, core_ids=list(range(NCORES)))
    if DEBUG:
        _CACHE["dbg_results"] = res.results
    out = np.concatenate([r["out_sl"] for r in res.results], axis=0)
    return out.reshape(1, N, D).astype(np.float32)


# version 12
# speedup vs baseline: 1.1106x; 1.0072x over previous
import numpy as np
import ml_dtypes

import concourse.bass as bass
import concourse.mybir as mybir
from concourse.bass import IndirectOffsetOnAxis
from concourse.bass_utils import run_bass_kernel_spmd

F32 = mybir.dt.float32
BF16 = mybir.dt.bfloat16
I32 = mybir.dt.int32
AF = mybir.ActivationFunctionType
ALU = mybir.AluOpType
AX = mybir.AxisListType
bfdt = ml_dtypes.bfloat16

NCORES = 8
N = 8192
D = 2048
DH = 128
HPC = 2
TPC = N // NCORES
E = 4
CAP = 2048
HALF = CAP // 2
HDIM = 8192
SCALE = float(DH) ** -0.5
BIG = 99999.0
NBLK = 16
NTILE = 64

PERM = np.concatenate([np.arange(0, 16), np.arange(32, 48),
                       np.arange(16, 32), np.arange(48, 128)])

_CACHE = {}
DEBUG = False


class Sem:
    def __init__(self, h):
        self.h = h
        self.v = 0

    def inc(self, n=1):
        self.v += n
        return self.v


class Gate:
    def __init__(self):
        self.sem = None
        self.val = 0

    def set(self, sem, val):
        self.sem = sem
        self.val = val


class Arena:
    def __init__(self, ap, cols, name):
        self.ap = ap
        self.cols = cols
        self.off = 0
        self.marks = []
        self.name = name
        self.peak = 0

    def take(self, w, align=16):
        if self.off % align:
            self.off += align - (self.off % align)
        o = self.off
        self.off += w
        self.peak = max(self.peak, self.off)
        assert self.off <= self.cols, f"{self.name} overflow {self.off}>{self.cols}"
        return o

    def push(self):
        self.marks.append(self.off)

    def pop(self):
        self.off = self.marks.pop()


def build_program(debug=False):
    nc = bass.Bass("TRN2", target_bir_lowering=False)

    def ext(name, shape, dt):
        return nc.dram_tensor(name, shape, dt, kind="ExternalInput").ap()

    x_sl = ext("x_sl", [TPC, D], F32)
    wqkvTi = ext("wqkvTi", [D, 768], BF16)
    woutFi = ext("woutFi", [D, D], BF16)
    aoixi = ext("aoixi", [128, 16], I32)
    wgi = ext("wgi", [D, E], F32)
    bgi = ext("bgi", [1, E], F32)
    w1i = ext("w1i", [D, HDIM], BF16)
    w2i = ext("w2i", [HDIM, D], BF16)
    rotAi = ext("rotAi", [64, N], BF16)
    rotBi = ext("rotBi", [64, N], BF16)
    trimT4i = ext("trimT4i", [128, 2048], BF16)
    identbfi = ext("identbfi", [128, 128], BF16)
    identf32i = ext("identf32i", [128, 128], F32)
    utrii = ext("utrii", [128, 128], BF16)
    onesbfi = ext("onesbfi", [128, 128], BF16)
    onesf32i = ext("onesf32i", [128, 128], F32)
    wseli = ext("wseli", [128, E], F32)
    hbasei = ext("hbasei", [128, 1], F32)
    ownpmi = ext("ownpmi", [128, 8], I32)

    out_sl = nc.dram_tensor("out_sl", [TPC, D], F32, kind="ExternalOutput").ap()

    cc_xn_in = nc.dram_tensor("cc_xn_in", [TPC, D], BF16)
    cc_xn_out = nc.dram_tensor("cc_xn_out", [N, D], BF16)
    cc_ao_in = [nc.dram_tensor(f"cc_ao_in{h}", [128, N], BF16)
                for h in range(2)]
    cc_ao_out = [nc.dram_tensor(f"cc_ao_out{h}", [NCORES * 128 * 8, N // 8],
                                BF16) for h in range(2)]
    cc_sown = nc.dram_tensor("cc_sown", [TPC, D], F32)
    cc_y_in = nc.dram_tensor("cc_y_in", [HALF, D], BF16)
    cc_y_out = nc.dram_tensor("cc_y_out", [NCORES * HALF, D], BF16)
    idxp_buf = nc.dram_tensor("idxp_buf", [HALF, 2], F32)
    rowab_buf = nc.dram_tensor("rowab_buf", [N, 2], F32)
    qkv1_buf = nc.dram_tensor("qkv1_buf", [128, 3 * N], BF16)
    if debug:
        dbg_o = {
            "d_xn": nc.dram_tensor("d_xn", [N, D], BF16,
                                   kind="ExternalOutput").ap(),
            "d_lg": nc.dram_tensor("d_lg", [128, 256], F32,
                                   kind="ExternalOutput").ap(),
            "d_rt": nc.dram_tensor("d_rt", [128, 1024], F32,
                                   kind="ExternalOutput").ap(),
            "d_rab": nc.dram_tensor("d_rab", [N, 2], F32,
                                    kind="ExternalOutput").ap(),
            "d_idx": nc.dram_tensor("d_idx", [HALF, 2], F32,
                                    kind="ExternalOutput").ap(),

            "d_qkv1": nc.dram_tensor("d_qkv1", [128, 3 * N], BF16,
                                     kind="ExternalOutput").ap(),
            "d_s": nc.dram_tensor("d_s", [1024, D], F32,
                                  kind="ExternalOutput").ap(),
            "d_y": nc.dram_tensor("d_y", [NCORES * HALF, D], BF16,
                                  kind="ExternalOutput").ap(),
        }

    import contextlib
    es = contextlib.ExitStack()
    nc._keepalive = es

    prog = {k: [] for k in ("sync", "tensor", "vector", "scalar", "gpsimd")}

    def op(eng, fn):
        prog[eng].append(fn)

    def sem(name):
        return Sem(es.enter_context(nc.semaphore(name)))

    te = sem("te")
    ve = sem("ve")
    se = sem("sescal")
    geio = sem("geio")
    cxn = sem("cxn")
    cy = sem("cy")
    s_const = sem("s_const")
    s_lnx = sem("s_lnx")
    s_xni = sem("s_xni")
    s_w0 = sem("s_w0")
    s_xnt = [sem("s_xnt0"), sem("s_xnt1")]
    s_rot = [sem("s_rot0"), sem("s_rot1")]
    s_st = [sem("s_st0"), sem("s_st1"), sem("s_st2")]
    s_rel = sem("s_rel")
    s_ao = sem("s_ao")
    cao = sem("cao")
    s_alh = sem("s_alh")
    s_wo = [sem("s_wo0"), sem("s_wo1")]
    s_sst = sem("s_sst")
    s_rab = sem("s_rab")
    s_scat = sem("s_scat")
    s_idx = sem("s_idx")
    s_rabg = sem("s_rabg")
    s_xg = [sem("s_xg0"), sem("s_xg1")]
    s_wout = sem("s_wout")
    s_w1 = [sem("s_w10"), sem("s_w11")]
    s_w2 = [sem("s_w20"), sem("s_w21")]
    s_yst = [sem("s_yst0"), sem("s_yst1")]
    s_y5 = sem("s_y5")
    s_p5x = sem("s_p5x")
    s_p5s = sem("s_p5s")
    s_out = sem("s_out")

    def wait(eng, s, v):
        if v <= 0:
            return
        op(eng, lambda e, _h=s.h, _v=v: e.wait_ge(_h, _v))

    def gwait(eng, gate):
        if gate.sem is not None and gate.val > 0:
            wait(eng, gate.sem, gate.val)

    def dma(eng, out, in_, s, transpose=False):
        def f(e, _o=out, _i=in_, _h=s.h, _t=transpose):
            if _t:
                e.dma_start(out=_o, in_=_i, transpose=True).then_inc(_h, 16)
            else:
                e.dma_start(out=_o, in_=_i).then_inc(_h, 16)
        op(eng, f)
        return s.inc(16)

    def mm(out, lhsT, rhs, start, stop, s=None):
        def f(e, _o=out, _l=lhsT, _r=rhs, _a=start, _b=stop, _s=s):
            i = e.matmul(_o, _l, _r, start=_a, stop=_b)
            if _s is not None:
                i.then_inc(_s.h, 1)
        op("tensor", f)
        return s.inc() if s is not None else None

    def petr(out, in_, ident, s=None):
        def f(e, _o=out, _i=in_, _d=ident, _s=s):
            i = e.transpose(_o, _i, _d)
            if _s is not None:
                i.then_inc(_s.h, 1)
        op("tensor", f)
        return s.inc() if s is not None else None

    EDGE_ALL = [False]

    def vop(meth, *args, s=None, eng="vector", **kw):
        if EDGE_ALL[0] and eng == "vector" and s is None:
            s = ve

        def f(e, _m=meth, _a=args, _k=kw, _s=s):
            i = getattr(e, _m)(*_a, **_k)
            if _s is not None:
                i.then_inc(_s.h, 1)
        op(eng, f)
        r = s.inc() if s is not None else None
        if EDGE_ALL[0] and eng == "vector":
            wait("vector", ve, ve.v)
        return r

    def act(out, in_, func, bias=0.0, scale=1.0, accum=None, s=None):
        def f(e, _o=out, _i=in_, _f=func, _b=bias, _c=scale, _ac=accum, _s=s):
            if _ac is not None:
                i = e.activation(_o, _i, _f, bias=_b, scale=_c, accum_out=_ac)
            else:
                i = e.activation(_o, _i, _f, bias=_b, scale=_c)
            if _s is not None:
                i.then_inc(_s.h, 1)
        op("scalar", f)
        return s.inc() if s is not None else None

    def coll(kind, alu, ins, outs, s):
        def f(e, _k=kind, _a=alu, _i=ins, _o=outs, _h=s.h):
            e.collective_compute(_k, _a, replica_groups=[list(range(NCORES))],
                                 ins=_i, outs=_o).then_inc(_h, 1)
        op("gpsimd", f)
        return s.inc()

    ABF, AFC, AIC = 81056, 12048, 608
    abf_t = nc.alloc_sbuf_tensor("abf", [128, ABF], BF16)
    af_t = nc.alloc_sbuf_tensor("af32", [128, AFC], F32)
    ai_t = nc.alloc_sbuf_tensor("ai32", [128, AIC], I32)
    abf = Arena(abf_t.ap(), ABF, "bf")
    af = Arena(af_t.ap(), AFC, "f32")
    ai = Arena(ai_t.ap(), AIC, "i32")

    def bfv(o, w):
        return abf.ap[:, o:o + w]

    def fv(o, w):
        return af.ap[:, o:o + w]

    def ivv(o, w):
        return ai.ap[:, o:o + w]

    pb = [es.enter_context(nc.psum_tensor(f"pb{i}", [128, 512], F32))
          for i in range(7)]
    pbm = es.enter_context(nc.psum_tensor("pbm", [128, 512], F32))
    pbg = [Gate() for _ in range(7)]
    pbmg = Gate()

    o_identbf = abf.take(128)
    o_utri = abf.take(128)
    o_onesbf = abf.take(128)
    o_selbf = abf.take(64)
    o_identf = af.take(128)
    o_onesf = af.take(128)
    o_wsel = af.take(E)
    o_hbase = af.take(1)
    o_iota = af.take(64)
    o_logits = af.take(256)
    o_t1 = af.take(64)
    o_cnt = af.take(256)
    o_p = af.take(256)
    o_score = af.take(256)
    o_eqm = af.take(256)
    o_rank = af.take(256)
    o_sel = af.take(256)
    o_r1 = af.take(256)
    o_r2 = af.take(256)
    o_cnt4 = af.take(4)
    o_lo = af.take(4)
    o_hi = af.take(4)
    o_mid = af.take(4)
    o_bc = af.take(4)
    o_ge4 = af.take(4)
    o_th = af.take(4)
    o_thbc = af.take(4)
    o_rr = af.take(4)
    o_colsum = af.take(64)
    o_colp = af.take(64)
    o_cps = af.take(64)
    o_capc = af.take(1)
    o_myrec = af.take(128)
    o_rowa = af.take(64)
    o_rowb = af.take(64)
    o_rowi = af.take(128)
    o_mysel = af.take(64)
    o_myrank = af.take(64)
    o_myp = af.take(64)
    o_pif = af.take(64)
    o_rab = af.take(16)
    o_rab2 = af.take(16)
    o_ptil = af.take(8)
    o_invr = af.take(512)
    o_ir1 = af.take(512)
    o_eps = af.take(1)
    o_ss4 = af.take(4)
    o_ss64 = af.take(64)
    o_ownpm = ai.take(8)
    o_aoix = ai.take(16)
    o_pi = ai.take(64)
    o_iadj = ai.take(64)
    o_ish = ai.take(64)
    o_tok = ai.take(8)
    o_rowA8 = ai.take(8)
    o_rowB8 = ai.take(8)
    o_im = ai.take(64)

    abf.push()
    o_xg = abf.take(2 * 2048)
    o_xgt = abf.take(16 * 1024)
    abf.push()
    o_qt = abf.take(N)
    o_kt = abf.take(N)
    o_vt = abf.take(N)
    abf.push()
    o_wqkvT = abf.take(16 * 768)
    o_wg = abf.take(16 * E)
    o_bg = abf.take(E)
    o_xnt = [abf.take(16 * 512) for _ in range(2)]
    o_xnstage = abf.take(2048)
    o_rot = [abf.take(1024) for _ in range(2)]
    o_rsc = abf.take(1024)
    o_qkv1st = abf.take(3 * 512)
    af.push()
    o_xsb = af.take(2048)
    o_stat = af.take(8)

    dma("sync", bfv(o_identbf, 128), identbfi, s_const)
    dma("sync", bfv(o_utri, 128), utrii, s_const)
    dma("sync", bfv(o_onesbf, 128), onesbfi, s_const)
    dma("sync", fv(o_identf, 128), identf32i, s_const)
    dma("sync", fv(o_onesf, 128), onesf32i, s_const)
    dma("sync", fv(o_wsel, E), wseli, s_const)
    dma("sync", fv(o_hbase, 1), hbasei, s_const)
    dma("sync", ivv(o_ownpm, 8), ownpmi, s_const)
    dma("sync", ivv(o_aoix, 16), aoixi, s_const)

    dma("gpsimd", bfv(o_wqkvT, 16 * 768),
        wqkvTi.rearrange("(a p) f -> p a f", p=128), s_w0)
    dma("gpsimd", bfv(o_wg, 16 * E),
        wgi.rearrange("(a p) e -> p a e", p=128), s_w0)
    dma("gpsimd", abf.ap[0:1, o_bg:o_bg + E], bgi, s_w0)
    vop("iota", ivv(o_pi, 64), pattern=[[128, 64]], base=0,
        channel_multiplier=1, eng="gpsimd", s=geio)

    wait("vector", geio, 1)
    vop("memset", fv(o_eps, 1), 1e-05)
    vop("tensor_copy", fv(o_iota, 64), ivv(o_pi, 64), s=ve)

    for k in range(8):
        if k >= 1:
            wait("sync", ve, ln_last_ve)
        dma("sync", fv(o_xsb, 2048), x_sl[k * 128:(k + 1) * 128, :], s_lnx)
        wait("vector", s_lnx, s_lnx.v)
        vop("reduce_sum", fv(o_stat, 1), fv(o_xsb, 2048), axis=AX.X, s=ve)
        wait("vector", ve, ve.v)
        vop("tensor_scalar", fv(o_stat + 1, 1), fv(o_stat, 1), 1.0 / D, None,
            op0=ALU.mult, s=ve)
        wait("vector", ve, ve.v)
        vop("tensor_scalar", fv(o_xsb, 2048), fv(o_xsb, 2048),
            fv(o_stat + 1, 1), None, op0=ALU.subtract, s=ve)
        wait("scalar", ve, ve.v)
        if k >= 1:
            wait("scalar", s_xni, ln_last_gq)
        act(bfv(o_xnstage, 2048), fv(o_xsb, 2048), AF.Square,
            accum=fv(o_stat + 2, 1), s=se)
        wait("scalar", se, se.v)
        act(fv(o_stat + 3, 1), fv(o_stat + 2, 1), AF.Ln, bias=fv(o_eps, 1),
            scale=1.0 / D, s=se)
        wait("scalar", se, se.v)
        act(fv(o_stat + 4, 1), fv(o_stat + 3, 1), AF.Exp, scale=-0.5, s=se)
        wait("vector", se, se.v)
        vop("tensor_scalar", bfv(o_xnstage, 2048), fv(o_xsb, 2048),
            fv(o_stat + 4, 1), None, op0=ALU.mult, s=ve)
        ln_last_ve = ve.v
        wait("gpsimd", ve, ve.v)
        dma("gpsimd", cc_xn_in.ap()[k * 128:(k + 1) * 128, :],
            bfv(o_xnstage, 2048), s_xni)
        ln_last_gq = s_xni.v
    wait("gpsimd", s_xni, s_xni.v)
    coll("AllGather", ALU.bypass, [cc_xn_in.ap().opt()],
         [cc_xn_out.ap().opt()], cxn)

    xnt_gate = [Gate(), Gate()]
    rot_gate = [Gate(), Gate()]
    st_gate = [Gate(), Gate(), Gate()]
    xnt_ld = {}
    rot_ld = {}

    def emit_xnt_load(j):
        buf = j % 2
        gwait("scalar", xnt_gate[buf])
        gwait("sync", xnt_gate[buf])
        for d16 in range(16):
            src = cc_xn_out.ap()[j * 512:(j + 1) * 512,
                                 d16 * 128:(d16 + 1) * 128]
            dma("scalar" if d16 % 2 else "sync",
                bfv(o_xnt[buf] + d16 * 512, 512), src, s_xnt[buf],
                transpose=True)
        xnt_ld[j] = s_xnt[buf].v

    def emit_rot_load(j):
        buf = j % 2
        gwait("sync", rot_gate[buf])
        dma("sync", abf.ap[0:64, o_rot[buf]:o_rot[buf] + 512],
            rotAi[:, j * 512:(j + 1) * 512], s_rot[buf])
        dma("sync", abf.ap[0:64, o_rot[buf] + 512:o_rot[buf] + 1024],
            rotBi[:, j * 512:(j + 1) * 512], s_rot[buf])
        rot_ld[j] = s_rot[buf].v

    wait("sync", cxn, 1)
    wait("scalar", cxn, 1)
    emit_xnt_load(0)
    emit_rot_load(0)
    emit_xnt_load(1)
    emit_rot_load(1)

    wait("tensor", s_w0, s_w0.v)
    wait("tensor", s_const, s_const.v)
    for j in range(NBLK):
        buf = j % 2
        wait("tensor", s_xnt[buf], xnt_ld[j])
        for F in range(4):
            gwait("tensor", pbg[F])
            for d16 in range(16):
                mm(pb[F][:, 0:512], bfv(o_wqkvT + d16 * 768 + F * 128, 128),
                   bfv(o_xnt[buf] + d16 * 512, 512),
                   d16 == 0, d16 == 15, s=te if d16 == 15 else None)
        gwait("tensor", pbmg)
        vpair_te = {}
        for pair in range(2):
            gwait("tensor", pbg[4])
            for tt in (2 * pair, 2 * pair + 1):
                colo = (tt % 2) * 256
                for d16 in range(16):
                    lhs = bfv(o_xnt[buf] + d16 * 512 + tt * 128, 128)
                    mm(pb[4][:, colo:colo + 256], lhs,
                       bfv(o_wqkvT + d16 * 768 + 512, 256),
                       d16 == 0, d16 == 15, s=te if d16 == 15 else None)
                    mm(pbm[:, tt * 4:tt * 4 + 4], lhs, bfv(o_wg + d16 * 4, 4),
                       d16 == 0, False)
                mm(pbm[:, tt * 4:tt * 4 + 4],
                   abf.ap[0:1, o_onesbf:o_onesbf + 128],
                   abf.ap[0:1, o_bg:o_bg + E], False, True,
                   s=te if tt == 3 else None)
            vpair_te[pair] = te.v
            wait("vector", te, te.v)
            for tt in (2 * pair, 2 * pair + 1):
                colo = (tt % 2) * 256
                vop("tensor_copy", bfv(o_vt + (j * 4 + tt) * 128, 128),
                    pb[4][:, colo:colo + 128])
                gwait("vector", st_gate[2])
                vop("tensor_copy", bfv(o_qkv1st + 1024 + tt * 128, 128),
                    pb[4][:, colo + 128:colo + 256], s=ve)
            pbg[4].set(ve, ve.v)
        te_blk = te.v
        xnt_gate[buf].set(te, te_blk)
        if j + 2 < NBLK:
            emit_xnt_load(j + 2)

        wait("vector", te, te_blk)
        wait("vector", s_rot[buf], rot_ld[j])
        for F in range(4):
            h = F % 2
            if h == 0:
                dsto = (o_qt if F == 0 else o_kt) + j * 512
            else:
                dsto = o_qkv1st + (F // 2) * 512
                gwait("vector", st_gate[F // 2])
            vop("tensor_copy", bfv(dsto, 512), pb[F][:, 0:512],
                s=ve if F == 3 else None)
        for F in range(4):
            pbg[F].set(ve, ve.v)
        for F in range(4):
            h = F % 2
            dsto = ((o_qt if F == 0 else o_kt) + j * 512) if h == 0 \
                else (o_qkv1st + (F // 2) * 512)
            in0 = abf.ap[0:32, dsto:dsto + 512]
            in32 = abf.ap[32:64, dsto:dsto + 512]
            cA = abf.ap[0:32, o_rot[buf]:o_rot[buf] + 512]
            sA = abf.ap[32:64, o_rot[buf]:o_rot[buf] + 512]
            sB = abf.ap[0:32, o_rot[buf] + 512:o_rot[buf] + 1024]
            cB = abf.ap[32:64, o_rot[buf] + 512:o_rot[buf] + 1024]
            t1 = abf.ap[0:32, o_rsc:o_rsc + 512]
            t2 = abf.ap[0:32, o_rsc + 512:o_rsc + 1024]
            t3 = abf.ap[32:64, o_rsc:o_rsc + 512]
            t4 = abf.ap[32:64, o_rsc + 512:o_rsc + 1024]
            vop("tensor_mul", t1, in0, cA)
            vop("tensor_mul", t2, in32, sA)
            vop("tensor_mul", t3, in32, cB)
            vop("tensor_mul", t4, in0, sB, s=ve)
            wait("vector", ve, ve.v)
            vop("tensor_add", in0, t1, t2)
            vop("tensor_add", in32, t3, t4, s=ve)
        rot_gate[buf].set(ve, ve.v)
        if j + 2 < NBLK:
            emit_rot_load(j + 2)
        vop("tensor_copy", fv(o_logits + j * 16, 16), pbm[:, 0:16], s=ve)
        pbmg.set(ve, ve.v)
        wait("sync", ve, ve.v)
        dma("sync", qkv1_buf.ap()[:, j * 512:(j + 1) * 512],
            bfv(o_qkv1st, 512), s_st[0])
        st_gate[0].set(s_st[0], s_st[0].v)
        dma("sync", qkv1_buf.ap()[:, N + j * 512:N + (j + 1) * 512],
            bfv(o_qkv1st + 512, 512), s_st[1])
        st_gate[1].set(s_st[1], s_st[1].v)
        dma("sync", qkv1_buf.ap()[:, 2 * N + j * 512:2 * N + (j + 1) * 512],
            bfv(o_qkv1st + 1024, 512), s_st[2])
        st_gate[2].set(s_st[2], s_st[2].v)
    P1_VE = ve.v
    P1_TE = te.v
    abf.pop()
    af.pop()

    EDGE_ALL[0] = True

    def eview(base, e):
        return af.ap[:, base:base + 256].rearrange("p (i e) -> p e i",
                                                   e=E)[:, e, :]

    vop("tensor_tensor", fv(o_t1, 64), eview(o_logits, 0),
        eview(o_logits, 1), op=ALU.max)
    vop("tensor_tensor", fv(o_r1, 64), eview(o_logits, 2),
        eview(o_logits, 3), op=ALU.max)
    vop("tensor_tensor", fv(o_t1, 64), fv(o_t1, 64), fv(o_r1, 64), op=ALU.max)
    for e in range(E):
        ce = eview(o_cnt, e)
        vop("memset", ce, 0.0)
        for ep in range(E):
            if ep == e:
                continue
            opk = ALU.is_gt if ep > e else ALU.is_ge
            vop("tensor_tensor", fv(o_r1, 64), eview(o_logits, ep),
                eview(o_logits, e), op=opk)
            vop("tensor_add", ce, ce, fv(o_r1, 64))
    for e in range(E):
        vop("tensor_tensor", eview(o_r1, e), eview(o_logits, e),
            fv(o_t1, 64), op=ALU.subtract, s=ve if e == E - 1 else None)
    wait("scalar", ve, ve.v)
    act(fv(o_p, 256), fv(o_r1, 256), AF.Exp, s=se)
    wait("vector", se, se.v)
    vop("tensor_scalar", fv(o_r2, 256), fv(o_cnt, 256), 1.5, None,
        op0=ALU.is_le)
    vop("tensor_mul", fv(o_p, 256), fv(o_p, 256), fv(o_r2, 256))
    vop("tensor_tensor", fv(o_t1, 64), eview(o_p, 0), eview(o_p, 1),
        op=ALU.add)
    vop("tensor_tensor", fv(o_r1, 64), eview(o_p, 2), eview(o_p, 3),
        op=ALU.add)
    vop("tensor_add", fv(o_t1, 64), fv(o_t1, 64), fv(o_r1, 64))
    vop("reciprocal", fv(o_r1, 64), fv(o_t1, 64), s=ve)
    wait("vector", ve, ve.v)
    for e in range(E):
        vop("tensor_tensor", eview(o_p, e), eview(o_p, e), fv(o_r1, 64),
            op=ALU.mult)
    vop("tensor_scalar", fv(o_score, 256), fv(o_r2, 256), 1.0, None,
        op0=ALU.subtract)
    vop("tensor_mul", fv(o_r2, 256), fv(o_p, 256), fv(o_r2, 256))
    vop("tensor_add", fv(o_score, 256), fv(o_score, 256), fv(o_r2, 256))
    vop("memset", fv(o_lo, 4), -2.0)
    vop("memset", fv(o_hi, 4), 2.0)
    vop("memset", fv(o_capc, 1), float(CAP) - 0.5, s=ve)

    def count4(cmp_op, scal_off, iota_mode=False):
        for e in range(E):
            tgt = eview(o_r1, e)
            if iota_mode:
                vop("tensor_scalar", tgt, fv(o_iota, 64),
                    fv(scal_off + e, 1), None, op0=cmp_op)
                vop("tensor_mul", tgt, tgt, eview(o_eqm, e))
            else:
                vop("tensor_scalar", tgt, eview(o_score, e),
                    fv(scal_off + e, 1), None, op0=cmp_op)
            vop("reduce_sum", fv(o_cnt4 + e, 1), tgt, axis=AX.X, s=ve)
            wait("vector", ve, ve.v)
        wait("tensor", ve, ve.v)
        gwait("tensor", pbmg)
        mm(pbm[:, 0:4], fv(o_onesf, 128), fv(o_cnt4, 4), True, True, s=te)
        wait("vector", te, te.v)
        vop("tensor_copy", fv(o_bc, 4), pbm[:, 0:4], s=ve)
        pbmg.set(ve, ve.v)

    for it in range(34):
        vop("tensor_tensor", fv(o_mid, 4), fv(o_lo, 4), fv(o_hi, 4),
            op=ALU.add)
        vop("tensor_scalar", fv(o_mid, 4), fv(o_mid, 4), 0.5, None,
            op0=ALU.mult, s=ve)
        count4(ALU.is_gt, o_mid)
        vop("tensor_scalar", fv(o_ge4, 4), fv(o_bc, 4), fv(o_capc, 1),
            None, op0=ALU.is_ge)
        vop("tensor_copy", ivv(o_im, 4), fv(o_ge4, 4))
        vop("select", fv(o_ss4, 4), ivv(o_im, 4), fv(o_mid, 4), fv(o_lo, 4))
        vop("tensor_copy", fv(o_lo, 4), fv(o_ss4, 4))
        vop("select", fv(o_ss4, 4), ivv(o_im, 4), fv(o_hi, 4), fv(o_mid, 4))
        vop("tensor_copy", fv(o_hi, 4), fv(o_ss4, 4), s=ve)
    vop("tensor_copy", fv(o_th, 4), fv(o_hi, 4), s=ve)
    count4(ALU.is_gt, o_th)
    vop("tensor_scalar", fv(o_rr, 4), fv(o_bc, 4), -1.0, float(CAP),
        op0=ALU.mult, op1=ALU.add, s=ve)
    for e in range(E):
        vop("tensor_scalar", eview(o_eqm, e), eview(o_score, e),
            fv(o_th + e, 1), None, op0=ALU.is_equal)
    vop("memset", fv(o_lo, 4), -1.0)
    vop("memset", fv(o_hi, 4), float(N), s=ve)
    for it in range(16):
        vop("tensor_tensor", fv(o_mid, 4), fv(o_lo, 4), fv(o_hi, 4),
            op=ALU.add)
        vop("tensor_scalar", fv(o_mid, 4), fv(o_mid, 4), 0.5, None,
            op0=ALU.mult, s=ve)
        count4(ALU.is_le, o_mid, iota_mode=True)
        vop("tensor_tensor", fv(o_ge4, 4), fv(o_bc, 4), fv(o_rr, 4),
            op=ALU.is_ge)
        vop("tensor_copy", ivv(o_im, 4), fv(o_ge4, 4))
        vop("select", fv(o_ss4, 4), ivv(o_im, 4), fv(o_mid, 4), fv(o_hi, 4))
        vop("tensor_copy", fv(o_hi, 4), fv(o_ss4, 4))
        vop("select", fv(o_ss4, 4), ivv(o_im, 4), fv(o_lo, 4), fv(o_mid, 4))
        vop("tensor_copy", fv(o_lo, 4), fv(o_ss4, 4), s=ve)
    for e in range(E):
        vop("tensor_scalar", eview(o_r1, e), eview(o_score, e),
            fv(o_th + e, 1), None, op0=ALU.is_gt)
        vop("tensor_scalar", eview(o_r2, e), fv(o_iota, 64),
            fv(o_hi + e, 1), None, op0=ALU.is_le)
        vop("tensor_mul", eview(o_r2, e), eview(o_r2, e), eview(o_eqm, e))
        vop("tensor_tensor", eview(o_sel, e), eview(o_r1, e), eview(o_r2, e),
            op=ALU.add)
    vop("tensor_mul", fv(o_p, 256), fv(o_p, 256), fv(o_sel, 256), s=ve)

    for e in range(E):
        vop("tensor_copy", bfv(o_selbf, 64), eview(o_sel, e), s=ve)
        wait("tensor", ve, ve.v)
        gwait("tensor", pbmg)
        mm(pbm[:, 64:128], bfv(o_utri, 128), bfv(o_selbf, 64), True, True)
        mm(pbm[0:1, 192:256], abf.ap[:, o_onesbf:o_onesbf + 1],
           bfv(o_selbf, 64), True, True, s=te)
        wait("vector", te, te.v)
        vop("tensor_copy", af.ap[0:1, o_colsum:o_colsum + 64],
            pbm[0:1, 192:256])
        vop("tensor_copy", af.ap[0:1, o_colp:o_colp + 64], pbm[0:1, 192:256])
        cur, oth = o_colp, o_cps
        sh = 1
        while sh < 64:
            vop("tensor_copy", af.ap[0:1, oth:oth + sh],
                af.ap[0:1, cur:cur + sh])
            vop("tensor_tensor", af.ap[0:1, oth + sh:oth + 64],
                af.ap[0:1, cur + sh:cur + 64],
                af.ap[0:1, cur:cur + 64 - sh], op=ALU.add)
            cur, oth = oth, cur
            sh *= 2
        vop("tensor_tensor", af.ap[0:1, o_colp:o_colp + 64],
            af.ap[0:1, cur:cur + 64],
            af.ap[0:1, o_colsum:o_colsum + 64], op=ALU.subtract, s=ve)
        wait("tensor", ve, ve.v)
        mm(pbm[:, 128:192], af.ap[0:1, o_onesf:o_onesf + 128],
           af.ap[0:1, o_colp:o_colp + 64], True, True, s=te)
        wait("vector", te, te.v)
        vop("tensor_copy", fv(o_r1, 64), pbm[:, 128:192])
        vop("tensor_tensor", eview(o_rank, e), pbm[:, 64:128],
            eview(o_sel, e), op=ALU.subtract)
        vop("tensor_tensor", eview(o_rank, e), eview(o_rank, e),
            fv(o_r1, 64), op=ALU.add, s=ve)
        pbmg.set(ve, ve.v)

    for src_b, dst_o in ((o_sel, o_mysel), (o_rank, o_myrank), (o_p, o_myp)):
        vop("memset", fv(dst_o, 64), 0.0)
        for e in range(E):
            vop("tensor_scalar", fv(o_r1, 64), eview(src_b, e),
                fv(o_wsel + e, 1), None, op0=ALU.mult)
            vop("tensor_add", fv(dst_o, 64), fv(dst_o, 64), fv(o_r1, 64))
    vop("tensor_scalar", fv(o_r1, 64), fv(o_myrank, 64), fv(o_hbase, 1),
        None, op0=ALU.subtract)
    vop("tensor_scalar", fv(o_r2, 64), fv(o_r1, 64), -0.5, None, op0=ALU.is_ge)
    vop("tensor_mul", fv(o_r2, 64), fv(o_r2, 64), fv(o_mysel, 64))
    vop("tensor_scalar", fv(o_mysel, 64), fv(o_r1, 64), float(HALF) - 0.5,
        None, op0=ALU.is_le)
    vop("tensor_mul", fv(o_r2, 64), fv(o_r2, 64), fv(o_mysel, 64))
    vop("tensor_copy", ivv(o_iadj, 64), fv(o_r1, 64))
    vop("memset", ivv(o_pi, 64), 127)
    vop("tensor_tensor", ivv(o_pi, 64), ivv(o_iadj, 64), ivv(o_pi, 64),
        op=ALU.bitwise_and)
    vop("memset", ivv(o_ish, 64), 3)
    vop("tensor_tensor", ivv(o_pi, 64), ivv(o_pi, 64), ivv(o_ish, 64),
        op=ALU.logical_shift_left)
    vop("memset", ivv(o_ish, 64), 7)
    vop("tensor_tensor", ivv(o_ish, 64), ivv(o_iadj, 64), ivv(o_ish, 64),
        op=ALU.logical_shift_right)
    vop("tensor_tensor", ivv(o_pi, 64), ivv(o_pi, 64), ivv(o_ish, 64),
        op=ALU.add)
    vop("tensor_copy", fv(o_pif, 64), ivv(o_pi, 64))
    vop("tensor_mul", fv(o_pif, 64), fv(o_pif, 64), fv(o_r2, 64))
    vop("tensor_scalar", fv(o_r1, 64), fv(o_r2, 64), -BIG, BIG, op0=ALU.mult,
        op1=ALU.add)
    vop("tensor_add", fv(o_pif, 64), fv(o_pif, 64), fv(o_r1, 64))
    vop("tensor_copy", ivv(o_pi, 64), fv(o_pif, 64))
    rec = af.ap[:, o_myrec:o_myrec + 128].rearrange("p (i z) -> p z i", z=2)
    vop("tensor_copy", rec[:, 0, :], fv(o_iota, 64))
    vop("tensor_copy", rec[:, 1, :], fv(o_myp, 64), s=ve)
    vop("memset", fv(o_rowa, 64), BIG)
    vop("memset", fv(o_rowb, 64), BIG)
    for e in range(E - 1, -1, -1):
        vop("tensor_scalar", fv(o_r1, 64), eview(o_rank, e), 1.0,
            float(CAP * e), op0=ALU.mult, op1=ALU.add)
        vop("tensor_copy", ivv(o_im, 64), eview(o_sel, e))
        vop("select", fv(o_ss64, 64), ivv(o_im, 64), fv(o_r1, 64),
            fv(o_rowa, 64))
        vop("tensor_copy", fv(o_rowa, 64), fv(o_ss64, 64))
    for e in range(E):
        vop("tensor_scalar", fv(o_r1, 64), eview(o_rank, e), 1.0,
            float(CAP * e), op0=ALU.mult, op1=ALU.add)
        vop("tensor_copy", ivv(o_im, 64), eview(o_sel, e))
        vop("select", fv(o_ss64, 64), ivv(o_im, 64), fv(o_r1, 64),
            fv(o_rowb, 64))
        vop("tensor_copy", fv(o_rowb, 64), fv(o_ss64, 64))
    vop("tensor_tensor", fv(o_r1, 64), eview(o_sel, 0), eview(o_sel, 1),
        op=ALU.add)
    vop("tensor_tensor", fv(o_r2, 64), eview(o_sel, 2), eview(o_sel, 3),
        op=ALU.add)
    vop("tensor_add", fv(o_r1, 64), fv(o_r1, 64), fv(o_r2, 64))
    vop("tensor_scalar", fv(o_r2, 64), fv(o_r1, 64), 1.5, None, op0=ALU.is_ge)
    vop("tensor_copy", ivv(o_im, 64), fv(o_r2, 64))
    vop("memset", fv(o_r1, 64), BIG)
    vop("select", fv(o_ss64, 64), ivv(o_im, 64), fv(o_rowb, 64), fv(o_r1, 64))
    vop("tensor_copy", fv(o_rowb, 64), fv(o_ss64, 64))
    rowi = af.ap[:, o_rowi:o_rowi + 128].rearrange("p (i z) -> p z i", z=2)
    vop("tensor_copy", rowi[:, 0, :], fv(o_rowa, 64))
    vop("tensor_copy", rowi[:, 1, :], fv(o_rowb, 64), s=ve)
    EDGE_ALL[0] = False
    VE_ROUTE = ve.v

    wait("gpsimd", ve, VE_ROUTE)
    dma("gpsimd", rowab_buf.ap().rearrange("(p i) z -> p (i z)", p=128),
        fv(o_rowi, 128), s_rab)

    def scat(e):
        with e.register("b1023") as r:
            e.reg_mov(r, HALF - 1)
            for i in range(64):
                e.indirect_dma_start(
                    out=idxp_buf.ap(),
                    out_offset=IndirectOffsetOnAxis(ivv(o_pi + i, 1), axis=0),
                    in_=af.ap[:, o_myrec + i * 2:o_myrec + i * 2 + 2],
                    in_offset=None, bounds_check=r,
                    oob_is_err=False).then_inc(s_scat.h, 16)
    op("gpsimd", scat)
    s_scat.inc(16 * 64)
    wait("gpsimd", s_scat, s_scat.v)
    dma("gpsimd", fv(o_rab, 16),
        idxp_buf.ap().rearrange("(p k) z -> p (k z)", p=128), s_idx)

    def rowabg(e):
        with e.register("brb") as r:
            e.reg_mov(r, N - 1)
            for k in range(8):
                e.indirect_dma_start(
                    out=af.ap[:, o_rab2 + k * 2:o_rab2 + k * 2 + 2],
                    out_offset=None, in_=rowab_buf.ap(),
                    in_offset=IndirectOffsetOnAxis(ivv(o_ownpm + k, 1), axis=0),
                    bounds_check=r, oob_is_err=False).then_inc(s_rabg.h, 16)
    op("gpsimd", rowabg)
    s_rabg.inc(16 * 8)
    wait("gpsimd", s_rab, s_rab.v)
    EDGE_ALL[0] = True
    wait("vector", s_idx, s_idx.v)
    wait("vector", s_rabg, s_rabg.v)
    rab = af.ap[:, o_rab:o_rab + 16].rearrange("p (k z) -> p z k", z=2)
    vop("tensor_copy", ivv(o_tok, 8), rab[:, 0, :])
    vop("tensor_copy", fv(o_ptil, 8), rab[:, 1, :])
    rab2 = af.ap[:, o_rab2:o_rab2 + 16].rearrange("p (k z) -> p z k", z=2)
    vop("tensor_copy", ivv(o_rowA8, 8), rab2[:, 0, :])
    vop("tensor_copy", ivv(o_rowB8, 8), rab2[:, 1, :], s=ve)
    EDGE_ALL[0] = False
    VE_TOK = ve.v

    wait("gpsimd", ve, VE_TOK)
    xg_gate = [Gate(), Gate()]
    xgs_gate = [Gate(), Gate()]
    gq_xg = {}
    o_xgf = af.take(256)
    for k in range(8):
        gwait("gpsimd", xg_gate[k % 2])

        def gat(e, _k=k):
            with e.register(f"bx{_k}") as r:
                e.reg_mov(r, N - 1)
                e.indirect_dma_start(
                    out=bfv(o_xg + (_k % 2) * 2048, 2048), out_offset=None,
                    in_=cc_xn_out.ap(),
                    in_offset=IndirectOffsetOnAxis(ivv(o_tok + _k, 1), axis=0),
                    bounds_check=r, oob_is_err=False).then_inc(
                        s_xg[_k % 2].h, 16)
        op("gpsimd", gat)
        gq_xg[k] = s_xg[k % 2].inc(16)
        wait("vector", s_xg[k % 2], gq_xg[k])
        for d16 in range(16):
            sl = d16 % 2
            gwait("vector", xgs_gate[sl])
            vop("tensor_copy", fv(o_xgf + sl * 128, 128),
                bfv(o_xg + (k % 2) * 2048 + d16 * 128, 128), s=ve)
            if d16 == 15:
                xg_gate[k % 2].set(ve, ve.v)
            wait("tensor", ve, ve.v)
            gwait("tensor", pbg[sl])
            petr(pb[sl][:, 0:128], fv(o_xgf + sl * 128, 128),
                 fv(o_identf, 128), s=te)
            xgs_gate[sl].set(te, te.v)
            wait("vector", te, te.v)
            vop("tensor_copy", bfv(o_xgt + d16 * 1024 + k * 128, 128),
                pb[sl][:, 0:128], s=ve)
            pbg[sl].set(ve, ve.v)
    VE_XGT = ve.v

    abf.push()
    o_pt = [abf.take(512) for _ in range(4)]
    o_pts = [abf.take(512) for _ in range(2)]
    o_drow = abf.take(N)
    o_dmul = abf.take(1024)
    o_negbf = abf.take(128)
    o_aoutT = abf.take(HPC * N)
    o_trimt4 = abf.take(2048)
    o_bias2 = af.take(512)
    s_tm4 = sem("s_tm4")
    dma("sync", bfv(o_trimt4, 2048), trimT4i, s_tm4)
    wait("vector", s_tm4, s_tm4.v)
    pt_gate = [Gate() for _ in range(4)]
    av_gate = [Gate(), Gate()]
    av_gate[0].set(pbg[4].sem, pbg[4].val)
    av_gate[1].set(pbg[5].sem, pbg[5].val)
    den_gate = Gate()
    pts_gate = [Gate(), Gate()]
    dmg = [Gate(), Gate()]
    pending_norm = [None]

    def gwait2(eng, ga, gb):
        if ga.sem is not None and gb.sem is not None and ga.sem is gb.sem:
            wait(eng, ga.sem, max(ga.val, gb.val))
        else:
            gwait(eng, ga)
            gwait(eng, gb)

    vop("memset", abf.ap[0:1, o_negbf:o_negbf + 128], -1.0, s=ve)
    wait("tensor", ve, ve.v)

    for h in range(HPC):
        if h == 1:
            wait("sync", te, te.v)
            for z in range(3):
                wait("sync", s_st[z], s_st[z].v)
            for z, dst in enumerate((o_qt, o_kt, o_vt)):
                dma("sync", bfv(dst, N), qkv1_buf.ap()[:, z * N:(z + 1) * N],
                    s_rel)
            wait("tensor", s_rel, s_rel.v)
            wait("vector", s_rel, s_rel.v)
        for GG in range(16):
            gwait("vector", dmg[GG % 2])
            vop("tensor_tensor", bfv(o_dmul + (GG % 2) * 512, 512),
                bfv(o_qt + GG * 512, 512), bfv(o_kt + GG * 512, 512),
                op=ALU.mult, s=ve)
            wait("tensor", ve, ve.v)
            gwait("tensor", pbmg)
            mm(pbm[0:1, 0:512], abf.ap[:, o_onesbf:o_onesbf + 1],
               bfv(o_dmul + (GG % 2) * 512, 512), True, True, s=te)
            dmg[GG % 2].set(te, te.v)
            wait("vector", te, te.v)
            vop("tensor_copy", abf.ap[0:1, o_drow + GG * 512:
                                      o_drow + (GG + 1) * 512],
                pbm[0:1, 0:512], s=ve)
            pbmg.set(ve, ve.v)

        for G in range(16):
            nkj = 4 * G + 4
            P = nkj // 2
            avbank = pb[4 + (G % 2)]
            dend = pb[6]
            gwait("tensor", pbmg)
            wait("tensor", ve, ve.v)
            mm(pbm[:, 0:512], abf.ap[0:1, o_negbf:o_negbf + 128],
               abf.ap[0:1, o_drow + G * 512:o_drow + (G + 1) * 512],
               True, True, s=te)
            wait("vector", te, te.v)
            vop("tensor_copy", fv(o_bias2, 512), pbm[:, 0:512], s=ve)
            pbmg.set(ve, ve.v)
            gwait("tensor", den_gate)
            gwait("tensor", av_gate[G % 2])
            smm_ts = {}
            exp_ts = {}
            pts_ts = {}
            for pj in range(P + 2):
                if pj < P:
                    c0, c1 = 2 * pj, 2 * pj + 1
                    gwait2("tensor", pbg[c0 % 4], pbg[c1 % 4])
                    for c in (c0, c1):
                        smm_ts[c] = mm(pb[c % 4][:, 0:512],
                                       bfv(o_kt + c * 128, 128),
                                       bfv(o_qt + G * 512, 512), True, True,
                                       s=te)
                if pj == 1 and pending_norm[0] is not None:
                    pending_norm[0]()
                    pending_norm[0] = None
                if 1 <= pj <= P:
                    p0, p1 = 2 * pj - 2, 2 * pj - 1
                    wait("tensor", se, exp_ts[p1])
                    mm(avbank[:, 0:512], bfv(o_vt + p0 * 128, 128),
                       bfv(o_pt[p0 % 4], 512), p0 == 0, False)
                    mm(avbank[:, 0:512], bfv(o_vt + p1 * 128, 128),
                       bfv(o_pt[p1 % 4], 512), False, p1 == nkj - 1, s=te)
                    pt_gate[p0 % 4].set(te, te.v)
                    pt_gate[p1 % 4].set(te, te.v)
                if pj >= 2:
                    q0 = 2 * pj - 4
                    wait("tensor", ve, pts_ts[pj - 2])
                    mm(dend[0:1, 0:512], abf.ap[:, o_onesbf:o_onesbf + 1],
                       bfv(o_pts[(pj - 2) % 2], 512),
                       q0 == 0, q0 + 1 == nkj - 1, s=te)
                    pts_gate[(pj - 2) % 2].set(te, te.v)
                if pj < P:
                    wait("vector", te, smm_ts[c1])
                    for c in (c0, c1):
                        sl = ve if c == c1 and c < 4 * G else None
                        vop("tensor_tensor", pb[c % 4][:, 0:512],
                            pb[c % 4][:, 0:512], fv(o_bias2, 512), op=ALU.add,
                            s=sl)
                        if c >= 4 * G:
                            vop("tensor_tensor", pb[c % 4][:, 0:512],
                                pb[c % 4][:, 0:512],
                                bfv(o_trimt4 + (c - 4 * G) * 512, 512),
                                op=ALU.add, s=ve if c == c1 else None)
                    add_inc = ve.v
                if 1 <= pj <= P:
                    wait("vector", se, exp_ts[2 * pj - 1])
                    gwait("vector", pts_gate[(pj - 1) % 2])
                    vop("tensor_tensor", bfv(o_pts[(pj - 1) % 2], 512),
                        bfv(o_pt[(2 * pj - 2) % 4], 512),
                        bfv(o_pt[(2 * pj - 1) % 4], 512), op=ALU.add, s=ve)
                    pts_ts[pj - 1] = ve.v
                if pj < P:
                    wait("scalar", ve, add_inc)
                    gwait2("scalar", pt_gate[c0 % 4], pt_gate[c1 % 4])
                    for c in (c0, c1):
                        act(bfv(o_pt[c % 4], 512), pb[c % 4][:, 0:512],
                            AF.Exp, scale=SCALE, s=se)
                        exp_ts[c] = se.v
                        pbg[c % 4].set(se, se.v)
            av_done = te.v

            def mknorm(_h=h, _G=G, _av=avbank, _done=av_done):
                def emit():
                    wait("vector", te, _done)
                    vop("reciprocal", af.ap[0:1, o_ir1:o_ir1 + 512],
                        pb[6][0:1, 0:512], s=ve)
                    den_gate.set(ve, ve.v)
                    wait("tensor", ve, ve.v)
                    gwait("tensor", pbmg)
                    mm(pbm[:, 0:512], af.ap[0:1, o_onesf:o_onesf + 128],
                       af.ap[0:1, o_ir1:o_ir1 + 512], True, True, s=te)
                    wait("vector", te, te.v)
                    vop("tensor_copy", fv(o_invr, 512), pbm[:, 0:512], s=ve)
                    pbmg.set(ve, ve.v)
                    vop("scalar_tensor_tensor",
                        bfv(o_aoutT + _h * N + _G * 512, 512),
                        _av[:, 0:512], 1.0, fv(o_invr, 512),
                        op0=ALU.mult, op1=ALU.mult, s=ve)
                    av_gate[_G % 2].set(ve, ve.v)
                return emit
            pending_norm[0] = mknorm()
        pending_norm[0]()
        pending_norm[0] = None
        wait("sync", ve, ve.v)
        dma("sync", cc_ao_in[h].ap(), bfv(o_aoutT + h * N, N), s_ao)
        wait("gpsimd", s_ao, s_ao.v)
        coll("AllGather", ALU.bypass, [cc_ao_in[h].ap().opt()],
             [cc_ao_out[h].ap().rearrange("(f g) x -> f (g x)", g=8).opt()],
             cao)
    P2_VE = ve.v
    P2_SE = se.v
    for t in range(4):
        pbg[t].set(se, P2_SE)
    pbg[4].set(ve, P2_VE)
    pbg[5].set(ve, P2_VE)
    pbg[6].set(ve, P2_VE)

    abf.pop()
    abf.pop()

    af.push()
    o_sst = [af.take(512) for _ in range(4)]
    sst_gate = [Gate() for _ in range(4)]
    abf.push()
    o_alh = abf.take(16 * 1024)
    o_wo = [abf.take(16 * 512) for _ in range(2)]
    wo_gate = [Gate(), Gate()]
    wo_ld = {}

    def emit_wo_load(cb):
        b = cb % 2
        gwait("gpsimd", wo_gate[b])
        src = woutFi[:, cb * 512:(cb + 1) * 512] \
            .rearrange("(a p) c -> p a c", p=128)
        dma("gpsimd", bfv(o_wo[b], 16 * 512), src, s_wo[b])
        wo_ld[cb] = s_wo[b].v

    wait("gpsimd", cao, 2)
    for a in range(16):
        def g(e, _a=a):
            with e.register(f"ba{_a}") as r:
                e.reg_mov(r, NCORES * 128 * 8 - 1)
                e.indirect_dma_start(
                    out=bfv(o_alh + _a * 1024, 1024), out_offset=None,
                    in_=cc_ao_out[_a % 2].ap(),
                    in_offset=IndirectOffsetOnAxis(ivv(o_aoix + _a, 1),
                                                   axis=0),
                    bounds_check=r, oob_is_err=False).then_inc(s_alh.h, 16)
        op("gpsimd", g)
        s_alh.inc(16)
    emit_wo_load(0)
    emit_wo_load(1)

    wait("tensor", s_alh, s_alh.v)
    wait("tensor", ve, P2_VE)
    for cb in range(4):
        b = cb % 2
        wait("tensor", s_wo[b], wo_ld[cb])
        for k in range(8):
            bank = (cb * 8 + k) % 4
            gwait("tensor", pbg[bank])
            for a in range(16):
                mm(pb[bank][:, 0:512], bfv(o_alh + a * 1024 + k * 128, 128),
                   bfv(o_wo[b] + a * 512, 512), a == 0, a == 15,
                   s=te if a == 15 else None)
            if k == 7:
                wo_gate[b].set(te, te.v)
            wait("vector", te, te.v)
            gwait("vector", sst_gate[bank])
            vop("tensor_copy", fv(o_sst[bank], 512), pb[bank][:, 0:512], s=ve)
            pbg[bank].set(ve, ve.v)
            wait("sync", ve, ve.v)
            dma("sync", cc_sown.ap()[k * 128:(k + 1) * 128,
                                     cb * 512:(cb + 1) * 512],
                fv(o_sst[bank], 512), s_sst)
            sst_gate[bank].set(s_sst, s_sst.v)
        if cb + 2 < 4:
            emit_wo_load(cb + 2)
    P3_TE = te.v
    abf.pop()
    af.pop()

    abf.push()
    o_w1b = [abf.take(2048) for _ in range(2)]
    o_w2sl = [abf.take(16 * 512) for _ in range(2)]
    o_ht = abf.take(16 * 1024)
    o_yacc = abf.take(8 * 2048)
    o_yst = [abf.take(512) for _ in range(2)]
    w1_gate = [Gate(), Gate()]
    w2_gate = [Gate(), Gate()]
    hps_gate = [Gate(), Gate()]
    hps_gate[0].set(ve, P2_VE)
    hps_gate[1].set(pbmg.sem, pbmg.val)
    ht_gate = Gate()
    yst_gate = [Gate(), Gate()]
    w1_ld = {}
    w2_ld = {}

    def emit_w1_load(q, ht):
        hb = ht % 2
        gwait("gpsimd", w1_gate[hb])
        src = w1i[:, (q * 16 + ht) * 128:(q * 16 + ht + 1) * 128] \
            .rearrange("(a p) h -> p a h", p=128)
        dma("gpsimd", bfv(o_w1b[hb], 2048), src, s_w1[hb])
        w1_ld[(q, ht)] = (hb, s_w1[hb].v)

    def emit_w2_load(q, cb):
        wb = cb % 2
        gwait("sync", w2_gate[wb])
        src = w2i[q * 2048:(q + 1) * 2048, cb * 512:(cb + 1) * 512] \
            .rearrange("(a p) c -> p a c", p=128)
        dma("sync", bfv(o_w2sl[wb], 16 * 512), src, s_w2[wb])
        w2_ld[(q, cb)] = (wb, s_w2[wb].v)

    wait("gpsimd", te, P3_TE)
    emit_w1_load(0, 0)
    emit_w1_load(0, 1)
    for q in range(4):
        gwait("scalar", ht_gate)
        for ht in range(16):
            hb = ht % 2
            _hb, _v = w1_ld[(q, ht)]
            wait("tensor", s_w1[_hb], _v)
            for sb in range(2):
                bank = pb[4] if sb == 0 else pbm
                gwait("tensor", hps_gate[sb])
                for d16 in range(16):
                    mm(bank[:, 0:512], bfv(o_w1b[hb] + d16 * 128, 128),
                       bfv(o_xgt + d16 * 1024 + sb * 512, 512),
                       d16 == 0, d16 == 15, s=te if d16 == 15 else None)
                if sb == 1:
                    w1_gate[hb].set(te, te.v)
                wait("scalar", te, te.v)
                act(bfv(o_ht + ht * 1024 + sb * 512, 512), bank[:, 0:512],
                    AF.Gelu, s=se)
                hps_gate[sb].set(se, se.v)
            nq, nht = (q, ht + 2) if ht + 2 < 16 else (q + 1, ht + 2 - 16)
            if nq < 4:
                emit_w1_load(nq, nht)
        se_h = se.v
        if q == 0:
            emit_w2_load(0, 0)
            emit_w2_load(0, 1)
        for cb in range(4):
            wb = cb % 2
            _wb, _v = w2_ld[(q, cb)]
            wait("tensor", s_w2[_wb], _v)
            wait("tensor", se, se_h)
            for stp in range(2):
                for st in range(4):
                    gwait("tensor", pbg[st])
                for ht2 in range(16):
                    for st in range(4):
                        sg = stp * 4 + st
                        mm(pb[st][:, 0:512],
                           bfv(o_ht + ht2 * 1024 + sg * 128, 128),
                           bfv(o_w2sl[wb] + ht2 * 512, 512),
                           ht2 == 0, ht2 == 15,
                           s=te if (ht2 == 15 and st == 3) else None)
                if cb == 3 and stp == 1:
                    ht_gate.set(te, te.v)
                if stp == 1:
                    w2_gate[wb].set(te, te.v)
                wait("vector", te, te.v)
                for st in range(4):
                    sg = stp * 4 + st
                    ya = bfv(o_yacc + sg * 2048 + cb * 512, 512)
                    if q == 0:
                        vop("tensor_copy", ya, pb[st][:, 0:512],
                            s=ve if st == 3 else None)
                    else:
                        vop("tensor_tensor", ya, pb[st][:, 0:512], ya,
                            op=ALU.add, s=ve if st == 3 else None)
                    if st == 3:
                        for st2 in range(4):
                            pbg[st2].set(ve, ve.v)
            nq, ncb = (q, cb + 2) if cb + 2 < 4 else (q + 1, cb + 2 - 4)
            if nq < 4:
                emit_w2_load(nq, ncb)
    for sg in range(8):
        for cb in range(4):
            b = (sg * 4 + cb) % 2
            gwait("vector", yst_gate[b])
            vop("tensor_scalar", bfv(o_yst[b], 512),
                bfv(o_yacc + sg * 2048 + cb * 512, 512),
                fv(o_ptil + sg, 1), None, op0=ALU.mult, s=ve)
            wait("sync", ve, ve.v)
            dma("sync",
                cc_y_in.ap()[sg * 128:(sg + 1) * 128, cb * 512:(cb + 1) * 512],
                bfv(o_yst[b], 512), s_yst[b])
            yst_gate[b].set(s_yst[b], s_yst[b].v)
    wait("gpsimd", s_yst[0], s_yst[0].v)
    wait("gpsimd", s_yst[1], s_yst[1].v)
    coll("AllGather", ALU.bypass, [cc_y_in.ap().opt()],
         [cc_y_out.ap().opt()], cy)
    abf.pop()
    abf.pop()

    abf.push()
    o_ya = abf.take(2048)
    o_yb = abf.take(2048)
    af.push()
    o_g = af.take(2048)
    o_xt = af.take(2048)
    o_st5 = af.take(2048)

    wait("gpsimd", cy, 1)
    wait("sync", s_sst, s_sst.v)
    vop("memset", bfv(o_ya, 2048), 0.0)
    vop("memset", bfv(o_yb, 2048), 0.0, s=ve)
    for k in range(8):
        wait("gpsimd", ve, ve.v)

        def gath(e, _k=k):
            with e.register(f"by{_k}") as r:
                e.reg_mov(r, NCORES * HALF - 1)
                e.indirect_dma_start(
                    out=bfv(o_ya, 2048), out_offset=None, in_=cc_y_out.ap(),
                    in_offset=IndirectOffsetOnAxis(ivv(o_rowA8 + _k, 1),
                                                   axis=0),
                    bounds_check=r, oob_is_err=False).then_inc(s_y5.h, 16)
                e.indirect_dma_start(
                    out=bfv(o_yb, 2048), out_offset=None, in_=cc_y_out.ap(),
                    in_offset=IndirectOffsetOnAxis(ivv(o_rowB8 + _k, 1),
                                                   axis=0),
                    bounds_check=r, oob_is_err=False).then_inc(s_y5.h, 16)
        op("gpsimd", gath)
        s_y5.inc(32)
        if k >= 1:
            wait("sync", s_out, s_out.v)
        dma("sync", fv(o_xt, 2048), x_sl[k * 128:(k + 1) * 128, :], s_p5x)
        if k >= 1:
            wait("sync", ve, ve.v)
        dma("sync", fv(o_st5, 2048),
            cc_sown.ap()[k * 128:(k + 1) * 128, :], s_p5s)
        wait("vector", s_y5, s_y5.v)
        vop("tensor_add", bfv(o_ya, 2048), bfv(o_ya, 2048), bfv(o_yb, 2048),
            s=ve)
        wait("scalar", ve, ve.v)
        act(fv(o_g, 2048), bfv(o_ya, 2048), AF.Silu, s=se)
        wait("vector", se, se.v)
        wait("vector", s_p5x, s_p5x.v)
        wait("vector", s_p5s, s_p5s.v)
        vop("tensor_mul", fv(o_g, 2048), fv(o_g, 2048), fv(o_st5, 2048),
            s=ve)
        wait("vector", ve, ve.v)
        vop("tensor_add", fv(o_xt, 2048), fv(o_xt, 2048), fv(o_g, 2048), s=ve)
        wait("sync", ve, ve.v)
        dma("sync", out_sl[k * 128:(k + 1) * 128, :], fv(o_xt, 2048), s_out)
        if k < 7:
            vop("memset", bfv(o_ya, 2048), 0.0)
            vop("memset", bfv(o_yb, 2048), 0.0, s=ve)
    if debug:
        s_dbg = sem("s_dbg")
        wait("gpsimd", s_out, s_out.v)
        dma("gpsimd", dbg_o["d_xn"], cc_xn_out.ap(), s_dbg)
        dma("gpsimd", dbg_o["d_lg"], fv(o_logits, 256), s_dbg)
        dma("gpsimd", dbg_o["d_rt"][:, 0:256], fv(o_sel, 256), s_dbg)
        dma("gpsimd", dbg_o["d_rt"][:, 256:512], fv(o_rank, 256), s_dbg)
        dma("gpsimd", dbg_o["d_rt"][:, 512:768], fv(o_p, 256), s_dbg)
        dma("gpsimd", dbg_o["d_rt"][:, 768:832], fv(o_mysel, 64), s_dbg)
        dma("gpsimd", dbg_o["d_rt"][:, 832:896], fv(o_pif, 64), s_dbg)
        dma("gpsimd", dbg_o["d_rt"][:, 896:900], fv(o_th, 4), s_dbg)
        dma("gpsimd", dbg_o["d_rt"][:, 900:904], fv(o_ptil + 0, 4), s_dbg)
        dma("gpsimd", dbg_o["d_rab"], rowab_buf.ap(), s_dbg)
        dma("gpsimd", dbg_o["d_idx"], idxp_buf.ap(), s_dbg)
        dma("gpsimd", dbg_o["d_qkv1"], qkv1_buf.ap(), s_dbg)
        dma("gpsimd", dbg_o["d_s"], cc_sown.ap(), s_dbg)
        dma("gpsimd", dbg_o["d_y"], cc_y_out.ap(), s_dbg)
        wait("gpsimd", s_dbg, s_dbg.v)
    wait("sync", s_out, s_out.v)

    nc._dbg = {k: v for k, v in locals().items()
               if isinstance(v, int) and k.startswith("o_")}
    with nc.Block() as block:
        @block.sync
        def _(e):
            for f in prog["sync"]:
                f(e)

        @block.tensor
        def _(e):
            for f in prog["tensor"]:
                f(e)

        @block.vector
        def _(e):
            for f in prog["vector"]:
                f(e)

        @block.scalar
        def _(e):
            for f in prog["scalar"]:
                f(e)

        @block.gpsimd
        def _(e):
            for f in prog["gpsimd"]:
                f(e)

    return nc


def _host_inputs(inputs):
    x = np.asarray(inputs["L0_attn_x"]).reshape(N, D).astype(np.float32)
    w_qkv = np.asarray(inputs["L0_attn_w_qkv"]).astype(np.float32)
    w_out = np.asarray(inputs["L0_attn_w_out"]).astype(np.float32)
    Wg = np.asarray(inputs["L0_ffn_Wg"]).astype(np.float32)
    bg = np.asarray(inputs["L0_ffn_bg"]).astype(np.float32)
    W1 = np.asarray(inputs["L0_ffn_W1"]).astype(np.float32)
    W2 = np.asarray(inputs["L0_ffn_W2"]).astype(np.float32)

    invf = (np.float32(1.0) / np.float32(10000.0) **
            (np.arange(0, 32, 2, dtype=np.float32) / np.float32(32)))
    freqs = np.arange(N, dtype=np.float32)[:, None] * invf[None, :]
    cos = np.cos(freqs.astype(np.float64)).astype(np.float32)
    sin = np.sin(freqs.astype(np.float64)).astype(np.float32)
    rotA = np.zeros((64, N), np.float32)
    rotB = np.zeros((64, N), np.float32)
    rotA[0:16] = cos.T
    rotA[16:32] = 1.0
    rotA[32:48] = -sin.T
    rotB[0:16] = sin.T
    rotB[32:48] = cos.T
    rotB[48:64] = 1.0

    trimask = np.where(np.triu(np.ones((128, 128), bool), 1),
                       np.float32(-1e30), np.float32(0.0)).astype(np.float32)
    trimT4 = np.zeros((128, 2048), np.float32)
    for t in range(4):
        trimT4[:, t * 512:t * 512 + t * 128] = -1e30
        trimT4[:, t * 512 + t * 128:t * 512 + (t + 1) * 128] = trimask.T

    common = {
        "rotAi": rotA.astype(bfdt),
        "rotBi": rotB.astype(bfdt),
        "trimT4i": trimT4.astype(bfdt),
        "identbfi": np.eye(128).astype(bfdt),
        "identf32i": np.eye(128).astype(np.float32),
        "utrii": (np.arange(128)[:, None] <= np.arange(128)[None, :])
        .astype(bfdt),
        "onesbfi": np.ones((128, 128), bfdt),
        "onesf32i": np.ones((128, 128), np.float32),
        "wgi": Wg,
        "bgi": bg.reshape(1, E),
    }

    in_maps = []
    for c in range(NCORES):
        h0 = HPC * c
        rows = []
        for blk in range(3):
            for hh in range(HPC):
                r = w_qkv[blk * 2048 + (h0 + hh) * DH:
                          blk * 2048 + (h0 + hh + 1) * DH]
                if blk < 2:
                    r = r[PERM]
                rows.append(r)
        wqkvT_c = np.ascontiguousarray(np.concatenate(rows, 0).T).astype(bfdt)

        e = c // 2
        wsel_c = np.zeros((128, E), np.float32)
        wsel_c[:, e] = 1.0
        q = np.arange(128)[:, None]
        kk = np.arange(8)[None, :]
        in_maps.append(dict(common, **{
            "x_sl": np.ascontiguousarray(x[c * TPC:(c + 1) * TPC]),
            "wqkvTi": wqkvT_c,
            "woutFi": np.ascontiguousarray(w_out.T).astype(bfdt),
            "aoixi": (((np.arange(16)[None, :] // 2) * 128 +
                       np.arange(128)[:, None]) * 8 + c).astype(np.int32),
            "w1i": np.ascontiguousarray(W1[e]).astype(bfdt),
            "w2i": np.ascontiguousarray(W2[e]).astype(bfdt),
            "wseli": wsel_c,
            "hbasei": np.full((128, 1), 1024.0 * (c % 2), np.float32),
            "ownpmi": (q * 64 + 8 * c + kk).astype(np.int32),
        }))
    return in_maps


def kernel(**inputs):
    key = ("ncd" if DEBUG else "nc")
    if key not in _CACHE:
        _CACHE[key] = build_program(debug=DEBUG)
    nc = _CACHE[key]
    in_maps = _host_inputs(inputs)
    res = run_bass_kernel_spmd(nc, in_maps, core_ids=list(range(NCORES)))
    if DEBUG:
        _CACHE["dbg_results"] = res.results
    out = np.concatenate([r["out_sl"] for r in res.results], axis=0)
    return out.reshape(1, N, D).astype(np.float32)


# version 13 (final)
# speedup vs baseline: 1.1115x; 1.0008x over previous
import numpy as np
import ml_dtypes

import concourse.bass as bass
import concourse.mybir as mybir
from concourse.bass import IndirectOffsetOnAxis
from concourse.bass_utils import run_bass_kernel_spmd

F32 = mybir.dt.float32
BF16 = mybir.dt.bfloat16
I32 = mybir.dt.int32
AF = mybir.ActivationFunctionType
ALU = mybir.AluOpType
AX = mybir.AxisListType
bfdt = ml_dtypes.bfloat16

NCORES = 8
N = 8192
D = 2048
DH = 128
HPC = 2
TPC = N // NCORES
E = 4
CAP = 2048
HALF = CAP // 2
HDIM = 8192
SCALE = float(DH) ** -0.5
BIG = 99999.0
NBLK = 16
NTILE = 64

PERM = np.concatenate([np.arange(0, 16), np.arange(32, 48),
                       np.arange(16, 32), np.arange(48, 128)])

_CACHE = {}
DEBUG = False


class Sem:
    def __init__(self, h):
        self.h = h
        self.v = 0

    def inc(self, n=1):
        self.v += n
        return self.v


class Gate:
    def __init__(self):
        self.sem = None
        self.val = 0

    def set(self, sem, val):
        self.sem = sem
        self.val = val


class Arena:
    def __init__(self, ap, cols, name):
        self.ap = ap
        self.cols = cols
        self.off = 0
        self.marks = []
        self.name = name
        self.peak = 0

    def take(self, w, align=16):
        if self.off % align:
            self.off += align - (self.off % align)
        o = self.off
        self.off += w
        self.peak = max(self.peak, self.off)
        assert self.off <= self.cols, f"{self.name} overflow {self.off}>{self.cols}"
        return o

    def push(self):
        self.marks.append(self.off)

    def pop(self):
        self.off = self.marks.pop()


def build_program(debug=False):
    nc = bass.Bass("TRN2", target_bir_lowering=False)

    def ext(name, shape, dt):
        return nc.dram_tensor(name, shape, dt, kind="ExternalInput").ap()

    x_sl = ext("x_sl", [TPC, D], F32)
    wqkvTi = ext("wqkvTi", [D, 768], BF16)
    woutFi = ext("woutFi", [D, D], BF16)
    aoixi = ext("aoixi", [128, 16], I32)
    wgi = ext("wgi", [D, E], F32)
    bgi = ext("bgi", [1, E], F32)
    w1i = ext("w1i", [D, HDIM], BF16)
    w2i = ext("w2i", [HDIM, D], BF16)
    rotAi = ext("rotAi", [64, N], BF16)
    rotBi = ext("rotBi", [64, N], BF16)
    trimT4i = ext("trimT4i", [128, 2048], BF16)
    identbfi = ext("identbfi", [128, 128], BF16)
    identf32i = ext("identf32i", [128, 128], F32)
    utrii = ext("utrii", [128, 128], BF16)
    onesbfi = ext("onesbfi", [128, 128], BF16)
    onesf32i = ext("onesf32i", [128, 128], F32)
    wseli = ext("wseli", [128, E], F32)
    hbasei = ext("hbasei", [128, 1], F32)
    ownpmi = ext("ownpmi", [128, 8], I32)

    out_sl = nc.dram_tensor("out_sl", [TPC, D], F32, kind="ExternalOutput").ap()

    cc_xn_in = nc.dram_tensor("cc_xn_in", [TPC, D], BF16)
    cc_xn_out = nc.dram_tensor("cc_xn_out", [N, D], BF16)
    cc_ao_in = [nc.dram_tensor(f"cc_ao_in{h}", [128, N], BF16)
                for h in range(2)]
    cc_ao_out = [nc.dram_tensor(f"cc_ao_out{h}", [NCORES * 128 * 8, N // 8],
                                BF16) for h in range(2)]
    cc_sown = nc.dram_tensor("cc_sown", [TPC, D], F32)
    cc_y_in = nc.dram_tensor("cc_y_in", [HALF, D], BF16)
    cc_y_out = nc.dram_tensor("cc_y_out", [NCORES * HALF, D], BF16)
    idxp_buf = nc.dram_tensor("idxp_buf", [HALF, 2], F32)
    rowab_buf = nc.dram_tensor("rowab_buf", [N, 2], F32)
    qkv1_buf = nc.dram_tensor("qkv1_buf", [128, 3 * N], BF16)
    if debug:
        dbg_o = {
            "d_xn": nc.dram_tensor("d_xn", [N, D], BF16,
                                   kind="ExternalOutput").ap(),
            "d_lg": nc.dram_tensor("d_lg", [128, 256], F32,
                                   kind="ExternalOutput").ap(),
            "d_rt": nc.dram_tensor("d_rt", [128, 1024], F32,
                                   kind="ExternalOutput").ap(),
            "d_rab": nc.dram_tensor("d_rab", [N, 2], F32,
                                    kind="ExternalOutput").ap(),
            "d_idx": nc.dram_tensor("d_idx", [HALF, 2], F32,
                                    kind="ExternalOutput").ap(),

            "d_qkv1": nc.dram_tensor("d_qkv1", [128, 3 * N], BF16,
                                     kind="ExternalOutput").ap(),
            "d_s": nc.dram_tensor("d_s", [1024, D], F32,
                                  kind="ExternalOutput").ap(),
            "d_y": nc.dram_tensor("d_y", [NCORES * HALF, D], BF16,
                                  kind="ExternalOutput").ap(),
        }

    import contextlib
    es = contextlib.ExitStack()
    nc._keepalive = es

    prog = {k: [] for k in ("sync", "tensor", "vector", "scalar", "gpsimd")}

    def op(eng, fn):
        prog[eng].append(fn)

    def sem(name):
        return Sem(es.enter_context(nc.semaphore(name)))

    te = sem("te")
    ve = sem("ve")
    se = sem("sescal")
    geio = sem("geio")
    cxn = sem("cxn")
    cy = sem("cy")
    s_const = sem("s_const")
    s_lnx = sem("s_lnx")
    s_xni = sem("s_xni")
    s_w0 = sem("s_w0")
    s_xnt = [sem("s_xnt0"), sem("s_xnt1")]
    s_rot = [sem("s_rot0"), sem("s_rot1")]
    s_st = [sem("s_st0"), sem("s_st1"), sem("s_st2")]
    s_rel = sem("s_rel")
    s_ao = sem("s_ao")
    cao = sem("cao")
    s_alh = sem("s_alh")
    s_wo = [sem("s_wo0"), sem("s_wo1")]
    s_sst = sem("s_sst")
    s_rab = sem("s_rab")
    s_scat = sem("s_scat")
    s_idx = sem("s_idx")
    s_rabg = sem("s_rabg")
    s_xg = [sem("s_xg0"), sem("s_xg1")]
    s_wout = sem("s_wout")
    s_w1 = [sem("s_w10"), sem("s_w11")]
    s_w2 = [sem("s_w20"), sem("s_w21")]
    s_yst = [sem("s_yst0"), sem("s_yst1")]
    s_y5 = sem("s_y5")
    s_p5x = sem("s_p5x")
    s_p5s = sem("s_p5s")
    s_out = sem("s_out")

    def wait(eng, s, v):
        if v <= 0:
            return
        op(eng, lambda e, _h=s.h, _v=v: e.wait_ge(_h, _v))

    def gwait(eng, gate):
        if gate.sem is not None and gate.val > 0:
            wait(eng, gate.sem, gate.val)

    def dma(eng, out, in_, s, transpose=False):
        def f(e, _o=out, _i=in_, _h=s.h, _t=transpose):
            if _t:
                e.dma_start(out=_o, in_=_i, transpose=True).then_inc(_h, 16)
            else:
                e.dma_start(out=_o, in_=_i).then_inc(_h, 16)
        op(eng, f)
        return s.inc(16)

    def mm(out, lhsT, rhs, start, stop, s=None):
        def f(e, _o=out, _l=lhsT, _r=rhs, _a=start, _b=stop, _s=s):
            i = e.matmul(_o, _l, _r, start=_a, stop=_b)
            if _s is not None:
                i.then_inc(_s.h, 1)
        op("tensor", f)
        return s.inc() if s is not None else None

    def petr(out, in_, ident, s=None):
        def f(e, _o=out, _i=in_, _d=ident, _s=s):
            i = e.transpose(_o, _i, _d)
            if _s is not None:
                i.then_inc(_s.h, 1)
        op("tensor", f)
        return s.inc() if s is not None else None

    EDGE_ALL = [False]

    def vop(meth, *args, s=None, eng="vector", **kw):
        if EDGE_ALL[0] and eng == "vector" and s is None:
            s = ve

        def f(e, _m=meth, _a=args, _k=kw, _s=s):
            i = getattr(e, _m)(*_a, **_k)
            if _s is not None:
                i.then_inc(_s.h, 1)
        op(eng, f)
        r = s.inc() if s is not None else None
        if EDGE_ALL[0] and eng == "vector":
            wait("vector", ve, ve.v)
        return r

    def act(out, in_, func, bias=0.0, scale=1.0, accum=None, s=None):
        def f(e, _o=out, _i=in_, _f=func, _b=bias, _c=scale, _ac=accum, _s=s):
            if _ac is not None:
                i = e.activation(_o, _i, _f, bias=_b, scale=_c, accum_out=_ac)
            else:
                i = e.activation(_o, _i, _f, bias=_b, scale=_c)
            if _s is not None:
                i.then_inc(_s.h, 1)
        op("scalar", f)
        return s.inc() if s is not None else None

    def coll(kind, alu, ins, outs, s):
        def f(e, _k=kind, _a=alu, _i=ins, _o=outs, _h=s.h):
            e.collective_compute(_k, _a, replica_groups=[list(range(NCORES))],
                                 ins=_i, outs=_o).then_inc(_h, 1)
        op("gpsimd", f)
        return s.inc()

    ABF, AFC, AIC = 81056, 12048, 608
    abf_t = nc.alloc_sbuf_tensor("abf", [128, ABF], BF16)
    af_t = nc.alloc_sbuf_tensor("af32", [128, AFC], F32)
    ai_t = nc.alloc_sbuf_tensor("ai32", [128, AIC], I32)
    abf = Arena(abf_t.ap(), ABF, "bf")
    af = Arena(af_t.ap(), AFC, "f32")
    ai = Arena(ai_t.ap(), AIC, "i32")

    def bfv(o, w):
        return abf.ap[:, o:o + w]

    def fv(o, w):
        return af.ap[:, o:o + w]

    def ivv(o, w):
        return ai.ap[:, o:o + w]

    pb = [es.enter_context(nc.psum_tensor(f"pb{i}", [128, 512], F32))
          for i in range(7)]
    pbm = es.enter_context(nc.psum_tensor("pbm", [128, 512], F32))
    pbg = [Gate() for _ in range(7)]
    pbmg = Gate()

    o_identbf = abf.take(128)
    o_utri = abf.take(128)
    o_onesbf = abf.take(128)
    o_selbf = abf.take(64)
    o_identf = af.take(128)
    o_onesf = af.take(128)
    o_wsel = af.take(E)
    o_hbase = af.take(1)
    o_iota = af.take(64)
    o_logits = af.take(256)
    o_t1 = af.take(64)
    o_cnt = af.take(256)
    o_p = af.take(256)
    o_score = af.take(256)
    o_eqm = af.take(256)
    o_rank = af.take(256)
    o_sel = af.take(256)
    o_r1 = af.take(256)
    o_r2 = af.take(256)
    o_cnt4 = af.take(4)
    o_lo = af.take(4)
    o_hi = af.take(4)
    o_mid = af.take(4)
    o_bc = af.take(4)
    o_ge4 = af.take(4)
    o_th = af.take(4)
    o_thbc = af.take(4)
    o_rr = af.take(4)
    o_colsum = af.take(64)
    o_colp = af.take(64)
    o_cps = af.take(64)
    o_capc = af.take(1)
    o_myrec = af.take(128)
    o_rowa = af.take(64)
    o_rowb = af.take(64)
    o_rowi = af.take(128)
    o_mysel = af.take(64)
    o_myrank = af.take(64)
    o_myp = af.take(64)
    o_pif = af.take(64)
    o_rab = af.take(16)
    o_rab2 = af.take(16)
    o_ptil = af.take(8)
    o_invr = af.take(512)
    o_ir1 = af.take(512)
    o_eps = af.take(1)
    o_ss4 = af.take(4)
    o_ss64 = af.take(64)
    o_ownpm = ai.take(8)
    o_aoix = ai.take(16)
    o_pi = ai.take(64)
    o_iadj = ai.take(64)
    o_ish = ai.take(64)
    o_tok = ai.take(8)
    o_rowA8 = ai.take(8)
    o_rowB8 = ai.take(8)
    o_im = ai.take(64)

    abf.push()
    o_xg = abf.take(2 * 2048)
    o_xgt = abf.take(16 * 1024)
    abf.push()
    o_qt = abf.take(N)
    o_kt = abf.take(N)
    o_vt = abf.take(N)
    abf.push()
    o_wqkvT = abf.take(16 * 768)
    o_wg = abf.take(16 * E)
    o_bg = abf.take(E)
    o_xnt = [abf.take(16 * 512) for _ in range(2)]
    o_xnstage = abf.take(2048)
    o_rot = [abf.take(1024) for _ in range(2)]
    o_rsc = abf.take(1024)
    o_qkv1st = abf.take(3 * 512)
    af.push()
    o_xsb = af.take(2048)
    o_stat = af.take(8)

    dma("sync", bfv(o_identbf, 128), identbfi, s_const)
    dma("sync", bfv(o_utri, 128), utrii, s_const)
    dma("sync", bfv(o_onesbf, 128), onesbfi, s_const)
    dma("sync", fv(o_identf, 128), identf32i, s_const)
    dma("sync", fv(o_onesf, 128), onesf32i, s_const)
    dma("sync", fv(o_wsel, E), wseli, s_const)
    dma("sync", fv(o_hbase, 1), hbasei, s_const)
    dma("sync", ivv(o_ownpm, 8), ownpmi, s_const)
    dma("sync", ivv(o_aoix, 16), aoixi, s_const)

    dma("gpsimd", bfv(o_wqkvT, 16 * 768),
        wqkvTi.rearrange("(a p) f -> p a f", p=128), s_w0)
    dma("gpsimd", bfv(o_wg, 16 * E),
        wgi.rearrange("(a p) e -> p a e", p=128), s_w0)
    dma("gpsimd", abf.ap[0:1, o_bg:o_bg + E], bgi, s_w0)
    vop("iota", ivv(o_pi, 64), pattern=[[128, 64]], base=0,
        channel_multiplier=1, eng="gpsimd", s=geio)

    wait("vector", geio, 1)
    vop("memset", fv(o_eps, 1), 1e-05)
    vop("tensor_copy", fv(o_iota, 64), ivv(o_pi, 64), s=ve)

    for k in range(8):
        if k >= 1:
            wait("sync", ve, ln_last_ve)
        dma("sync", fv(o_xsb, 2048), x_sl[k * 128:(k + 1) * 128, :], s_lnx)
        wait("vector", s_lnx, s_lnx.v)
        vop("reduce_sum", fv(o_stat, 1), fv(o_xsb, 2048), axis=AX.X, s=ve)
        wait("vector", ve, ve.v)
        vop("tensor_scalar", fv(o_stat + 1, 1), fv(o_stat, 1), 1.0 / D, None,
            op0=ALU.mult, s=ve)
        wait("vector", ve, ve.v)
        vop("tensor_scalar", fv(o_xsb, 2048), fv(o_xsb, 2048),
            fv(o_stat + 1, 1), None, op0=ALU.subtract, s=ve)
        wait("scalar", ve, ve.v)
        if k >= 1:
            wait("scalar", s_xni, ln_last_gq)
        act(bfv(o_xnstage, 2048), fv(o_xsb, 2048), AF.Square,
            accum=fv(o_stat + 2, 1), s=se)
        wait("scalar", se, se.v)
        act(fv(o_stat + 3, 1), fv(o_stat + 2, 1), AF.Ln, bias=fv(o_eps, 1),
            scale=1.0 / D, s=se)
        wait("scalar", se, se.v)
        act(fv(o_stat + 4, 1), fv(o_stat + 3, 1), AF.Exp, scale=-0.5, s=se)
        wait("vector", se, se.v)
        vop("tensor_scalar", bfv(o_xnstage, 2048), fv(o_xsb, 2048),
            fv(o_stat + 4, 1), None, op0=ALU.mult, s=ve)
        ln_last_ve = ve.v
        wait("gpsimd", ve, ve.v)
        dma("gpsimd", cc_xn_in.ap()[k * 128:(k + 1) * 128, :],
            bfv(o_xnstage, 2048), s_xni)
        ln_last_gq = s_xni.v
    wait("gpsimd", s_xni, s_xni.v)
    coll("AllGather", ALU.bypass, [cc_xn_in.ap().opt()],
         [cc_xn_out.ap().opt()], cxn)

    xnt_gate = [Gate(), Gate()]
    rot_gate = [Gate(), Gate()]
    st_gate = [Gate(), Gate(), Gate()]
    xnt_ld = {}
    rot_ld = {}

    def emit_xnt_load(j):
        buf = j % 2
        gwait("scalar", xnt_gate[buf])
        for d16 in range(16):
            src = cc_xn_out.ap()[j * 512:(j + 1) * 512,
                                 d16 * 128:(d16 + 1) * 128]
            dma("scalar", bfv(o_xnt[buf] + d16 * 512, 512), src, s_xnt[buf],
                transpose=True)
        xnt_ld[j] = s_xnt[buf].v

    def emit_rot_load(j):
        buf = j % 2
        gwait("sync", rot_gate[buf])
        dma("sync", abf.ap[0:64, o_rot[buf]:o_rot[buf] + 512],
            rotAi[:, j * 512:(j + 1) * 512], s_rot[buf])
        dma("sync", abf.ap[0:64, o_rot[buf] + 512:o_rot[buf] + 1024],
            rotBi[:, j * 512:(j + 1) * 512], s_rot[buf])
        rot_ld[j] = s_rot[buf].v

    wait("sync", cxn, 1)
    wait("scalar", cxn, 1)
    emit_xnt_load(0)
    emit_rot_load(0)
    emit_xnt_load(1)
    emit_rot_load(1)

    wait("tensor", s_w0, s_w0.v)
    wait("tensor", s_const, s_const.v)
    for j in range(NBLK):
        buf = j % 2
        wait("tensor", s_xnt[buf], xnt_ld[j])
        for F in range(4):
            gwait("tensor", pbg[F])
            for d16 in range(16):
                mm(pb[F][:, 0:512], bfv(o_wqkvT + d16 * 768 + F * 128, 128),
                   bfv(o_xnt[buf] + d16 * 512, 512),
                   d16 == 0, d16 == 15, s=te if d16 == 15 else None)
        gwait("tensor", pbmg)
        vpair_te = {}
        for pair in range(2):
            gwait("tensor", pbg[4])
            for tt in (2 * pair, 2 * pair + 1):
                colo = (tt % 2) * 256
                for d16 in range(16):
                    lhs = bfv(o_xnt[buf] + d16 * 512 + tt * 128, 128)
                    mm(pb[4][:, colo:colo + 256], lhs,
                       bfv(o_wqkvT + d16 * 768 + 512, 256),
                       d16 == 0, d16 == 15, s=te if d16 == 15 else None)
                    mm(pbm[:, tt * 4:tt * 4 + 4], lhs, bfv(o_wg + d16 * 4, 4),
                       d16 == 0, False)
                mm(pbm[:, tt * 4:tt * 4 + 4],
                   abf.ap[0:1, o_onesbf:o_onesbf + 128],
                   abf.ap[0:1, o_bg:o_bg + E], False, True,
                   s=te if tt == 3 else None)
            vpair_te[pair] = te.v
            wait("vector", te, te.v)
            for tt in (2 * pair, 2 * pair + 1):
                colo = (tt % 2) * 256
                vop("tensor_copy", bfv(o_vt + (j * 4 + tt) * 128, 128),
                    pb[4][:, colo:colo + 128])
                gwait("vector", st_gate[2])
                vop("tensor_copy", bfv(o_qkv1st + 1024 + tt * 128, 128),
                    pb[4][:, colo + 128:colo + 256], s=ve)
            pbg[4].set(ve, ve.v)
        te_blk = te.v
        xnt_gate[buf].set(te, te_blk)
        if j + 2 < NBLK:
            emit_xnt_load(j + 2)

        wait("vector", te, te_blk)
        wait("vector", s_rot[buf], rot_ld[j])
        for F in range(4):
            h = F % 2
            if h == 0:
                dsto = (o_qt if F == 0 else o_kt) + j * 512
            else:
                dsto = o_qkv1st + (F // 2) * 512
                gwait("vector", st_gate[F // 2])
            vop("tensor_copy", bfv(dsto, 512), pb[F][:, 0:512],
                s=ve if F == 3 else None)
        for F in range(4):
            pbg[F].set(ve, ve.v)
        for F in range(4):
            h = F % 2
            dsto = ((o_qt if F == 0 else o_kt) + j * 512) if h == 0 \
                else (o_qkv1st + (F // 2) * 512)
            in0 = abf.ap[0:32, dsto:dsto + 512]
            in32 = abf.ap[32:64, dsto:dsto + 512]
            cA = abf.ap[0:32, o_rot[buf]:o_rot[buf] + 512]
            sA = abf.ap[32:64, o_rot[buf]:o_rot[buf] + 512]
            sB = abf.ap[0:32, o_rot[buf] + 512:o_rot[buf] + 1024]
            cB = abf.ap[32:64, o_rot[buf] + 512:o_rot[buf] + 1024]
            t1 = abf.ap[0:32, o_rsc:o_rsc + 512]
            t2 = abf.ap[0:32, o_rsc + 512:o_rsc + 1024]
            t3 = abf.ap[32:64, o_rsc:o_rsc + 512]
            t4 = abf.ap[32:64, o_rsc + 512:o_rsc + 1024]
            vop("tensor_mul", t1, in0, cA)
            vop("tensor_mul", t2, in32, sA)
            vop("tensor_mul", t3, in32, cB)
            vop("tensor_mul", t4, in0, sB, s=ve)
            wait("vector", ve, ve.v)
            vop("tensor_add", in0, t1, t2)
            vop("tensor_add", in32, t3, t4, s=ve)
        rot_gate[buf].set(ve, ve.v)
        if j + 2 < NBLK:
            emit_rot_load(j + 2)
        vop("tensor_copy", fv(o_logits + j * 16, 16), pbm[:, 0:16], s=ve)
        pbmg.set(ve, ve.v)
        wait("sync", ve, ve.v)
        dma("sync", qkv1_buf.ap()[:, j * 512:(j + 1) * 512],
            bfv(o_qkv1st, 512), s_st[0])
        st_gate[0].set(s_st[0], s_st[0].v)
        dma("sync", qkv1_buf.ap()[:, N + j * 512:N + (j + 1) * 512],
            bfv(o_qkv1st + 512, 512), s_st[1])
        st_gate[1].set(s_st[1], s_st[1].v)
        dma("sync", qkv1_buf.ap()[:, 2 * N + j * 512:2 * N + (j + 1) * 512],
            bfv(o_qkv1st + 1024, 512), s_st[2])
        st_gate[2].set(s_st[2], s_st[2].v)
    P1_VE = ve.v
    P1_TE = te.v
    abf.pop()
    af.pop()

    EDGE_ALL[0] = True

    def eview(base, e):
        return af.ap[:, base:base + 256].rearrange("p (i e) -> p e i",
                                                   e=E)[:, e, :]

    vop("tensor_tensor", fv(o_t1, 64), eview(o_logits, 0),
        eview(o_logits, 1), op=ALU.max)
    vop("tensor_tensor", fv(o_r1, 64), eview(o_logits, 2),
        eview(o_logits, 3), op=ALU.max)
    vop("tensor_tensor", fv(o_t1, 64), fv(o_t1, 64), fv(o_r1, 64), op=ALU.max)
    for e in range(E):
        ce = eview(o_cnt, e)
        vop("memset", ce, 0.0)
        for ep in range(E):
            if ep == e:
                continue
            opk = ALU.is_gt if ep > e else ALU.is_ge
            vop("tensor_tensor", fv(o_r1, 64), eview(o_logits, ep),
                eview(o_logits, e), op=opk)
            vop("tensor_add", ce, ce, fv(o_r1, 64))
    for e in range(E):
        vop("tensor_tensor", eview(o_r1, e), eview(o_logits, e),
            fv(o_t1, 64), op=ALU.subtract, s=ve if e == E - 1 else None)
    wait("scalar", ve, ve.v)
    act(fv(o_p, 256), fv(o_r1, 256), AF.Exp, s=se)
    wait("vector", se, se.v)
    vop("tensor_scalar", fv(o_r2, 256), fv(o_cnt, 256), 1.5, None,
        op0=ALU.is_le)
    vop("tensor_mul", fv(o_p, 256), fv(o_p, 256), fv(o_r2, 256))
    vop("tensor_tensor", fv(o_t1, 64), eview(o_p, 0), eview(o_p, 1),
        op=ALU.add)
    vop("tensor_tensor", fv(o_r1, 64), eview(o_p, 2), eview(o_p, 3),
        op=ALU.add)
    vop("tensor_add", fv(o_t1, 64), fv(o_t1, 64), fv(o_r1, 64))
    vop("reciprocal", fv(o_r1, 64), fv(o_t1, 64), s=ve)
    wait("vector", ve, ve.v)
    for e in range(E):
        vop("tensor_tensor", eview(o_p, e), eview(o_p, e), fv(o_r1, 64),
            op=ALU.mult)
    vop("tensor_scalar", fv(o_score, 256), fv(o_r2, 256), 1.0, None,
        op0=ALU.subtract)
    vop("tensor_mul", fv(o_r2, 256), fv(o_p, 256), fv(o_r2, 256))
    vop("tensor_add", fv(o_score, 256), fv(o_score, 256), fv(o_r2, 256))
    vop("memset", fv(o_lo, 4), -2.0)
    vop("memset", fv(o_hi, 4), 2.0)
    vop("memset", fv(o_capc, 1), float(CAP) - 0.5, s=ve)

    def count4(cmp_op, scal_off, iota_mode=False):
        for e in range(E):
            tgt = eview(o_r1, e)
            if iota_mode:
                vop("tensor_scalar", tgt, fv(o_iota, 64),
                    fv(scal_off + e, 1), None, op0=cmp_op)
                vop("tensor_mul", tgt, tgt, eview(o_eqm, e))
            else:
                vop("tensor_scalar", tgt, eview(o_score, e),
                    fv(scal_off + e, 1), None, op0=cmp_op)
            vop("reduce_sum", fv(o_cnt4 + e, 1), tgt, axis=AX.X, s=ve)
            wait("vector", ve, ve.v)
        wait("tensor", ve, ve.v)
        gwait("tensor", pbmg)
        mm(pbm[:, 0:4], fv(o_onesf, 128), fv(o_cnt4, 4), True, True, s=te)
        wait("vector", te, te.v)
        vop("tensor_copy", fv(o_bc, 4), pbm[:, 0:4], s=ve)
        pbmg.set(ve, ve.v)

    for it in range(34):
        vop("tensor_tensor", fv(o_mid, 4), fv(o_lo, 4), fv(o_hi, 4),
            op=ALU.add)
        vop("tensor_scalar", fv(o_mid, 4), fv(o_mid, 4), 0.5, None,
            op0=ALU.mult, s=ve)
        count4(ALU.is_gt, o_mid)
        vop("tensor_scalar", fv(o_ge4, 4), fv(o_bc, 4), fv(o_capc, 1),
            None, op0=ALU.is_ge)
        vop("tensor_copy", ivv(o_im, 4), fv(o_ge4, 4))
        vop("select", fv(o_ss4, 4), ivv(o_im, 4), fv(o_mid, 4), fv(o_lo, 4))
        vop("tensor_copy", fv(o_lo, 4), fv(o_ss4, 4))
        vop("select", fv(o_ss4, 4), ivv(o_im, 4), fv(o_hi, 4), fv(o_mid, 4))
        vop("tensor_copy", fv(o_hi, 4), fv(o_ss4, 4), s=ve)
    vop("tensor_copy", fv(o_th, 4), fv(o_hi, 4), s=ve)
    count4(ALU.is_gt, o_th)
    vop("tensor_scalar", fv(o_rr, 4), fv(o_bc, 4), -1.0, float(CAP),
        op0=ALU.mult, op1=ALU.add, s=ve)
    for e in range(E):
        vop("tensor_scalar", eview(o_eqm, e), eview(o_score, e),
            fv(o_th + e, 1), None, op0=ALU.is_equal)
    vop("memset", fv(o_lo, 4), -1.0)
    vop("memset", fv(o_hi, 4), float(N), s=ve)
    for it in range(16):
        vop("tensor_tensor", fv(o_mid, 4), fv(o_lo, 4), fv(o_hi, 4),
            op=ALU.add)
        vop("tensor_scalar", fv(o_mid, 4), fv(o_mid, 4), 0.5, None,
            op0=ALU.mult, s=ve)
        count4(ALU.is_le, o_mid, iota_mode=True)
        vop("tensor_tensor", fv(o_ge4, 4), fv(o_bc, 4), fv(o_rr, 4),
            op=ALU.is_ge)
        vop("tensor_copy", ivv(o_im, 4), fv(o_ge4, 4))
        vop("select", fv(o_ss4, 4), ivv(o_im, 4), fv(o_mid, 4), fv(o_hi, 4))
        vop("tensor_copy", fv(o_hi, 4), fv(o_ss4, 4))
        vop("select", fv(o_ss4, 4), ivv(o_im, 4), fv(o_lo, 4), fv(o_mid, 4))
        vop("tensor_copy", fv(o_lo, 4), fv(o_ss4, 4), s=ve)
    for e in range(E):
        vop("tensor_scalar", eview(o_r1, e), eview(o_score, e),
            fv(o_th + e, 1), None, op0=ALU.is_gt)
        vop("tensor_scalar", eview(o_r2, e), fv(o_iota, 64),
            fv(o_hi + e, 1), None, op0=ALU.is_le)
        vop("tensor_mul", eview(o_r2, e), eview(o_r2, e), eview(o_eqm, e))
        vop("tensor_tensor", eview(o_sel, e), eview(o_r1, e), eview(o_r2, e),
            op=ALU.add)
    vop("tensor_mul", fv(o_p, 256), fv(o_p, 256), fv(o_sel, 256), s=ve)

    for e in range(E):
        vop("tensor_copy", bfv(o_selbf, 64), eview(o_sel, e), s=ve)
        wait("tensor", ve, ve.v)
        gwait("tensor", pbmg)
        mm(pbm[:, 64:128], bfv(o_utri, 128), bfv(o_selbf, 64), True, True)
        mm(pbm[0:1, 192:256], abf.ap[:, o_onesbf:o_onesbf + 1],
           bfv(o_selbf, 64), True, True, s=te)
        wait("vector", te, te.v)
        vop("tensor_copy", af.ap[0:1, o_colsum:o_colsum + 64],
            pbm[0:1, 192:256])
        vop("tensor_copy", af.ap[0:1, o_colp:o_colp + 64], pbm[0:1, 192:256])
        cur, oth = o_colp, o_cps
        sh = 1
        while sh < 64:
            vop("tensor_copy", af.ap[0:1, oth:oth + sh],
                af.ap[0:1, cur:cur + sh])
            vop("tensor_tensor", af.ap[0:1, oth + sh:oth + 64],
                af.ap[0:1, cur + sh:cur + 64],
                af.ap[0:1, cur:cur + 64 - sh], op=ALU.add)
            cur, oth = oth, cur
            sh *= 2
        vop("tensor_tensor", af.ap[0:1, o_colp:o_colp + 64],
            af.ap[0:1, cur:cur + 64],
            af.ap[0:1, o_colsum:o_colsum + 64], op=ALU.subtract, s=ve)
        wait("tensor", ve, ve.v)
        mm(pbm[:, 128:192], af.ap[0:1, o_onesf:o_onesf + 128],
           af.ap[0:1, o_colp:o_colp + 64], True, True, s=te)
        wait("vector", te, te.v)
        vop("tensor_copy", fv(o_r1, 64), pbm[:, 128:192])
        vop("tensor_tensor", eview(o_rank, e), pbm[:, 64:128],
            eview(o_sel, e), op=ALU.subtract)
        vop("tensor_tensor", eview(o_rank, e), eview(o_rank, e),
            fv(o_r1, 64), op=ALU.add, s=ve)
        pbmg.set(ve, ve.v)

    for src_b, dst_o in ((o_sel, o_mysel), (o_rank, o_myrank), (o_p, o_myp)):
        vop("memset", fv(dst_o, 64), 0.0)
        for e in range(E):
            vop("tensor_scalar", fv(o_r1, 64), eview(src_b, e),
                fv(o_wsel + e, 1), None, op0=ALU.mult)
            vop("tensor_add", fv(dst_o, 64), fv(dst_o, 64), fv(o_r1, 64))
    vop("tensor_scalar", fv(o_r1, 64), fv(o_myrank, 64), fv(o_hbase, 1),
        None, op0=ALU.subtract)
    vop("tensor_scalar", fv(o_r2, 64), fv(o_r1, 64), -0.5, None, op0=ALU.is_ge)
    vop("tensor_mul", fv(o_r2, 64), fv(o_r2, 64), fv(o_mysel, 64))
    vop("tensor_scalar", fv(o_mysel, 64), fv(o_r1, 64), float(HALF) - 0.5,
        None, op0=ALU.is_le)
    vop("tensor_mul", fv(o_r2, 64), fv(o_r2, 64), fv(o_mysel, 64))
    vop("tensor_copy", ivv(o_iadj, 64), fv(o_r1, 64))
    vop("memset", ivv(o_pi, 64), 127)
    vop("tensor_tensor", ivv(o_pi, 64), ivv(o_iadj, 64), ivv(o_pi, 64),
        op=ALU.bitwise_and)
    vop("memset", ivv(o_ish, 64), 3)
    vop("tensor_tensor", ivv(o_pi, 64), ivv(o_pi, 64), ivv(o_ish, 64),
        op=ALU.logical_shift_left)
    vop("memset", ivv(o_ish, 64), 7)
    vop("tensor_tensor", ivv(o_ish, 64), ivv(o_iadj, 64), ivv(o_ish, 64),
        op=ALU.logical_shift_right)
    vop("tensor_tensor", ivv(o_pi, 64), ivv(o_pi, 64), ivv(o_ish, 64),
        op=ALU.add)
    vop("tensor_copy", fv(o_pif, 64), ivv(o_pi, 64))
    vop("tensor_mul", fv(o_pif, 64), fv(o_pif, 64), fv(o_r2, 64))
    vop("tensor_scalar", fv(o_r1, 64), fv(o_r2, 64), -BIG, BIG, op0=ALU.mult,
        op1=ALU.add)
    vop("tensor_add", fv(o_pif, 64), fv(o_pif, 64), fv(o_r1, 64))
    vop("tensor_copy", ivv(o_pi, 64), fv(o_pif, 64))
    rec = af.ap[:, o_myrec:o_myrec + 128].rearrange("p (i z) -> p z i", z=2)
    vop("tensor_copy", rec[:, 0, :], fv(o_iota, 64))
    vop("tensor_copy", rec[:, 1, :], fv(o_myp, 64), s=ve)
    vop("memset", fv(o_rowa, 64), BIG)
    vop("memset", fv(o_rowb, 64), BIG)
    for e in range(E - 1, -1, -1):
        vop("tensor_scalar", fv(o_r1, 64), eview(o_rank, e), 1.0,
            float(CAP * e), op0=ALU.mult, op1=ALU.add)
        vop("tensor_copy", ivv(o_im, 64), eview(o_sel, e))
        vop("select", fv(o_ss64, 64), ivv(o_im, 64), fv(o_r1, 64),
            fv(o_rowa, 64))
        vop("tensor_copy", fv(o_rowa, 64), fv(o_ss64, 64))
    for e in range(E):
        vop("tensor_scalar", fv(o_r1, 64), eview(o_rank, e), 1.0,
            float(CAP * e), op0=ALU.mult, op1=ALU.add)
        vop("tensor_copy", ivv(o_im, 64), eview(o_sel, e))
        vop("select", fv(o_ss64, 64), ivv(o_im, 64), fv(o_r1, 64),
            fv(o_rowb, 64))
        vop("tensor_copy", fv(o_rowb, 64), fv(o_ss64, 64))
    vop("tensor_tensor", fv(o_r1, 64), eview(o_sel, 0), eview(o_sel, 1),
        op=ALU.add)
    vop("tensor_tensor", fv(o_r2, 64), eview(o_sel, 2), eview(o_sel, 3),
        op=ALU.add)
    vop("tensor_add", fv(o_r1, 64), fv(o_r1, 64), fv(o_r2, 64))
    vop("tensor_scalar", fv(o_r2, 64), fv(o_r1, 64), 1.5, None, op0=ALU.is_ge)
    vop("tensor_copy", ivv(o_im, 64), fv(o_r2, 64))
    vop("memset", fv(o_r1, 64), BIG)
    vop("select", fv(o_ss64, 64), ivv(o_im, 64), fv(o_rowb, 64), fv(o_r1, 64))
    vop("tensor_copy", fv(o_rowb, 64), fv(o_ss64, 64))
    rowi = af.ap[:, o_rowi:o_rowi + 128].rearrange("p (i z) -> p z i", z=2)
    vop("tensor_copy", rowi[:, 0, :], fv(o_rowa, 64))
    vop("tensor_copy", rowi[:, 1, :], fv(o_rowb, 64), s=ve)
    EDGE_ALL[0] = False
    VE_ROUTE = ve.v

    wait("gpsimd", ve, VE_ROUTE)
    dma("gpsimd", rowab_buf.ap().rearrange("(p i) z -> p (i z)", p=128),
        fv(o_rowi, 128), s_rab)

    def scat(e):
        with e.register("b1023") as r:
            e.reg_mov(r, HALF - 1)
            for i in range(64):
                e.indirect_dma_start(
                    out=idxp_buf.ap(),
                    out_offset=IndirectOffsetOnAxis(ivv(o_pi + i, 1), axis=0),
                    in_=af.ap[:, o_myrec + i * 2:o_myrec + i * 2 + 2],
                    in_offset=None, bounds_check=r,
                    oob_is_err=False).then_inc(s_scat.h, 16)
    op("gpsimd", scat)
    s_scat.inc(16 * 64)
    wait("gpsimd", s_scat, s_scat.v)
    dma("gpsimd", fv(o_rab, 16),
        idxp_buf.ap().rearrange("(p k) z -> p (k z)", p=128), s_idx)

    def rowabg(e):
        with e.register("brb") as r:
            e.reg_mov(r, N - 1)
            for k in range(8):
                e.indirect_dma_start(
                    out=af.ap[:, o_rab2 + k * 2:o_rab2 + k * 2 + 2],
                    out_offset=None, in_=rowab_buf.ap(),
                    in_offset=IndirectOffsetOnAxis(ivv(o_ownpm + k, 1), axis=0),
                    bounds_check=r, oob_is_err=False).then_inc(s_rabg.h, 16)
    op("gpsimd", rowabg)
    s_rabg.inc(16 * 8)
    wait("gpsimd", s_rab, s_rab.v)
    EDGE_ALL[0] = True
    wait("vector", s_idx, s_idx.v)
    wait("vector", s_rabg, s_rabg.v)
    rab = af.ap[:, o_rab:o_rab + 16].rearrange("p (k z) -> p z k", z=2)
    vop("tensor_copy", ivv(o_tok, 8), rab[:, 0, :])
    vop("tensor_copy", fv(o_ptil, 8), rab[:, 1, :])
    rab2 = af.ap[:, o_rab2:o_rab2 + 16].rearrange("p (k z) -> p z k", z=2)
    vop("tensor_copy", ivv(o_rowA8, 8), rab2[:, 0, :])
    vop("tensor_copy", ivv(o_rowB8, 8), rab2[:, 1, :], s=ve)
    EDGE_ALL[0] = False
    VE_TOK = ve.v

    wait("gpsimd", ve, VE_TOK)
    xg_gate = [Gate(), Gate()]
    xgs_gate = [Gate(), Gate()]
    gq_xg = {}
    o_xgf = af.take(256)
    for k in range(8):
        gwait("gpsimd", xg_gate[k % 2])

        def gat(e, _k=k):
            with e.register(f"bx{_k}") as r:
                e.reg_mov(r, N - 1)
                e.indirect_dma_start(
                    out=bfv(o_xg + (_k % 2) * 2048, 2048), out_offset=None,
                    in_=cc_xn_out.ap(),
                    in_offset=IndirectOffsetOnAxis(ivv(o_tok + _k, 1), axis=0),
                    bounds_check=r, oob_is_err=False).then_inc(
                        s_xg[_k % 2].h, 16)
        op("gpsimd", gat)
        gq_xg[k] = s_xg[k % 2].inc(16)
        wait("vector", s_xg[k % 2], gq_xg[k])
        for d16 in range(16):
            sl = d16 % 2
            gwait("vector", xgs_gate[sl])
            vop("tensor_copy", fv(o_xgf + sl * 128, 128),
                bfv(o_xg + (k % 2) * 2048 + d16 * 128, 128), s=ve)
            if d16 == 15:
                xg_gate[k % 2].set(ve, ve.v)
            wait("tensor", ve, ve.v)
            gwait("tensor", pbg[sl])
            petr(pb[sl][:, 0:128], fv(o_xgf + sl * 128, 128),
                 fv(o_identf, 128), s=te)
            xgs_gate[sl].set(te, te.v)
            wait("vector", te, te.v)
            vop("tensor_copy", bfv(o_xgt + d16 * 1024 + k * 128, 128),
                pb[sl][:, 0:128], s=ve)
            pbg[sl].set(ve, ve.v)
    VE_XGT = ve.v

    abf.push()
    o_pt = [abf.take(512) for _ in range(4)]
    o_pts = [abf.take(512) for _ in range(2)]
    o_drow = abf.take(N)
    o_dmul = abf.take(1024)
    o_negbf = abf.take(128)
    o_aoutT = abf.take(HPC * N)
    o_trimt4 = abf.take(2048)
    o_bias2 = af.take(512)
    s_tm4 = sem("s_tm4")
    dma("sync", bfv(o_trimt4, 2048), trimT4i, s_tm4)
    wait("vector", s_tm4, s_tm4.v)
    pt_gate = [Gate() for _ in range(4)]
    av_gate = [Gate(), Gate()]
    av_gate[0].set(pbg[4].sem, pbg[4].val)
    av_gate[1].set(pbg[5].sem, pbg[5].val)
    den_gate = Gate()
    pts_gate = [Gate(), Gate()]
    dmg = [Gate(), Gate()]
    pending_norm = [None]

    def gwait2(eng, ga, gb):
        if ga.sem is not None and gb.sem is not None and ga.sem is gb.sem:
            wait(eng, ga.sem, max(ga.val, gb.val))
        else:
            gwait(eng, ga)
            gwait(eng, gb)

    vop("memset", abf.ap[0:1, o_negbf:o_negbf + 128], -1.0, s=ve)
    wait("tensor", ve, ve.v)

    for h in range(HPC):
        if h == 1:
            wait("sync", te, te.v)
            for z in range(3):
                wait("sync", s_st[z], s_st[z].v)
            for z, dst in enumerate((o_qt, o_kt, o_vt)):
                dma("sync", bfv(dst, N), qkv1_buf.ap()[:, z * N:(z + 1) * N],
                    s_rel)
            wait("tensor", s_rel, s_rel.v)
            wait("vector", s_rel, s_rel.v)
        for GG in range(16):
            gwait("vector", dmg[GG % 2])
            vop("tensor_tensor", bfv(o_dmul + (GG % 2) * 512, 512),
                bfv(o_qt + GG * 512, 512), bfv(o_kt + GG * 512, 512),
                op=ALU.mult, s=ve)
            wait("tensor", ve, ve.v)
            gwait("tensor", pbmg)
            mm(pbm[0:1, 0:512], abf.ap[:, o_onesbf:o_onesbf + 1],
               bfv(o_dmul + (GG % 2) * 512, 512), True, True, s=te)
            dmg[GG % 2].set(te, te.v)
            wait("vector", te, te.v)
            vop("tensor_copy", abf.ap[0:1, o_drow + GG * 512:
                                      o_drow + (GG + 1) * 512],
                pbm[0:1, 0:512], s=ve)
            pbmg.set(ve, ve.v)

        for G in range(16):
            nkj = 4 * G + 4
            P = nkj // 2
            avbank = pb[4 + (G % 2)]
            dend = pb[6]
            gwait("tensor", pbmg)
            wait("tensor", ve, ve.v)
            mm(pbm[:, 0:512], abf.ap[0:1, o_negbf:o_negbf + 128],
               abf.ap[0:1, o_drow + G * 512:o_drow + (G + 1) * 512],
               True, True, s=te)
            wait("vector", te, te.v)
            vop("tensor_copy", fv(o_bias2, 512), pbm[:, 0:512], s=ve)
            pbmg.set(ve, ve.v)
            gwait("tensor", den_gate)
            gwait("tensor", av_gate[G % 2])
            smm_ts = {}
            exp_ts = {}
            pts_ts = {}
            for pj in range(P + 2):
                if pj < P:
                    c0, c1 = 2 * pj, 2 * pj + 1
                    gwait2("tensor", pbg[c0 % 4], pbg[c1 % 4])
                    for c in (c0, c1):
                        smm_ts[c] = mm(pb[c % 4][:, 0:512],
                                       bfv(o_kt + c * 128, 128),
                                       bfv(o_qt + G * 512, 512), True, True,
                                       s=te)
                if pj == 1 and pending_norm[0] is not None:
                    pending_norm[0]()
                    pending_norm[0] = None
                if 1 <= pj <= P:
                    p0, p1 = 2 * pj - 2, 2 * pj - 1
                    wait("tensor", se, exp_ts[p1])
                    mm(avbank[:, 0:512], bfv(o_vt + p0 * 128, 128),
                       bfv(o_pt[p0 % 4], 512), p0 == 0, False)
                    mm(avbank[:, 0:512], bfv(o_vt + p1 * 128, 128),
                       bfv(o_pt[p1 % 4], 512), False, p1 == nkj - 1, s=te)
                    pt_gate[p0 % 4].set(te, te.v)
                    pt_gate[p1 % 4].set(te, te.v)
                if pj >= 2:
                    q0 = 2 * pj - 4
                    wait("tensor", ve, pts_ts[pj - 2])
                    mm(dend[0:1, 0:512], abf.ap[:, o_onesbf:o_onesbf + 1],
                       bfv(o_pts[(pj - 2) % 2], 512),
                       q0 == 0, q0 + 1 == nkj - 1, s=te)
                    pts_gate[(pj - 2) % 2].set(te, te.v)
                if pj < P:
                    wait("vector", te, smm_ts[c1])
                    for c in (c0, c1):
                        sl = ve if c == c1 and c < 4 * G else None
                        vop("tensor_tensor", pb[c % 4][:, 0:512],
                            pb[c % 4][:, 0:512], fv(o_bias2, 512), op=ALU.add,
                            s=sl)
                        if c >= 4 * G:
                            vop("tensor_tensor", pb[c % 4][:, 0:512],
                                pb[c % 4][:, 0:512],
                                bfv(o_trimt4 + (c - 4 * G) * 512, 512),
                                op=ALU.add, s=ve if c == c1 else None)
                    add_inc = ve.v
                if 1 <= pj <= P:
                    wait("vector", se, exp_ts[2 * pj - 1])
                    gwait("vector", pts_gate[(pj - 1) % 2])
                    vop("tensor_tensor", bfv(o_pts[(pj - 1) % 2], 512),
                        bfv(o_pt[(2 * pj - 2) % 4], 512),
                        bfv(o_pt[(2 * pj - 1) % 4], 512), op=ALU.add, s=ve)
                    pts_ts[pj - 1] = ve.v
                if pj < P:
                    wait("scalar", ve, add_inc)
                    gwait2("scalar", pt_gate[c0 % 4], pt_gate[c1 % 4])
                    for c in (c0, c1):
                        act(bfv(o_pt[c % 4], 512), pb[c % 4][:, 0:512],
                            AF.Exp, scale=SCALE, s=se)
                        exp_ts[c] = se.v
                        pbg[c % 4].set(se, se.v)
            av_done = te.v

            def mknorm(_h=h, _G=G, _av=avbank, _done=av_done):
                def emit():
                    wait("vector", te, _done)
                    vop("reciprocal", af.ap[0:1, o_ir1:o_ir1 + 512],
                        pb[6][0:1, 0:512], s=ve)
                    den_gate.set(ve, ve.v)
                    wait("tensor", ve, ve.v)
                    gwait("tensor", pbmg)
                    mm(pbm[:, 0:512], af.ap[0:1, o_onesf:o_onesf + 128],
                       af.ap[0:1, o_ir1:o_ir1 + 512], True, True, s=te)
                    wait("vector", te, te.v)
                    vop("tensor_copy", fv(o_invr, 512), pbm[:, 0:512], s=ve)
                    pbmg.set(ve, ve.v)
                    vop("scalar_tensor_tensor",
                        bfv(o_aoutT + _h * N + _G * 512, 512),
                        _av[:, 0:512], 1.0, fv(o_invr, 512),
                        op0=ALU.mult, op1=ALU.mult, s=ve)
                    av_gate[_G % 2].set(ve, ve.v)
                return emit
            pending_norm[0] = mknorm()
        pending_norm[0]()
        pending_norm[0] = None
        wait("sync", ve, ve.v)
        dma("sync", cc_ao_in[h].ap(), bfv(o_aoutT + h * N, N), s_ao)
        wait("gpsimd", s_ao, s_ao.v)
        coll("AllGather", ALU.bypass, [cc_ao_in[h].ap().opt()],
             [cc_ao_out[h].ap().rearrange("(f g) x -> f (g x)", g=8).opt()],
             cao)
    P2_VE = ve.v
    P2_SE = se.v
    for t in range(4):
        pbg[t].set(se, P2_SE)
    pbg[4].set(ve, P2_VE)
    pbg[5].set(ve, P2_VE)
    pbg[6].set(ve, P2_VE)

    abf.pop()
    abf.pop()

    af.push()
    o_sst = [af.take(512) for _ in range(4)]
    sst_gate = [Gate() for _ in range(4)]
    abf.push()
    o_alh = abf.take(16 * 1024)
    o_wo = [abf.take(16 * 512) for _ in range(2)]
    wo_gate = [Gate(), Gate()]
    wo_ld = {}

    def emit_wo_load(cb):
        b = cb % 2
        gwait("gpsimd", wo_gate[b])
        src = woutFi[:, cb * 512:(cb + 1) * 512] \
            .rearrange("(a p) c -> p a c", p=128)
        dma("gpsimd", bfv(o_wo[b], 16 * 512), src, s_wo[b])
        wo_ld[cb] = s_wo[b].v

    wait("gpsimd", cao, 2)
    for a in range(16):
        def g(e, _a=a):
            with e.register(f"ba{_a}") as r:
                e.reg_mov(r, NCORES * 128 * 8 - 1)
                e.indirect_dma_start(
                    out=bfv(o_alh + _a * 1024, 1024), out_offset=None,
                    in_=cc_ao_out[_a % 2].ap(),
                    in_offset=IndirectOffsetOnAxis(ivv(o_aoix + _a, 1),
                                                   axis=0),
                    bounds_check=r, oob_is_err=False).then_inc(s_alh.h, 16)
        op("gpsimd", g)
        s_alh.inc(16)
    emit_wo_load(0)
    emit_wo_load(1)

    wait("tensor", s_alh, s_alh.v)
    wait("tensor", ve, P2_VE)
    for cb in range(4):
        b = cb % 2
        wait("tensor", s_wo[b], wo_ld[cb])
        for k in range(8):
            bank = (cb * 8 + k) % 4
            gwait("tensor", pbg[bank])
            for a in range(16):
                mm(pb[bank][:, 0:512], bfv(o_alh + a * 1024 + k * 128, 128),
                   bfv(o_wo[b] + a * 512, 512), a == 0, a == 15,
                   s=te if a == 15 else None)
            if k == 7:
                wo_gate[b].set(te, te.v)
            wait("vector", te, te.v)
            gwait("vector", sst_gate[bank])
            vop("tensor_copy", fv(o_sst[bank], 512), pb[bank][:, 0:512], s=ve)
            pbg[bank].set(ve, ve.v)
            wait("sync", ve, ve.v)
            dma("sync", cc_sown.ap()[k * 128:(k + 1) * 128,
                                     cb * 512:(cb + 1) * 512],
                fv(o_sst[bank], 512), s_sst)
            sst_gate[bank].set(s_sst, s_sst.v)
        if cb + 2 < 4:
            emit_wo_load(cb + 2)
    P3_TE = te.v
    abf.pop()
    af.pop()

    abf.push()
    o_w1b = [abf.take(2048) for _ in range(2)]
    o_w2sl = [abf.take(16 * 512) for _ in range(2)]
    o_ht = abf.take(16 * 1024)
    o_yacc = abf.take(8 * 2048)
    o_yst = [abf.take(512) for _ in range(2)]
    w1_gate = [Gate(), Gate()]
    w2_gate = [Gate(), Gate()]
    hps_gate = [Gate(), Gate()]
    hps_gate[0].set(ve, P2_VE)
    hps_gate[1].set(pbmg.sem, pbmg.val)
    ht_gate = Gate()
    yst_gate = [Gate(), Gate()]
    w1_ld = {}
    w2_ld = {}

    def emit_w1_load(q, ht):
        hb = ht % 2
        gwait("gpsimd", w1_gate[hb])
        src = w1i[:, (q * 16 + ht) * 128:(q * 16 + ht + 1) * 128] \
            .rearrange("(a p) h -> p a h", p=128)
        dma("gpsimd", bfv(o_w1b[hb], 2048), src, s_w1[hb])
        w1_ld[(q, ht)] = (hb, s_w1[hb].v)

    def emit_w2_load(q, cb):
        wb = cb % 2
        gwait("sync", w2_gate[wb])
        src = w2i[q * 2048:(q + 1) * 2048, cb * 512:(cb + 1) * 512] \
            .rearrange("(a p) c -> p a c", p=128)
        dma("sync", bfv(o_w2sl[wb], 16 * 512), src, s_w2[wb])
        w2_ld[(q, cb)] = (wb, s_w2[wb].v)

    wait("gpsimd", te, P3_TE)
    emit_w1_load(0, 0)
    emit_w1_load(0, 1)
    for q in range(4):
        gwait("scalar", ht_gate)
        for ht in range(16):
            hb = ht % 2
            _hb, _v = w1_ld[(q, ht)]
            wait("tensor", s_w1[_hb], _v)
            for sb in range(2):
                bank = pb[4] if sb == 0 else pbm
                gwait("tensor", hps_gate[sb])
                for d16 in range(16):
                    mm(bank[:, 0:512], bfv(o_w1b[hb] + d16 * 128, 128),
                       bfv(o_xgt + d16 * 1024 + sb * 512, 512),
                       d16 == 0, d16 == 15, s=te if d16 == 15 else None)
                if sb == 1:
                    w1_gate[hb].set(te, te.v)
                wait("scalar", te, te.v)
                act(bfv(o_ht + ht * 1024 + sb * 512, 512), bank[:, 0:512],
                    AF.Gelu, s=se)
                hps_gate[sb].set(se, se.v)
            nq, nht = (q, ht + 2) if ht + 2 < 16 else (q + 1, ht + 2 - 16)
            if nq < 4:
                emit_w1_load(nq, nht)
        se_h = se.v
        if q == 0:
            emit_w2_load(0, 0)
            emit_w2_load(0, 1)
        for cb in range(4):
            wb = cb % 2
            _wb, _v = w2_ld[(q, cb)]
            wait("tensor", s_w2[_wb], _v)
            wait("tensor", se, se_h)
            for stp in range(2):
                for st in range(4):
                    gwait("tensor", pbg[st])
                for ht2 in range(16):
                    for st in range(4):
                        sg = stp * 4 + st
                        mm(pb[st][:, 0:512],
                           bfv(o_ht + ht2 * 1024 + sg * 128, 128),
                           bfv(o_w2sl[wb] + ht2 * 512, 512),
                           ht2 == 0, ht2 == 15,
                           s=te if (ht2 == 15 and st == 3) else None)
                if cb == 3 and stp == 1:
                    ht_gate.set(te, te.v)
                if stp == 1:
                    w2_gate[wb].set(te, te.v)
                wait("vector", te, te.v)
                for st in range(4):
                    sg = stp * 4 + st
                    ya = bfv(o_yacc + sg * 2048 + cb * 512, 512)
                    if q == 0:
                        vop("tensor_copy", ya, pb[st][:, 0:512],
                            s=ve if st == 3 else None)
                    else:
                        vop("tensor_tensor", ya, pb[st][:, 0:512], ya,
                            op=ALU.add, s=ve if st == 3 else None)
                    if st == 3:
                        for st2 in range(4):
                            pbg[st2].set(ve, ve.v)
            nq, ncb = (q, cb + 2) if cb + 2 < 4 else (q + 1, cb + 2 - 4)
            if nq < 4:
                emit_w2_load(nq, ncb)
    for sg in range(8):
        for cb in range(4):
            b = (sg * 4 + cb) % 2
            gwait("vector", yst_gate[b])
            vop("tensor_scalar", bfv(o_yst[b], 512),
                bfv(o_yacc + sg * 2048 + cb * 512, 512),
                fv(o_ptil + sg, 1), None, op0=ALU.mult, s=ve)
            wait("sync", ve, ve.v)
            dma("sync",
                cc_y_in.ap()[sg * 128:(sg + 1) * 128, cb * 512:(cb + 1) * 512],
                bfv(o_yst[b], 512), s_yst[b])
            yst_gate[b].set(s_yst[b], s_yst[b].v)
    wait("gpsimd", s_yst[0], s_yst[0].v)
    wait("gpsimd", s_yst[1], s_yst[1].v)
    coll("AllGather", ALU.bypass, [cc_y_in.ap().opt()],
         [cc_y_out.ap().opt()], cy)
    abf.pop()
    abf.pop()

    abf.push()
    o_ya = [abf.take(2048) for _ in range(2)]
    o_yb = [abf.take(2048) for _ in range(2)]
    af.push()
    o_g = af.take(2048)
    o_xt = af.take(2048)
    o_st5 = af.take(2048)
    s_y5p = [sem("s_y5a"), sem("s_y5b")]
    yag = [Gate(), Gate()]
    y5_tgt = {}

    wait("gpsimd", cy, 1)
    wait("sync", s_sst, s_sst.v)
    for b in range(2):
        vop("memset", bfv(o_ya[b], 2048), 0.0)
        vop("memset", bfv(o_yb[b], 2048), 0.0, s=ve if b == 1 else None)
    yag[0].set(ve, ve.v)
    yag[1].set(ve, ve.v)
    for k in range(8):
        b = k % 2
        gwait("gpsimd", yag[b])

        def gath(e, _k=k, _b=b):
            with e.register(f"by{_k}") as r:
                e.reg_mov(r, NCORES * HALF - 1)
                e.indirect_dma_start(
                    out=bfv(o_ya[_b], 2048), out_offset=None,
                    in_=cc_y_out.ap(),
                    in_offset=IndirectOffsetOnAxis(ivv(o_rowA8 + _k, 1),
                                                   axis=0),
                    bounds_check=r, oob_is_err=False).then_inc(s_y5p[_b].h, 16)
                e.indirect_dma_start(
                    out=bfv(o_yb[_b], 2048), out_offset=None,
                    in_=cc_y_out.ap(),
                    in_offset=IndirectOffsetOnAxis(ivv(o_rowB8 + _k, 1),
                                                   axis=0),
                    bounds_check=r, oob_is_err=False).then_inc(s_y5p[_b].h, 16)
        op("gpsimd", gath)
        y5_tgt[k] = s_y5p[b].inc(32)
        if k >= 1:
            wait("sync", s_out, s_out.v)
        dma("sync", fv(o_xt, 2048), x_sl[k * 128:(k + 1) * 128, :], s_p5x)
        if k >= 1:
            wait("sync", ve, ve.v)
        dma("sync", fv(o_st5, 2048),
            cc_sown.ap()[k * 128:(k + 1) * 128, :], s_p5s)
        wait("vector", s_y5p[b], y5_tgt[k])
        vop("tensor_add", bfv(o_ya[b], 2048), bfv(o_ya[b], 2048),
            bfv(o_yb[b], 2048), s=ve)
        wait("scalar", ve, ve.v)
        act(fv(o_g, 2048), bfv(o_ya[b], 2048), AF.Silu, s=se)
        wait("vector", se, se.v)
        wait("vector", s_p5x, s_p5x.v)
        wait("vector", s_p5s, s_p5s.v)
        vop("tensor_mul", fv(o_g, 2048), fv(o_g, 2048), fv(o_st5, 2048),
            s=ve)
        wait("vector", ve, ve.v)
        vop("tensor_add", fv(o_xt, 2048), fv(o_xt, 2048), fv(o_g, 2048), s=ve)
        wait("sync", ve, ve.v)
        dma("sync", out_sl[k * 128:(k + 1) * 128, :], fv(o_xt, 2048), s_out)
        if k < 6:
            vop("memset", bfv(o_ya[b], 2048), 0.0)
            vop("memset", bfv(o_yb[b], 2048), 0.0, s=ve)
            yag[b].set(ve, ve.v)
    if debug:
        s_dbg = sem("s_dbg")
        wait("gpsimd", s_out, s_out.v)
        dma("gpsimd", dbg_o["d_xn"], cc_xn_out.ap(), s_dbg)
        dma("gpsimd", dbg_o["d_lg"], fv(o_logits, 256), s_dbg)
        dma("gpsimd", dbg_o["d_rt"][:, 0:256], fv(o_sel, 256), s_dbg)
        dma("gpsimd", dbg_o["d_rt"][:, 256:512], fv(o_rank, 256), s_dbg)
        dma("gpsimd", dbg_o["d_rt"][:, 512:768], fv(o_p, 256), s_dbg)
        dma("gpsimd", dbg_o["d_rt"][:, 768:832], fv(o_mysel, 64), s_dbg)
        dma("gpsimd", dbg_o["d_rt"][:, 832:896], fv(o_pif, 64), s_dbg)
        dma("gpsimd", dbg_o["d_rt"][:, 896:900], fv(o_th, 4), s_dbg)
        dma("gpsimd", dbg_o["d_rt"][:, 900:904], fv(o_ptil + 0, 4), s_dbg)
        dma("gpsimd", dbg_o["d_rab"], rowab_buf.ap(), s_dbg)
        dma("gpsimd", dbg_o["d_idx"], idxp_buf.ap(), s_dbg)
        dma("gpsimd", dbg_o["d_qkv1"], qkv1_buf.ap(), s_dbg)
        dma("gpsimd", dbg_o["d_s"], cc_sown.ap(), s_dbg)
        dma("gpsimd", dbg_o["d_y"], cc_y_out.ap(), s_dbg)
        wait("gpsimd", s_dbg, s_dbg.v)
    wait("sync", s_out, s_out.v)

    nc._dbg = {k: v for k, v in locals().items()
               if isinstance(v, int) and k.startswith("o_")}
    with nc.Block() as block:
        @block.sync
        def _(e):
            for f in prog["sync"]:
                f(e)

        @block.tensor
        def _(e):
            for f in prog["tensor"]:
                f(e)

        @block.vector
        def _(e):
            for f in prog["vector"]:
                f(e)

        @block.scalar
        def _(e):
            for f in prog["scalar"]:
                f(e)

        @block.gpsimd
        def _(e):
            for f in prog["gpsimd"]:
                f(e)

    return nc


def _host_inputs(inputs):
    x = np.asarray(inputs["L0_attn_x"]).reshape(N, D).astype(np.float32)
    w_qkv = np.asarray(inputs["L0_attn_w_qkv"]).astype(np.float32)
    w_out = np.asarray(inputs["L0_attn_w_out"]).astype(np.float32)
    Wg = np.asarray(inputs["L0_ffn_Wg"]).astype(np.float32)
    bg = np.asarray(inputs["L0_ffn_bg"]).astype(np.float32)
    W1 = np.asarray(inputs["L0_ffn_W1"]).astype(np.float32)
    W2 = np.asarray(inputs["L0_ffn_W2"]).astype(np.float32)

    invf = (np.float32(1.0) / np.float32(10000.0) **
            (np.arange(0, 32, 2, dtype=np.float32) / np.float32(32)))
    freqs = np.arange(N, dtype=np.float32)[:, None] * invf[None, :]
    cos = np.cos(freqs.astype(np.float64)).astype(np.float32)
    sin = np.sin(freqs.astype(np.float64)).astype(np.float32)
    rotA = np.zeros((64, N), np.float32)
    rotB = np.zeros((64, N), np.float32)
    rotA[0:16] = cos.T
    rotA[16:32] = 1.0
    rotA[32:48] = -sin.T
    rotB[0:16] = sin.T
    rotB[32:48] = cos.T
    rotB[48:64] = 1.0

    trimask = np.where(np.triu(np.ones((128, 128), bool), 1),
                       np.float32(-1e30), np.float32(0.0)).astype(np.float32)
    trimT4 = np.zeros((128, 2048), np.float32)
    for t in range(4):
        trimT4[:, t * 512:t * 512 + t * 128] = -1e30
        trimT4[:, t * 512 + t * 128:t * 512 + (t + 1) * 128] = trimask.T

    common = {
        "rotAi": rotA.astype(bfdt),
        "rotBi": rotB.astype(bfdt),
        "trimT4i": trimT4.astype(bfdt),
        "identbfi": np.eye(128).astype(bfdt),
        "identf32i": np.eye(128).astype(np.float32),
        "utrii": (np.arange(128)[:, None] <= np.arange(128)[None, :])
        .astype(bfdt),
        "onesbfi": np.ones((128, 128), bfdt),
        "onesf32i": np.ones((128, 128), np.float32),
        "wgi": Wg,
        "bgi": bg.reshape(1, E),
    }

    in_maps = []
    for c in range(NCORES):
        h0 = HPC * c
        rows = []
        for blk in range(3):
            for hh in range(HPC):
                r = w_qkv[blk * 2048 + (h0 + hh) * DH:
                          blk * 2048 + (h0 + hh + 1) * DH]
                if blk < 2:
                    r = r[PERM]
                rows.append(r)
        wqkvT_c = np.ascontiguousarray(np.concatenate(rows, 0).T).astype(bfdt)

        e = c // 2
        wsel_c = np.zeros((128, E), np.float32)
        wsel_c[:, e] = 1.0
        q = np.arange(128)[:, None]
        kk = np.arange(8)[None, :]
        in_maps.append(dict(common, **{
            "x_sl": np.ascontiguousarray(x[c * TPC:(c + 1) * TPC]),
            "wqkvTi": wqkvT_c,
            "woutFi": np.ascontiguousarray(w_out.T).astype(bfdt),
            "aoixi": (((np.arange(16)[None, :] // 2) * 128 +
                       np.arange(128)[:, None]) * 8 + c).astype(np.int32),
            "w1i": np.ascontiguousarray(W1[e]).astype(bfdt),
            "w2i": np.ascontiguousarray(W2[e]).astype(bfdt),
            "wseli": wsel_c,
            "hbasei": np.full((128, 1), 1024.0 * (c % 2), np.float32),
            "ownpmi": (q * 64 + 8 * c + kk).astype(np.int32),
        }))
    return in_maps


def kernel(**inputs):
    key = ("ncd" if DEBUG else "nc")
    if key not in _CACHE:
        _CACHE[key] = build_program(debug=DEBUG)
    nc = _CACHE[key]
    in_maps = _host_inputs(inputs)
    res = run_bass_kernel_spmd(nc, in_maps, core_ids=list(range(NCORES)))
    if DEBUG:
        _CACHE["dbg_results"] = res.results
    out = np.concatenate([r["out_sl"] for r in res.results], axis=0)
    return out.reshape(1, N, D).astype(np.float32)
